# version 04
# baseline (speedup 1.0000x reference)
import numpy as np
import ml_dtypes
from contextlib import ExitStack

import concourse.bass as bass
import concourse.mybir as mybir
from concourse.ap import AP
from concourse.bass_utils import run_bass_kernel_spmd

F32 = mybir.dt.float32
BF16 = mybir.dt.bfloat16
AFT = mybir.ActivationFunctionType
ALU = mybir.AluOpType
AX = mybir.AxisListType

_PHASES = "full"
_DETECT_RACES = False

B, S, E = 2, 4096, 2048
HG = 4
TOK = 1024
QG = 512
INV_SQRT_D = float(1.0 / np.sqrt(128.0))
LN_EPS = 1e-5


def _build_nc():
    nc = bass.Bass("TRN2", target_bir_lowering=False, debug=False,
                   detect_race_conditions=_DETECT_RACES)

    XT = nc.dram_tensor("XT", [E, S], F32, kind="ExternalInput").ap()
    XR = nc.dram_tensor("XR", [TOK, E], F32, kind="ExternalInput").ap()
    WQT = nc.dram_tensor("WQT", [E, 512], F32, kind="ExternalInput").ap()
    WKT = nc.dram_tensor("WKT", [E, 512], F32, kind="ExternalInput").ap()
    WVT = nc.dram_tensor("WVT", [E, 512], F32, kind="ExternalInput").ap()
    WOT = nc.dram_tensor("WOT", [E, E], F32, kind="ExternalInput").ap()
    WFT = nc.dram_tensor("WFT", [E, E], F32, kind="ExternalInput").ap()
    GAT = nc.dram_tensor("GAT", [E, 136], F32, kind="ExternalInput").ap()
    BTD = nc.dram_tensor("BTD", [128, E], F32, kind="ExternalInput").ap()
    COS = nc.dram_tensor("COS", [64, S], BF16, kind="ExternalInput").ap()
    SIN = nc.dram_tensor("SIN", [64, S], BF16, kind="ExternalInput").ap()
    RTD = nc.dram_tensor("RTD", [64, 64], BF16, kind="ExternalInput").ap()
    ID16 = nc.dram_tensor("ID16", [128, 128], BF16, kind="ExternalInput").ap()
    ID32 = nc.dram_tensor("ID32", [128, 128], F32, kind="ExternalInput").ap()
    MASKS = nc.dram_tensor("MASKS", [128, 4 * QG], BF16, kind="ExternalInput").ap()
    Y = nc.dram_tensor("Y", [TOK, E], F32, kind="ExternalOutput").ap()
    PH = _PHASES
    if PH == "A":
        QD = nc.dram_tensor("QD", [128, HG * S], BF16, kind="ExternalOutput").ap()
        KD = nc.dram_tensor("KD", [128, HG * S], BF16, kind="ExternalOutput").ap()
        VD = nc.dram_tensor("VD", [128, HG * 32 * 128], BF16, kind="ExternalOutput").ap()
    if PH == "B":
        AOD = nc.dram_tensor("AOD", [128, HG * S], BF16, kind="ExternalOutput").ap()
    if PH == "X":
        XOD = nc.dram_tensor("XOD", [E, S], BF16, kind="ExternalOutput").ap()
    if PH == "C1":
        X1O = nc.dram_tensor("X1O", [TOK, E], F32, kind="ExternalOutput").ap()
    if PH == "AG2":
        AGD = nc.dram_tensor("AGD", [128, 16 * 128], BF16, kind="ExternalOutput").ap()
    if PH == "OP":
        OPD = nc.dram_tensor("OPD", [128, E], F32, kind="ExternalOutput").ap()
    if PH == "L1":
        L1D = nc.dram_tensor("L1D", [128, E], F32, kind="ExternalOutput").ap()

    AGIN = nc.dram_tensor("AGIN", [4 * 128, S], BF16)
    AGOUT = nc.dram_tensor("AGOUT", [E, S], BF16)
    BARIN = nc.dram_tensor("BARIN", [1, 16], F32)
    BAROUT = nc.dram_tensor("BAROUT", [4, 16], F32)
    X1D = nc.dram_tensor("X1D", [TOK, E], F32)

    g = nc.gpsimd
    sy = nc.sync
    t = nc.tensor
    v = nc.vector
    sc = nc.scalar

    with ExitStack() as top:
        et = top.enter_context
        ones_col = et(nc.sbuf_tensor("ones_col", [128, 1], BF16))
        ones_row = et(nc.sbuf_tensor("ones_row", [1, 128], BF16))
        id16 = et(nc.sbuf_tensor("id16", [128, 128], BF16))
        id32 = et(nc.sbuf_tensor("id32", [128, 128], F32))
        eps_b = et(nc.sbuf_tensor("eps_b", [128, 1], F32))
        zb = et(nc.sbuf_tensor("zb", [1, 16], F32))

        tC = et(nc.semaphore("tC"))
        aC = et(nc.semaphore("aC"))
        vC = et(nc.semaphore("vC"))
        cC = et(nc.semaphore("cC"))

        class Stream:

            def __init__(self, name):
                self.sem = et(nc.semaphore(name))
                self.v = 0

            def dma(self, eng, out, in_):
                eng.dma_start(out=out, in_=in_).then_inc(self.sem, 16)
                self.v += 16
                return self.v

            def wait(self, eng, val=None):
                eng.wait_ge(self.sem, self.v if val is None else val)

        g_xt = [Stream("g_xt0"), Stream("g_xt1")]
        g_w = Stream("g_w")
        g_wot = Stream("g_wot")
        g_wf = Stream("g_wf")
        g_x1 = [Stream("g_x10"), Stream("g_x11")]
        s_const = Stream("s_const")
        s_masks = Stream("s_masks")
        s_agin = Stream("s_agin")
        s_ag = [Stream("s_ag0"), Stream("s_ag1")]
        s_xr = [Stream("s_xr0"), Stream("s_xr1")]
        s_x1st = [Stream("s_x1st0"), Stream("s_x1st1")]
        s_y = [Stream("s_y0"), Stream("s_y1")]
        s_bar = Stream("s_bar")
        s_dump = Stream("s_dump")

        n = {"tC": 0, "aC": 0, "vC": 0}

        def vsync(ins):
            ev = vinc(ins)
            v.wait_ge(vC, ev)
            return ev

        def tinc(ins):
            ins.then_inc(tC, 1)
            n["tC"] += 1
            return n["tC"]

        def ainc(ins):
            ins.then_inc(aC, 1)
            n["aC"] += 1
            return n["aC"]

        def vinc(ins):
            ins.then_inc(vC, 1)
            n["vC"] += 1
            return n["vC"]

        v.memset(ones_col[:, :], 1.0)
        v.memset(ones_row[:, :], 1.0)
        v.memset(eps_b[:, :], LN_EPS)
        zb_evt = vinc(v.memset(zb[:, :], 0.0))
        s_const.dma(sy, id16[:, :], ID16)
        s_const.dma(sy, id32[:, :], ID32)

        with ExitStack() as abx:
            eab = abx.enter_context
            qs = eab(nc.sbuf_tensor("qs", [128, HG * S], BF16, side="right"))
            ks = eab(nc.sbuf_tensor("ks", [128, HG * S], BF16, side="right"))
            vs = eab(nc.sbuf_tensor("vs", [128, HG * 32 * 128], BF16, side="right"))

            with ExitStack() as ax:
                ea = ax.enter_context
                wq = ea(nc.sbuf_tensor("wq", [128, 16 * 512], BF16, side="right"))
                wk = ea(nc.sbuf_tensor("wk", [128, 16 * 512], BF16, side="right"))
                wv = ea(nc.sbuf_tensor("wv", [128, 16 * 512], BF16, side="right"))
                xtb = [ea(nc.sbuf_tensor(f"xtb{i}", [128, 16 * 512], BF16, side="right"))
                       for i in range(2)]
                cos = ea(nc.sbuf_tensor("cos", [64, S], BF16, side="right"))
                sin = ea(nc.sbuf_tensor("sin", [64, S], BF16, side="right"))
                rts = ea(nc.sbuf_tensor("rts", [64, 64], BF16, side="right"))
                rtmp1 = ea(nc.sbuf_tensor("rtmp1", [64, 512], BF16, side="right"))
                rtmp2 = ea(nc.sbuf_tensor("rtmp2", [64, 512], BF16, side="right"))
                qk_ps = [ea(nc.psum_tensor(f"qk_ps{i}", [128, 512], F32))
                         for i in range(2)]
                v_ps = [ea(nc.psum_tensor(f"v_ps{i}", [128, 512], F32))
                        for i in range(2)]
                rz_ps = [ea(nc.psum_tensor(f"rz_ps{i}", [64, 512], F32))
                         for i in range(2)]

                s_const.dma(sy, cos[:, :], COS)
                s_const.dma(sy, sin[:, :], SIN)
                s_const.dma(sy, rts[:, :], RTD)

                def load_xt(i):
                    return g_xt[i % 2].dma(
                        g,
                        xtb[i % 2][:, :].rearrange("p (ec t) -> p ec t", ec=16),
                        XT.rearrange("(ec p) t -> p ec t", p=128)[
                            :, :, i * 512:(i + 1) * 512],
                    )

                load_xt(0)
                g_w.dma(g, wq[:, :].rearrange("p (ec f) -> p ec f", ec=16),
                        WQT.rearrange("(ec p) f -> p ec f", p=128))
                g_w.dma(g, wk[:, :].rearrange("p (ec f) -> p ec f", ec=16),
                        WKT.rearrange("(ec p) f -> p ec f", p=128))
                g_w.dma(g, wv[:, :].rearrange("p (ec f) -> p ec f", ec=16),
                        WVT.rearrange("(ec p) f -> p ec f", p=128))
                load_xt(1)

                qk_free = [0, 0]
                v_free = [0, 0]
                tile_done_t = [0] * 8

                for it in range(8):
                    t0 = it * 512
                    t.wait_ge(g_xt[it % 2].sem, 16 * (it // 2 + 1))
                    if it == 0:
                        g_w.wait(t)
                    for fc in range(8):
                        w = wq if fc < 4 else wk
                        hl = fc % 4
                        ps = qk_ps[fc % 2]
                        if qk_free[fc % 2]:
                            t.wait_ge(vC, qk_free[fc % 2])
                        for ec in range(16):
                            ins = t.matmul(
                                ps[:, :],
                                w[:, ec * 512 + hl * 128: ec * 512 + hl * 128 + 128],
                                xtb[it % 2][:, ec * 512:(ec + 1) * 512],
                                start=(ec == 0), stop=(ec == 15),
                            )
                        evt = tinc(ins)
                        dst = qs if fc < 4 else ks
                        v.wait_ge(tC, evt)
                        qk_free[fc % 2] = vinc(v.tensor_copy(
                            dst[:, hl * S + t0: hl * S + t0 + 512], ps[:, :]))
                    for sub in range(4):
                        ps = v_ps[sub % 2]
                        if v_free[sub % 2]:
                            t.wait_ge(vC, v_free[sub % 2])
                        for ec in range(16):
                            ins = t.matmul(
                                ps[:, :],
                                xtb[it % 2][:, ec * 512 + sub * 128:
                                            ec * 512 + sub * 128 + 128],
                                wv[:, ec * 512:(ec + 1) * 512],
                                start=(ec == 0), stop=(ec == 15),
                            )
                        evt = tinc(ins)
                        kt = it * 4 + sub
                        v.wait_ge(tC, evt)
                        v_free[sub % 2] = vinc(v.tensor_copy(
                            vs[:, :].rearrange("p (h k d) -> p h k d",
                                               h=HG, k=32)[:, :, kt, :],
                            ps[:, :].rearrange("p (h d) -> p h d", h=HG)))
                    tile_done_t[it] = n["tC"]
                    if it + 2 < 8:
                        g.wait_ge(tC, tile_done_t[it])
                        load_xt(it + 2)

                qkv_evac_all = n["vC"]
                t.wait_ge(vC, qkv_evac_all)
                s_const.wait(v)
                s_const.wait(t)
                rz_free = [0, 0]
                ri = 0
                for zs in (qs, ks):
                    for h in range(HG):
                        for sl in range(8):
                            s0 = sl * 512
                            par = ri % 2
                            ri += 1
                            ps = rz_ps[par]
                            if rz_free[par]:
                                t.wait_ge(vC, rz_free[par])
                            ins = t.matmul(
                                ps[:, :], rts[:, :],
                                zs[0:64, h * S + s0: h * S + s0 + 512],
                                start=True, stop=True)
                            evt = tinc(ins)
                            v.wait_ge(tC, evt)
                            v.tensor_mul(rtmp1[:, :], ps[:, :],
                                         sin[:, s0:s0 + 512])
                            vsync(v.tensor_mul(
                                rtmp2[:, :],
                                zs[0:64, h * S + s0: h * S + s0 + 512],
                                cos[:, s0:s0 + 512]))
                            rz_free[par] = vinc(v.tensor_add(
                                zs[0:64, h * S + s0: h * S + s0 + 512],
                                rtmp1[:, :], rtmp2[:, :]))
                rope_done_v = n["vC"]
                rope_done_t = n["tC"]

            if PH == "A":
                sy.wait_ge(vC, rope_done_v)
                s_dump.dma(sy, QD, qs[:, :])
                s_dump.dma(sy, KD, ks[:, :])
                s_dump.dma(sy, VD, vs[:, :])
                s_dump.wait(sy)
                return nc

            wot = et(nc.sbuf_tensor("wot", [128, 16 * E], BF16))
            g.wait_ge(tC, rope_done_t)
            g.wait_ge(vC, rope_done_v)
            g_wot.dma(g, wot[:, :].rearrange("p (ec f) -> p ec f", ec=16),
                      WOT.rearrange("(ec p) f -> p ec f", p=128))

            with ExitStack() as bx:
                ebx = bx.enter_context
                ao = ebx(nc.sbuf_tensor("ao", [128, HG * S], BF16, side="right"))
                masks = ebx(nc.sbuf_tensor("masks", [128, 4 * QG], BF16, side="right"))
                pt = [ebx(nc.sbuf_tensor(f"pt{i}", [128, QG], BF16, side="right"))
                      for i in range(2)]
                nrec = [ebx(nc.sbuf_tensor(f"nrec{i}", [1, QG], BF16, side="right"))
                        for i in range(2)]
                bc_sb = [ebx(nc.sbuf_tensor(f"bc_sb{i}", [128, QG], BF16, side="right"))
                         for i in range(2)]
                s_ps = [ebx(nc.psum_tensor(f"s_ps{i}", [128, QG], F32))
                        for i in range(2)]
                o_ps = [ebx(nc.psum_tensor(f"o_ps{i}", [128, QG], F32))
                        for i in range(2)]
                n_ps = [ebx(nc.psum_tensor(f"n_ps{i}", [1, QG], F32))
                        for i in range(2)]
                b_ps = [ebx(nc.psum_tensor(f"b_ps{i}", [128, QG], F32))
                        for i in range(2)]

                sy.wait_ge(tC, rope_done_t)
                sy.wait_ge(vC, rope_done_v)
                s_masks.dma(sy, masks[:, :], MASKS)
                s_masks.wait(v)
                t.wait_ge(vC, rope_done_v)

                pt_evt_a = [0, 0]
                pt_evt_v = [0, 0]
                pt_masked = [False, False]
                pt_used = [0, 0]
                s_exped = [0, 0]
                on_free = [0, 0]
                ktc = 0
                for h in range(HG):
                    for qg in range(8):
                        q0 = qg * QG
                        ob = (h * 8 + qg) % 2
                        if on_free[ob]:
                            t.wait_ge(vC, on_free[ob])
                        nkt = 4 * (qg + 1)
                        for kt in range(nkt):
                            par = ktc % 2
                            ktc += 1
                            if s_exped[par]:
                                t.wait_ge(aC, s_exped[par])
                            ins = t.matmul(
                                s_ps[par][:, :],
                                ks[:, h * S + kt * 128:
                                   h * S + kt * 128 + 128],
                                qs[:, h * S + q0: h * S + q0 + QG],
                                start=True, stop=True,
                                skip_group_check=True)
                            s_evt = tinc(ins)
                            sc.wait_ge(tC, s_evt)
                            if pt_used[par]:
                                sc.wait_ge(tC, pt_used[par])
                            s_exped[par] = ainc(sc.activation(
                                pt[par][:, :], s_ps[par][:, :], AFT.Exp,
                                scale=INV_SQRT_D))
                            delta = kt - 4 * qg
                            if delta >= 0:
                                v.wait_ge(aC, s_exped[par])
                                pt_evt_v[par] = vinc(v.tensor_mul(
                                    pt[par][:, :], pt[par][:, :],
                                    masks[:, delta * QG:(delta + 1) * QG]))
                                pt_masked[par] = True
                            else:
                                pt_evt_a[par] = s_exped[par]
                                pt_masked[par] = False
                            if pt_masked[par]:
                                t.wait_ge(vC, pt_evt_v[par])
                            else:
                                t.wait_ge(aC, pt_evt_a[par])
                            t.matmul(o_ps[ob][:, :],
                                     vs[:, (h * 32 + kt) * 128:
                                        (h * 32 + kt) * 128 + 128],
                                     pt[par][:, :],
                                     start=(kt == 0), stop=(kt == nkt - 1),
                                     skip_group_check=True)
                            ins = t.matmul(
                                n_ps[ob][0:1, :], ones_col[:, :],
                                pt[par][:, :],
                                start=(kt == 0), stop=(kt == nkt - 1),
                                skip_group_check=True)
                            pt_used[par] = tinc(ins)
                        v.wait_ge(tC, pt_used[(ktc - 1) % 2])
                        with nc.allow_low_precision(
                                reason="softmax normalizer reciprocal kept in "
                                       "bf16; same precision class as the "
                                       "bf16 attention matmul path"):
                            rec_evt = vinc(v.reciprocal(nrec[ob][:, :],
                                                        n_ps[ob][0:1, :]))
                        t.wait_ge(vC, rec_evt)
                        ins = t.matmul(
                            b_ps[ob][:, :], ones_row[:, :],
                            nrec[ob][:, :], start=True, stop=True,
                            skip_group_check=True)
                        bc_evt = tinc(ins)
                        v.wait_ge(tC, bc_evt)
                        vsync(v.tensor_copy(bc_sb[ob][:, :], b_ps[ob][:, :]))
                        on_free[ob] = vinc(v.tensor_mul(
                            ao[:, h * S + q0: h * S + q0 + QG],
                            o_ps[ob][:, :], bc_sb[ob][:, :]))
                ao_done_v = n["vC"]

                if PH == "B":
                    sy.wait_ge(vC, ao_done_v)
                    s_dump.dma(sy, AOD, ao[:, :])
                    s_dump.wait(sy)
                    return nc

                sy.wait_ge(vC, ao_done_v)
                sy.wait_ge(vC, zb_evt)
                s_bar.dma(sy, BARIN.ap(), zb[:, :])
                s_agin.dma(sy, AGIN.rearrange("(h p) q -> p h q", p=128),
                           ao[:, :].rearrange("p (h q) -> p h q", h=HG))
                s_agin.wait(g)
                s_bar.wait(g)
                g.collective_compute(
                    "AllGather", ALU.bypass,
                    replica_groups=[[0, 1, 2, 3], [4, 5, 6, 7]],
                    ins=[AGIN.ap().opt()], outs=[AGOUT.ap().opt()],
                ).then_inc(cC)
                g.collective_compute(
                    "AllGather", ALU.bypass,
                    replica_groups=[[0, 1, 2, 3], [4, 5, 6, 7]],
                    ins=[BARIN.ap().opt()], outs=[BAROUT.ap().opt()],
                ).then_inc(cC)

        if PH == "X":
            sy.wait_ge(cC, 2)
            s_dump.dma(sy, XOD, AGOUT.ap())
            s_dump.wait(sy)
            return nc

        t0_gp = (g.partition_id() % 4) * TOK

        with ExitStack() as cx:
            ecx = cx.enter_context
            wft = ecx(nc.sbuf_tensor("wft", [128, 16 * E], BF16))
            gat = ecx(nc.sbuf_tensor("gat", [128, 16 * 136], BF16))
            btb = ecx(nc.sbuf_tensor("btb", [128, E], BF16))
            g_wf.dma(g, wft[:, :].rearrange("p (ec f) -> p ec f", ec=16),
                     WFT.rearrange("(ec p) f -> p ec f", p=128))
            g_wf.dma(g, gat[:, :].rearrange("p (ec f) -> p ec f", ec=16),
                     GAT.rearrange("(ec p) f -> p ec f", p=128))
            g_wf.dma(g, btb[:, :], BTD)

            def ln_tail(resid_ap, scr2_ap, lb, out_ap, resid_evt_v,
                        scr2_free):
                ve1 = vinc(v.reduce_sum(lb["mraw"][:, :], resid_ap, AX.X))
                sc.wait_ge(vC, resid_evt_v)
                if scr2_free[0]:
                    sc.wait_ge(vC, scr2_free[0])
                a1 = ainc(sc.activation(scr2_ap, resid_ap, AFT.Square))
                v.wait_ge(vC, ve1)
                v.tensor_scalar_mul(lb["m"][:, :], lb["mraw"][:, :], 1.0 / E)
                v.wait_ge(aC, a1)
                scr2_free[0] = vsync(v.reduce_sum(lb["s2"][:, :], scr2_ap,
                                                  AX.X))
                vsync(v.tensor_mul(lb["msq"][:, :], lb["m"][:, :],
                                   lb["m"][:, :]))
                ve4 = vinc(v.scalar_tensor_tensor(
                    lb["var"][:, :], lb["s2"][:, :], 1.0 / E,
                    lb["msq"][:, :], ALU.mult, ALU.subtract))
                sc.wait_ge(vC, ve4)
                aln = ainc(sc.activation(lb["lnv"][:, :], lb["var"][:, :],
                                         AFT.Ln, bias=eps_b[:, :]))
                sc.wait_ge(aC, aln)
                ae = ainc(sc.activation(lb["rstd"][:, :], lb["lnv"][:, :],
                                        AFT.Exp, scale=-0.5))
                v.wait_ge(aC, ae)
                vsync(v.tensor_scalar(lb["nmr"][:, :], lb["m"][:, :],
                                      lb["rstd"][:, :], -1.0, ALU.mult,
                                      ALU.mult))
                return vinc(v.tensor_scalar(out_ap, resid_ap,
                                            lb["rstd"][:, :], lb["nmr"][:, :],
                                            ALU.mult, ALU.add))

            with ExitStack() as c1x:
                e1 = c1x.enter_context
                agsb = [e1(nc.sbuf_tensor(f"agsb{i}", [128, 16 * 128], BF16))
                        for i in range(2)]
                xr = [e1(nc.sbuf_tensor(f"xr{i}", [128, E], F32))
                      for i in range(2)]
                resid = [e1(nc.sbuf_tensor(f"resid{i}", [128, E], F32))
                         for i in range(2)]
                x1o = [e1(nc.sbuf_tensor(f"x1o{i}", [128, E], F32))
                       for i in range(2)]
                scr2 = e1(nc.sbuf_tensor("scr2", [128, E], F32))
                scr2_free = [0]
                lnb = [{k: e1(nc.sbuf_tensor(f"ln_{k}{i}", [128, 1], F32))
                        for k in ("mraw", "m", "s2", "msq", "var", "lnv",
                                  "rstd", "nmr")} for i in range(2)]
                op_ps = [e1(nc.psum_tensor(f"op_ps{i}", [128, E], F32))
                         for i in range(2)]

                def load_agsb(i):
                    base = AGOUT.rearrange("(ec p) q -> p ec q", p=128)[
                        :, :, i * 128:(i + 1) * 128]
                    dyn = AP(base.tensor, base.offset + t0_gp, base.ap)
                    return s_ag[i % 2].dma(
                        g,
                        agsb[i % 2][:, :].rearrange("p (ec c) -> p ec c",
                                                    ec=16), dyn)

                def load_xr(i):
                    return s_xr[i % 2].dma(sy, xr[i % 2][:, :],
                                           XR[i * 128:(i + 1) * 128, :])

                g.wait_ge(cC, 2)
                load_agsb(0)
                if PH == "AG2":
                    s_ag[0].wait(sy)
                    s_dump.dma(sy, AGD, agsb[0][:, :])
                    s_dump.wait(sy)
                    return nc
                load_xr(0)
                load_agsb(1)
                load_xr(1)

                mm_done = [0] * 8
                resid_evt = [0] * 8
                opfree = [0, 0]
                x1_evt = [0] * 8
                g_wot.wait(t)
                for i in range(8):
                    pb = i % 2
                    t.wait_ge(s_ag[i % 2].sem, 16 * (i // 2 + 1))
                    if opfree[pb]:
                        t.wait_ge(vC, opfree[pb])
                    for ec in range(16):
                        for f4 in range(4):
                            ins = t.matmul(
                                op_ps[pb][:, f4 * 512:(f4 + 1) * 512],
                                agsb[pb][:, ec * 128:(ec + 1) * 128],
                                wot[:, ec * E + f4 * 512:
                                    ec * E + f4 * 512 + 512],
                                start=(ec == 0), stop=(ec == 15),
                                skip_group_check=True,
                            )
                    mm_done[i] = tinc(ins)
                    v.wait_ge(tC, mm_done[i])
                    v.wait_ge(s_xr[i % 2].sem, 16 * (i // 2 + 1))
                    for f4 in range(4):
                        ins = v.scalar_tensor_tensor(
                            resid[pb][:, f4 * 512:(f4 + 1) * 512],
                            op_ps[pb][:, f4 * 512:(f4 + 1) * 512], 1.0,
                            xr[pb][:, f4 * 512:(f4 + 1) * 512],
                            ALU.mult, ALU.add)
                    opfree[pb] = vinc(ins)
                    resid_evt[i] = opfree[pb]
                    if PH == "OP" and i == 0:
                        sy.wait_ge(vC, resid_evt[0])
                        s_dump.dma(sy, OPD, resid[0][:, :])
                        s_dump.wait(sy)
                        return nc
                    if i >= 2:
                        v.wait_ge(s_x1st[i % 2].sem, 16 * (i // 2))
                    x1_evt[i] = ln_tail(resid[pb][:, :], scr2[:, :],
                                        lnb[pb], x1o[pb][:, :], resid_evt[i],
                                        scr2_free)
                    sy.wait_ge(vC, x1_evt[i])
                    if PH == "L1" and i == 0:
                        s_dump.dma(sy, L1D, x1o[0][:, :])
                        s_dump.wait(sy)
                        return nc
                    s_x1st[i % 2].dma(sy, X1D[i * 128:(i + 1) * 128, :],
                                      x1o[pb][:, :])
                    if i + 2 < 8:
                        g.wait_ge(tC, mm_done[i])
                        load_agsb(i + 2)
                        sy.wait_ge(vC, resid_evt[i])
                        load_xr(i + 2)
            if PH == "C1":
                s_x1st[0].wait(sy)
                s_x1st[1].wait(sy)
                s_dump.dma(sy, X1O, X1D.ap())
                s_dump.wait(sy)
                return nc

            with ExitStack() as c2x:
                e2 = c2x.enter_context
                x1i = [e2(nc.sbuf_tensor(f"x1i{i}", [128, E], F32))
                       for i in range(2)]
                x1t = [e2(nc.sbuf_tensor(f"x1t{i}", [128, 16 * 128], BF16))
                       for i in range(2)]
                ysb = [e2(nc.sbuf_tensor(f"ysb{i}", [128, E], F32))
                       for i in range(2)]
                resid2 = [e2(nc.sbuf_tensor(f"resid2{i}", [128, E], F32))
                          for i in range(2)]
                qe = e2(nc.sbuf_tensor("qe", [128, 8], F32))
                qlt = e2(nc.sbuf_tensor("qlt", [128, 8], F32))
                cw = e2(nc.sbuf_tensor("cw", [128, 8], F32))
                g1 = e2(nc.sbuf_tensor("g1", [128, 1], F32))
                g2 = e2(nc.sbuf_tensor("g2", [128, 1], F32))
                gneg = e2(nc.sbuf_tensor("gneg", [128, 1], F32))
                gss = e2(nc.sbuf_tensor("gss", [128, 1], F32))
                t_bf = e2(nc.sbuf_tensor("t_bf", [128, 128], BF16))
                tt_sb = e2(nc.sbuf_tensor("tt_sb", [128, 128], BF16))
                scr2b = e2(nc.sbuf_tensor("scr2b", [128, E], F32))
                scr2b_free = [0]
                lnb2 = [{k: e2(nc.sbuf_tensor(f"ln2_{k}{i}", [128, 1], F32))
                         for k in ("mraw", "m", "s2", "msq", "var", "lnv",
                                   "rstd", "nmr")} for i in range(2)]
                mn_ps = e2(nc.psum_tensor("mn_ps", [128, E], F32))
                ga_ps = e2(nc.psum_tensor("ga_ps", [128, 136], F32))
                tt_ps = e2(nc.psum_tensor("tt_ps", [128, 128], BF16))
                x1t_ps = [e2(nc.psum_tensor(f"x1t_ps{i}", [128, 512], F32))
                          for i in range(2)]

                def load_x1(i):
                    return g_x1[i % 2].dma(g, x1i[i % 2][:, :],
                                           X1D[i * 128:(i + 1) * 128, :])

                s_x1st[0].wait(g)
                s_x1st[1].wait(g)
                g.wait_ge(vC, x1_evt[7])
                g.wait_ge(tC, mm_done[7])
                load_x1(0)
                load_x1(1)

                g_wf.wait(t)
                s_const.wait(t)

                x1t_evt = [0] * 8
                ga_done = [0] * 8
                x1t_psfree = [0, 0]
                mnfree = [0] * 8
                gafree = 0
                ttfree = 0
                y_evt = [0] * 8

                for i in range(8):
                    pb = i % 2
                    t.wait_ge(g_x1[i % 2].sem, 16 * (i // 2 + 1))
                    for q4 in range(4):
                        qp = q4 % 2
                        if x1t_psfree[qp]:
                            t.wait_ge(vC, x1t_psfree[qp])
                        for j in range(4):
                            ins = t.transpose(
                                x1t_ps[qp][:, j * 128:(j + 1) * 128],
                                x1i[pb][:, (q4 * 4 + j) * 128:
                                        (q4 * 4 + j) * 128 + 128],
                                id32[:, :])
                        evt = tinc(ins)
                        v.wait_ge(tC, evt)
                        if i >= 2:
                            v.wait_ge(tC, ga_done[i - 2])
                        x1t_psfree[qp] = vinc(v.tensor_copy(
                            x1t[pb][:, q4 * 512:(q4 + 1) * 512],
                            x1t_ps[qp][:, :]))
                    x1t_evt[i] = n["vC"]

                    t.wait_ge(vC, x1t_evt[i])
                    if mnfree[i - 1] and i >= 1:
                        t.wait_ge(vC, mnfree[i - 1])
                    if gafree:
                        t.wait_ge(vC, gafree)
                    for ec in range(16):
                        t.matmul(ga_ps[:, :],
                                 x1t[pb][:, ec * 128:(ec + 1) * 128],
                                 gat[:, ec * 136:(ec + 1) * 136],
                                 start=(ec == 0), stop=(ec == 15),
                                 skip_group_check=True)
                        for f4 in range(4):
                            ins = t.matmul(
                                mn_ps[:, f4 * 512:(f4 + 1) * 512],
                                x1t[pb][:, ec * 128:(ec + 1) * 128],
                                wft[:, ec * E + f4 * 512:
                                     ec * E + f4 * 512 + 512],
                                start=(ec == 0), stop=False,
                                skip_group_check=True)
                    ga_done[i] = tinc(ins)

                    v.wait_ge(tC, ga_done[i])
                    ve = vinc(v.tensor_reduce(gneg[:, :], ga_ps[:, 0:8],
                                              AX.X, ALU.max, negate=True))
                    sc.wait_ge(vC, ve)
                    ae = ainc(sc.activation(qe[:, :], ga_ps[:, 0:8], AFT.Exp,
                                            bias=gneg[:, :]))
                    v.wait_ge(aC, ae)
                    vsync(v.reduce_max(g1[:, :], qe[:, :], AX.X))
                    vsync(v.tensor_single_scalar(qlt[:, :], qe[:, :],
                                                 g1[:, :], ALU.is_lt))
                    vsync(v.tensor_mul(qlt[:, :], qe[:, :], qlt[:, :]))
                    vsync(v.reduce_max(g2[:, :], qlt[:, :], AX.X))
                    vsync(v.tensor_add(gss[:, :], g1[:, :], g2[:, :]))
                    vsync(v.reciprocal(gss[:, :], gss[:, :]))
                    vsync(v.tensor_scalar_mul(gss[:, :], gss[:, :], 16.0))
                    vsync(v.tensor_single_scalar(qlt[:, :], qe[:, :],
                                                 g2[:, :], ALU.is_ge))
                    vsync(v.scalar_tensor_tensor(cw[:, :], qe[:, :],
                                                 gss[:, :], qlt[:, :],
                                                 ALU.mult, ALU.mult))
                    for ex in range(8):
                        ins = v.tensor_scalar(
                            t_bf[:, ex * 16:(ex + 1) * 16],
                            ga_ps[:, 8 + ex * 16: 8 + ex * 16 + 16],
                            cw[:, ex:ex + 1], None, ALU.mult)
                    gafree = vinc(ins)

                    t.wait_ge(vC, gafree)
                    if ttfree:
                        t.wait_ge(vC, ttfree)
                    tte = tinc(t.transpose(tt_ps[:, :], t_bf[:, :],
                                           id16[:, :]))
                    v.wait_ge(tC, tte)
                    ttv = vinc(v.tensor_copy(tt_sb[:, :], tt_ps[:, :]))
                    ttfree = ttv
                    t.wait_ge(vC, ttv)
                    for f4 in range(4):
                        ins = t.matmul(mn_ps[:, f4 * 512:(f4 + 1) * 512],
                                       tt_sb[:, :],
                                       btb[:, f4 * 512:(f4 + 1) * 512],
                                       start=False, stop=True,
                                       skip_group_check=True)
                    ffn_done = tinc(ins)

                    v.wait_ge(tC, ffn_done)
                    for f4 in range(4):
                        ins = v.scalar_tensor_tensor(
                            resid2[pb][:, f4 * 512:(f4 + 1) * 512],
                            mn_ps[:, f4 * 512:(f4 + 1) * 512], 1.0,
                            x1i[pb][:, f4 * 512:(f4 + 1) * 512],
                            ALU.mult, ALU.add)
                    mnfree[i] = vinc(ins)
                    if i >= 2:
                        v.wait_ge(s_y[i % 2].sem, 16 * (i // 2))
                    y_evt[i] = ln_tail(resid2[pb][:, :], scr2b[:, :],
                                       lnb2[pb], ysb[pb][:, :], mnfree[i],
                                       scr2b_free)
                    sy.wait_ge(vC, y_evt[i])
                    s_y[i % 2].dma(sy, Y[i * 128:(i + 1) * 128, :],
                                   ysb[pb][:, :])
                    if i + 2 < 8:
                        g.wait_ge(vC, mnfree[i])
                        load_x1(i + 2)

                s_y[0].wait(sy)
                s_y[1].wait(sy)

    return nc


_NC = None


def _get_nc():
    global _NC
    if _NC is None:
        _NC = _build_nc()
    return _NC


def _make_consts():
    ROT = 64
    inv_freq = (1.0 / 10000.0 ** (np.arange(0, ROT, 2) / ROT)).astype(np.float32)
    pos = np.arange(S, dtype=np.float32)
    freqs = pos[None, :] * inv_freq[:, None]
    cos = np.repeat(np.cos(freqs), 2, axis=0).astype(ml_dtypes.bfloat16)
    sin = np.repeat(np.sin(freqs), 2, axis=0).astype(ml_dtypes.bfloat16)
    R = np.zeros((64, 64), np.float32)
    for i in range(32):
        R[2 * i, 2 * i + 1] = -1.0
        R[2 * i + 1, 2 * i] = 1.0
    RT = np.ascontiguousarray(R.T).astype(ml_dtypes.bfloat16)
    id16 = np.eye(128, dtype=np.float32).astype(ml_dtypes.bfloat16)
    id32 = np.eye(128, dtype=np.float32)
    k = np.arange(128)[:, None]
    q = np.arange(QG)[None, :]
    masks = np.concatenate(
        [(k + 128 * d <= q).astype(np.float32) for d in range(4)],
        axis=1).astype(ml_dtypes.bfloat16)
    return cos, sin, RT, id16, id32, masks


def _make_in_maps(L0_attn_hidden_states, L0_attn_w_qkv, L0_attn_w_out,
                  L0_ffn_weight, L0_ffn_gate_w, L0_ffn_A_w, L0_ffn_B_w):
    X = np.ascontiguousarray(np.asarray(L0_attn_hidden_states, np.float32))
    Wqkv = np.asarray(L0_attn_w_qkv, np.float32)
    Wout = np.asarray(L0_attn_w_out, np.float32)
    Wffn = np.asarray(L0_ffn_weight, np.float32)
    Gw = np.asarray(L0_ffn_gate_w, np.float32)
    Aw = np.asarray(L0_ffn_A_w, np.float32)
    Bw = np.asarray(L0_ffn_B_w, np.float32)

    cos, sin, RT, id16, id32, masks = _make_consts()
    WOTn = np.ascontiguousarray(Wout.T)
    WFTn = np.ascontiguousarray(Wffn.T)
    GATn = np.ascontiguousarray(np.concatenate(
        [Gw.T, Aw.reshape(128, E).T], axis=1))
    BTDn = np.ascontiguousarray(Bw.transpose(0, 2, 1).reshape(128, E))

    in_maps = []
    for c in range(8):
        b, gq = c // 4, c % 4
        in_maps.append({
            "XT": np.ascontiguousarray(X[b].T),
            "XR": np.ascontiguousarray(X[b, gq * TOK:(gq + 1) * TOK]),
            "WQT": np.ascontiguousarray(Wqkv[gq * 1536: gq * 1536 + 512].T),
            "WVT": np.ascontiguousarray(
                Wqkv[gq * 1536 + 512: gq * 1536 + 1024].T),
            "WKT": np.ascontiguousarray(
                Wqkv[gq * 1536 + 1024: gq * 1536 + 1536].T),
            "WOT": WOTn, "WFT": WFTn, "GAT": GATn, "BTD": BTDn,
            "COS": cos, "SIN": sin, "RTD": RT, "ID16": id16, "ID32": id32,
            "MASKS": masks,
        })

    return in_maps


def kernel(**inputs):
    in_maps = _make_in_maps(**inputs)
    nc = _get_nc()
    res = run_bass_kernel_spmd(nc, in_maps, core_ids=list(range(8)))
    out = np.empty((B, S, E), np.float32)
    for c in range(8):
        b, gq = c // 4, c % 4
        out[b, gq * TOK:(gq + 1) * TOK] = res.results[c]["Y"]
    return out


# version 05
# speedup vs baseline: 1.5782x; 1.5782x over previous
import numpy as np
import ml_dtypes
from contextlib import ExitStack

import concourse.bass as bass
import concourse.mybir as mybir
from concourse.ap import AP
from concourse.bass_utils import run_bass_kernel_spmd

F32 = mybir.dt.float32
BF16 = mybir.dt.bfloat16
AFT = mybir.ActivationFunctionType
ALU = mybir.AluOpType
AX = mybir.AxisListType

_PHASES = "full"
_DETECT_RACES = False

B, S, E = 2, 4096, 2048
HG = 4
TOK = 1024
QG = 512
INV_SQRT_D = float(1.0 / np.sqrt(128.0))
LN_EPS = 1e-5


def _build_nc():
    nc = bass.Bass("TRN2", target_bir_lowering=False, debug=False,
                   detect_race_conditions=_DETECT_RACES)

    XT = nc.dram_tensor("XT", [E, S], F32, kind="ExternalInput").ap()
    XR = nc.dram_tensor("XR", [TOK, E], F32, kind="ExternalInput").ap()
    WQT = nc.dram_tensor("WQT", [E, 512], F32, kind="ExternalInput").ap()
    WKT = nc.dram_tensor("WKT", [E, 512], F32, kind="ExternalInput").ap()
    WVT = nc.dram_tensor("WVT", [E, 512], F32, kind="ExternalInput").ap()
    WOT = nc.dram_tensor("WOT", [E, E], F32, kind="ExternalInput").ap()
    WFT = nc.dram_tensor("WFT", [E, E], F32, kind="ExternalInput").ap()
    GAT = nc.dram_tensor("GAT", [E, 136], F32, kind="ExternalInput").ap()
    BTD = nc.dram_tensor("BTD", [128, E], F32, kind="ExternalInput").ap()
    COS = nc.dram_tensor("COS", [64, S], BF16, kind="ExternalInput").ap()
    SIN = nc.dram_tensor("SIN", [64, S], BF16, kind="ExternalInput").ap()
    RTD = nc.dram_tensor("RTD", [64, 64], BF16, kind="ExternalInput").ap()
    ID16 = nc.dram_tensor("ID16", [128, 128], BF16, kind="ExternalInput").ap()
    ID32 = nc.dram_tensor("ID32", [128, 128], F32, kind="ExternalInput").ap()
    MASKS = nc.dram_tensor("MASKS", [128, 4 * QG], BF16, kind="ExternalInput").ap()
    Y = nc.dram_tensor("Y", [TOK, E], F32, kind="ExternalOutput").ap()
    PH = _PHASES
    if PH == "A":
        QD = nc.dram_tensor("QD", [128, HG * S], BF16, kind="ExternalOutput").ap()
        KD = nc.dram_tensor("KD", [128, HG * S], BF16, kind="ExternalOutput").ap()
        VD = nc.dram_tensor("VD", [128, HG * 32 * 128], BF16, kind="ExternalOutput").ap()
    if PH == "B":
        AOD = nc.dram_tensor("AOD", [128, HG * S], BF16, kind="ExternalOutput").ap()
    if PH == "X":
        XOD = nc.dram_tensor("XOD", [E, S], BF16, kind="ExternalOutput").ap()
    if PH == "C1":
        X1O = nc.dram_tensor("X1O", [TOK, E], F32, kind="ExternalOutput").ap()
    if PH == "AG2":
        AGD = nc.dram_tensor("AGD", [128, 16 * 128], BF16, kind="ExternalOutput").ap()
    if PH == "OP":
        OPD = nc.dram_tensor("OPD", [128, E], F32, kind="ExternalOutput").ap()
    if PH == "L1":
        L1D = nc.dram_tensor("L1D", [128, E], F32, kind="ExternalOutput").ap()

    AGINH = [nc.dram_tensor(f"AGINH{h}", [128, S], BF16) for h in range(HG)]
    AGOUTH = [nc.dram_tensor(f"AGOUTH{h}", [4 * 128, S], BF16)
              for h in range(HG)]
    BARIN = nc.dram_tensor("BARIN", [1, 16], F32)
    BAROUT = nc.dram_tensor("BAROUT", [4, 16], F32)
    X1D = nc.dram_tensor("X1D", [TOK, E], F32)

    g = nc.gpsimd
    sy = nc.sync
    t = nc.tensor
    v = nc.vector
    sc = nc.scalar

    with ExitStack() as top:
        et = top.enter_context
        ones_col = et(nc.sbuf_tensor("ones_col", [128, 1], BF16))
        ones_row = et(nc.sbuf_tensor("ones_row", [1, 128], BF16))
        id16 = et(nc.sbuf_tensor("id16", [128, 128], BF16))
        id32 = et(nc.sbuf_tensor("id32", [128, 128], F32))
        eps_b = et(nc.sbuf_tensor("eps_b", [128, 1], F32))
        zb = et(nc.sbuf_tensor("zb", [1, 16], F32))

        tC = et(nc.semaphore("tC"))
        aC = et(nc.semaphore("aC"))
        vC = et(nc.semaphore("vC"))
        cC = et(nc.semaphore("cC"))

        class Stream:

            def __init__(self, name):
                self.sem = et(nc.semaphore(name))
                self.v = 0

            def dma(self, eng, out, in_):
                eng.dma_start(out=out, in_=in_).then_inc(self.sem, 16)
                self.v += 16
                return self.v

            def wait(self, eng, val=None):
                eng.wait_ge(self.sem, self.v if val is None else val)

        g_xt = [Stream("g_xt0"), Stream("g_xt1")]
        g_w = Stream("g_w")
        g_wot = Stream("g_wot")
        g_wf = Stream("g_wf")
        g_x1 = [Stream("g_x10"), Stream("g_x11")]
        s_const = Stream("s_const")
        s_masks = Stream("s_masks")
        s_agin = Stream("s_agin")
        s_ag = [Stream("s_ag0"), Stream("s_ag1")]
        s_xr = [Stream("s_xr0"), Stream("s_xr1")]
        s_x1st = [Stream("s_x1st0"), Stream("s_x1st1")]
        s_y = [Stream("s_y0"), Stream("s_y1")]
        s_bar = Stream("s_bar")
        s_dump = Stream("s_dump")

        n = {"tC": 0, "aC": 0, "vC": 0}

        def vsync(ins):
            ev = vinc(ins)
            v.wait_ge(vC, ev)
            return ev

        def tinc(ins):
            ins.then_inc(tC, 1)
            n["tC"] += 1
            return n["tC"]

        def ainc(ins):
            ins.then_inc(aC, 1)
            n["aC"] += 1
            return n["aC"]

        def vinc(ins):
            ins.then_inc(vC, 1)
            n["vC"] += 1
            return n["vC"]

        v.memset(ones_col[:, :], 1.0)
        v.memset(ones_row[:, :], 1.0)
        v.memset(eps_b[:, :], LN_EPS)
        zb_evt = vinc(v.memset(zb[:, :], 0.0))
        s_const.dma(sy, id16[:, :], ID16)
        s_const.dma(sy, id32[:, :], ID32)

        with ExitStack() as abx:
            eab = abx.enter_context
            qs = eab(nc.sbuf_tensor("qs", [128, HG * S], BF16, side="right"))
            ks = eab(nc.sbuf_tensor("ks", [128, HG * S], BF16, side="right"))
            vs = eab(nc.sbuf_tensor("vs", [128, HG * 32 * 128], BF16, side="right"))

            with ExitStack() as ax:
                ea = ax.enter_context
                wq = ea(nc.sbuf_tensor("wq", [128, 16 * 512], BF16, side="right"))
                wk = ea(nc.sbuf_tensor("wk", [128, 16 * 512], BF16, side="right"))
                wv = ea(nc.sbuf_tensor("wv", [128, 16 * 512], BF16, side="right"))
                xtb = [ea(nc.sbuf_tensor(f"xtb{i}", [128, 16 * 512], BF16, side="right"))
                       for i in range(2)]
                cos = ea(nc.sbuf_tensor("cos", [64, S], BF16, side="right"))
                sin = ea(nc.sbuf_tensor("sin", [64, S], BF16, side="right"))
                rts = ea(nc.sbuf_tensor("rts", [64, 64], BF16, side="right"))
                rtmp1 = ea(nc.sbuf_tensor("rtmp1", [64, 512], BF16, side="right"))
                rtmp2 = ea(nc.sbuf_tensor("rtmp2", [64, 512], BF16, side="right"))
                qk_ps = [ea(nc.psum_tensor(f"qk_ps{i}", [128, 512], F32))
                         for i in range(2)]
                v_ps = [ea(nc.psum_tensor(f"v_ps{i}", [128, 512], F32))
                        for i in range(2)]
                rz_ps = [ea(nc.psum_tensor(f"rz_ps{i}", [64, 512], F32))
                         for i in range(2)]

                s_const.dma(sy, cos[:, :], COS)
                s_const.dma(sy, sin[:, :], SIN)
                s_const.dma(sy, rts[:, :], RTD)

                def load_xt(i):
                    return g_xt[i % 2].dma(
                        g,
                        xtb[i % 2][:, :].rearrange("p (ec t) -> p ec t", ec=16),
                        XT.rearrange("(ec p) t -> p ec t", p=128)[
                            :, :, i * 512:(i + 1) * 512],
                    )

                load_xt(0)
                g_w.dma(g, wq[:, :].rearrange("p (ec f) -> p ec f", ec=16),
                        WQT.rearrange("(ec p) f -> p ec f", p=128))
                g_w.dma(g, wk[:, :].rearrange("p (ec f) -> p ec f", ec=16),
                        WKT.rearrange("(ec p) f -> p ec f", p=128))
                g_w.dma(g, wv[:, :].rearrange("p (ec f) -> p ec f", ec=16),
                        WVT.rearrange("(ec p) f -> p ec f", p=128))
                load_xt(1)

                qk_free = [0, 0]
                v_free = [0, 0]
                tile_done_t = [0] * 8

                for it in range(8):
                    t0 = it * 512
                    t.wait_ge(g_xt[it % 2].sem, 16 * (it // 2 + 1))
                    if it == 0:
                        g_w.wait(t)
                    for fc in range(8):
                        w = wq if fc < 4 else wk
                        hl = fc % 4
                        ps = qk_ps[fc % 2]
                        if qk_free[fc % 2]:
                            t.wait_ge(vC, qk_free[fc % 2])
                        for ec in range(16):
                            ins = t.matmul(
                                ps[:, :],
                                w[:, ec * 512 + hl * 128: ec * 512 + hl * 128 + 128],
                                xtb[it % 2][:, ec * 512:(ec + 1) * 512],
                                start=(ec == 0), stop=(ec == 15),
                            )
                        evt = tinc(ins)
                        dst = qs if fc < 4 else ks
                        v.wait_ge(tC, evt)
                        qk_free[fc % 2] = vinc(v.tensor_copy(
                            dst[:, hl * S + t0: hl * S + t0 + 512], ps[:, :]))
                    for sub in range(4):
                        ps = v_ps[sub % 2]
                        if v_free[sub % 2]:
                            t.wait_ge(vC, v_free[sub % 2])
                        for ec in range(16):
                            ins = t.matmul(
                                ps[:, :],
                                xtb[it % 2][:, ec * 512 + sub * 128:
                                            ec * 512 + sub * 128 + 128],
                                wv[:, ec * 512:(ec + 1) * 512],
                                start=(ec == 0), stop=(ec == 15),
                            )
                        evt = tinc(ins)
                        kt = it * 4 + sub
                        v.wait_ge(tC, evt)
                        v_free[sub % 2] = vinc(v.tensor_copy(
                            vs[:, :].rearrange("p (h k d) -> p h k d",
                                               h=HG, k=32)[:, :, kt, :],
                            ps[:, :].rearrange("p (h d) -> p h d", h=HG)))
                    tile_done_t[it] = n["tC"]
                    if it + 2 < 8:
                        g.wait_ge(tC, tile_done_t[it])
                        load_xt(it + 2)

                qkv_evac_all = n["vC"]
                t.wait_ge(vC, qkv_evac_all)
                s_const.wait(v)
                s_const.wait(t)
                rz_free = [0, 0]
                ri = 0
                for zs in (qs, ks):
                    for h in range(HG):
                        for sl in range(8):
                            s0 = sl * 512
                            par = ri % 2
                            ri += 1
                            ps = rz_ps[par]
                            if rz_free[par]:
                                t.wait_ge(vC, rz_free[par])
                            ins = t.matmul(
                                ps[:, :], rts[:, :],
                                zs[0:64, h * S + s0: h * S + s0 + 512],
                                start=True, stop=True)
                            evt = tinc(ins)
                            v.wait_ge(tC, evt)
                            v.tensor_mul(rtmp1[:, :], ps[:, :],
                                         sin[:, s0:s0 + 512])
                            vsync(v.tensor_mul(
                                rtmp2[:, :],
                                zs[0:64, h * S + s0: h * S + s0 + 512],
                                cos[:, s0:s0 + 512]))
                            rz_free[par] = vinc(v.tensor_add(
                                zs[0:64, h * S + s0: h * S + s0 + 512],
                                rtmp1[:, :], rtmp2[:, :]))
                rope_done_v = n["vC"]
                rope_done_t = n["tC"]

            if PH == "A":
                sy.wait_ge(vC, rope_done_v)
                s_dump.dma(sy, QD, qs[:, :])
                s_dump.dma(sy, KD, ks[:, :])
                s_dump.dma(sy, VD, vs[:, :])
                s_dump.wait(sy)
                return nc

            wot = et(nc.sbuf_tensor("wot", [128, 16 * E], BF16))
            g.wait_ge(tC, rope_done_t)
            g.wait_ge(vC, rope_done_v)
            g_wot.dma(g, wot[:, :].rearrange("p (ec f) -> p ec f", ec=16),
                      WOT.rearrange("(ec p) f -> p ec f", p=128))

            with ExitStack() as bx:
                ebx = bx.enter_context
                ao = ebx(nc.sbuf_tensor("ao", [128, HG * S], BF16, side="right"))
                masks = ebx(nc.sbuf_tensor("masks", [128, 4 * QG], BF16, side="right"))
                pt = [ebx(nc.sbuf_tensor(f"pt{i}", [128, QG], BF16, side="right"))
                      for i in range(2)]
                nrec = [ebx(nc.sbuf_tensor(f"nrec{i}", [1, QG], BF16, side="right"))
                        for i in range(2)]
                bc_sb = [ebx(nc.sbuf_tensor(f"bc_sb{i}", [128, QG], BF16, side="right"))
                         for i in range(2)]
                s_ps = [ebx(nc.psum_tensor(f"s_ps{i}", [128, QG], F32))
                        for i in range(2)]
                o_ps = [ebx(nc.psum_tensor(f"o_ps{i}", [128, QG], F32))
                        for i in range(2)]
                n_ps = [ebx(nc.psum_tensor(f"n_ps{i}", [1, QG], F32))
                        for i in range(2)]
                b_ps = [ebx(nc.psum_tensor(f"b_ps{i}", [128, QG], F32))
                        for i in range(2)]

                sy.wait_ge(tC, rope_done_t)
                sy.wait_ge(vC, rope_done_v)
                s_masks.dma(sy, masks[:, :], MASKS)
                s_masks.wait(v)
                t.wait_ge(vC, rope_done_v)

                sy.wait_ge(vC, zb_evt)
                s_bar.dma(sy, BARIN.ap(), zb[:, :])

                pt_evt_a = [0, 0]
                pt_evt_v = [0, 0]
                pt_masked = [False, False]
                pt_used = [0, 0]
                s_exped = [0, 0]
                on_free = [0, 0]
                rec_evt_ob = [0, 0]
                ktc = 0
                state = {"pending": None, "bc": None}

                def emit_pv():
                    if state["pending"] is None:
                        return
                    par, ob2, hh, kt2, first, last, q0b = state["pending"]
                    state["pending"] = None
                    if pt_masked[par]:
                        t.wait_ge(vC, pt_evt_v[par])
                    else:
                        t.wait_ge(aC, pt_evt_a[par])
                    t.matmul(o_ps[ob2][:, :],
                             vs[:, (hh * 32 + kt2) * 128:
                                (hh * 32 + kt2) * 128 + 128],
                             pt[par][:, :], start=first, stop=last,
                             skip_group_check=True)
                    ins2 = t.matmul(n_ps[ob2][0:1, :], ones_col[:, :],
                                    pt[par][:, :], start=first, stop=last,
                                    skip_group_check=True)
                    pt_used[par] = tinc(ins2)
                    if last:
                        v.wait_ge(tC, pt_used[par])
                        with nc.allow_low_precision(
                                reason="softmax normalizer reciprocal kept in"
                                       " bf16; same precision class as the "
                                       "bf16 attention matmul path"):
                            rec_evt_ob[ob2] = vinc(v.reciprocal(
                                nrec[ob2][:, :], n_ps[ob2][0:1, :]))
                        state["bc"] = (ob2, hh, q0b)

                def emit_bcast():
                    if state["bc"] is None:
                        return
                    ob2, hh, q0b = state["bc"]
                    state["bc"] = None
                    t.wait_ge(vC, rec_evt_ob[ob2])
                    ins3 = t.matmul(b_ps[ob2][:, :], ones_row[:, :],
                                    nrec[ob2][:, :], start=True, stop=True,
                                    skip_group_check=True)
                    bce = tinc(ins3)
                    v.wait_ge(tC, bce)
                    vsync(v.tensor_copy(bc_sb[ob2][:, :], b_ps[ob2][:, :]))
                    on_free[ob2] = vinc(v.tensor_mul(
                        ao[:, hh * S + q0b: hh * S + q0b + QG],
                        o_ps[ob2][:, :], bc_sb[ob2][:, :]))
                    if q0b == 7 * QG:
                        sy.wait_ge(vC, on_free[ob2])
                        s_agin.dma(sy, AGINH[hh].ap(),
                                   ao[:, hh * S:(hh + 1) * S])
                        s_agin.wait(g)
                        g.collective_compute(
                            "AllGather", ALU.bypass,
                            replica_groups=[[0, 1, 2, 3], [4, 5, 6, 7]],
                            ins=[AGINH[hh].ap().opt()],
                            outs=[AGOUTH[hh].ap().opt()],
                        ).then_inc(cC)

                for gi in range(HG * 8):
                    h, qg = gi // 8, gi % 8
                    q0 = qg * QG
                    ob = gi % 2
                    if on_free[ob]:
                        t.wait_ge(vC, on_free[ob])
                    nkt = 4 * (qg + 1)
                    for kt in range(nkt):
                        par = ktc % 2
                        ktc += 1
                        if s_exped[par]:
                            t.wait_ge(aC, s_exped[par])
                        ins = t.matmul(
                            s_ps[par][:, :],
                            ks[:, h * S + kt * 128: h * S + kt * 128 + 128],
                            qs[:, h * S + q0: h * S + q0 + QG],
                            start=True, stop=True, skip_group_check=True)
                        s_evt = tinc(ins)
                        emit_pv()
                        if kt >= 1:
                            emit_bcast()
                        sc.wait_ge(tC, s_evt)
                        if pt_used[par]:
                            sc.wait_ge(tC, pt_used[par])
                        s_exped[par] = ainc(sc.activation(
                            pt[par][:, :], s_ps[par][:, :], AFT.Exp,
                            scale=INV_SQRT_D))
                        delta = kt - 4 * qg
                        if delta >= 0:
                            v.wait_ge(aC, s_exped[par])
                            pt_evt_v[par] = vinc(v.tensor_mul(
                                pt[par][:, :], pt[par][:, :],
                                masks[:, delta * QG:(delta + 1) * QG]))
                            pt_masked[par] = True
                        else:
                            pt_evt_a[par] = s_exped[par]
                            pt_masked[par] = False
                        state["pending"] = (par, ob, h, kt, kt == 0,
                                            kt == nkt - 1, q0)
                emit_pv()
                emit_bcast()
                ao_done_v = n["vC"]

                if PH == "B":
                    sy.wait_ge(vC, ao_done_v)
                    s_dump.dma(sy, AOD, ao[:, :])
                    s_dump.wait(sy)
                    return nc

                s_bar.wait(g)
                g.collective_compute(
                    "AllGather", ALU.bypass,
                    replica_groups=[[0, 1, 2, 3], [4, 5, 6, 7]],
                    ins=[BARIN.ap().opt()], outs=[BAROUT.ap().opt()],
                ).then_inc(cC)

        if PH == "X":
            sy.wait_ge(cC, 5)
            for hh in range(HG):
                s_dump.dma(sy, XOD.rearrange("(cg hh p) q -> cg hh p q",
                                             hh=4, p=128)[:, hh],
                           AGOUTH[hh].ap().rearrange("(cg p) q -> cg p q",
                                                     p=128))
            s_dump.wait(sy)
            return nc

        t0_gp = (g.partition_id() % 4) * TOK

        with ExitStack() as cx:
            ecx = cx.enter_context
            wft = ecx(nc.sbuf_tensor("wft", [128, 16 * E], BF16))
            gat = ecx(nc.sbuf_tensor("gat", [128, 16 * 136], BF16))
            btb = ecx(nc.sbuf_tensor("btb", [128, E], BF16))
            g_wf.dma(g, wft[:, :].rearrange("p (ec f) -> p ec f", ec=16),
                     WFT.rearrange("(ec p) f -> p ec f", p=128))
            g_wf.dma(g, gat[:, :].rearrange("p (ec f) -> p ec f", ec=16),
                     GAT.rearrange("(ec p) f -> p ec f", p=128))
            g_wf.dma(g, btb[:, :], BTD)

            def ln_tail(resid_ap, scr2_ap, lb, out_ap, resid_evt_v,
                        scr2_free):
                ve1 = vinc(v.reduce_sum(lb["mraw"][:, :], resid_ap, AX.X))
                sc.wait_ge(vC, resid_evt_v)
                if scr2_free[0]:
                    sc.wait_ge(vC, scr2_free[0])
                a1 = ainc(sc.activation(scr2_ap, resid_ap, AFT.Square))
                v.wait_ge(vC, ve1)
                v.tensor_scalar_mul(lb["m"][:, :], lb["mraw"][:, :], 1.0 / E)
                v.wait_ge(aC, a1)
                scr2_free[0] = vsync(v.reduce_sum(lb["s2"][:, :], scr2_ap,
                                                  AX.X))
                vsync(v.tensor_mul(lb["msq"][:, :], lb["m"][:, :],
                                   lb["m"][:, :]))
                ve4 = vinc(v.scalar_tensor_tensor(
                    lb["var"][:, :], lb["s2"][:, :], 1.0 / E,
                    lb["msq"][:, :], ALU.mult, ALU.subtract))
                sc.wait_ge(vC, ve4)
                aln = ainc(sc.activation(lb["lnv"][:, :], lb["var"][:, :],
                                         AFT.Ln, bias=eps_b[:, :]))
                sc.wait_ge(aC, aln)
                ae = ainc(sc.activation(lb["rstd"][:, :], lb["lnv"][:, :],
                                        AFT.Exp, scale=-0.5))
                v.wait_ge(aC, ae)
                vsync(v.tensor_scalar(lb["nmr"][:, :], lb["m"][:, :],
                                      lb["rstd"][:, :], -1.0, ALU.mult,
                                      ALU.mult))
                return vinc(v.tensor_scalar(out_ap, resid_ap,
                                            lb["rstd"][:, :], lb["nmr"][:, :],
                                            ALU.mult, ALU.add))

            with ExitStack() as c1x:
                e1 = c1x.enter_context
                agsb = [e1(nc.sbuf_tensor(f"agsb{i}", [128, 16 * 128], BF16))
                        for i in range(2)]
                xr = [e1(nc.sbuf_tensor(f"xr{i}", [128, E], F32))
                      for i in range(2)]
                resid = [e1(nc.sbuf_tensor(f"resid{i}", [128, E], F32))
                         for i in range(2)]
                x1o = [e1(nc.sbuf_tensor(f"x1o{i}", [128, E], F32))
                       for i in range(2)]
                scr2 = e1(nc.sbuf_tensor("scr2", [128, E], F32))
                scr2_free = [0]
                lnb = [{k: e1(nc.sbuf_tensor(f"ln_{k}{i}", [128, 1], F32))
                        for k in ("mraw", "m", "s2", "msq", "var", "lnv",
                                  "rstd", "nmr")} for i in range(2)]
                op_ps = [e1(nc.psum_tensor(f"op_ps{i}", [128, E], F32))
                         for i in range(2)]

                def load_agsb(i):
                    for hh in range(HG):
                        base = AGOUTH[hh].ap().rearrange(
                            "(cg p) q -> p cg q", p=128)[
                            :, :, i * 128:(i + 1) * 128]
                        dyn = AP(base.tensor, base.offset + t0_gp, base.ap)
                        s_ag[i % 2].dma(
                            g,
                            agsb[i % 2][:, :].rearrange(
                                "p (cg hh c) -> p cg hh c", hh=4, c=128)[
                                :, :, hh], dyn)
                    return s_ag[i % 2].v

                def load_xr(i):
                    return s_xr[i % 2].dma(sy, xr[i % 2][:, :],
                                           XR[i * 128:(i + 1) * 128, :])

                g.wait_ge(cC, 5)
                load_agsb(0)
                if PH == "AG2":
                    s_ag[0].wait(sy)
                    s_dump.dma(sy, AGD, agsb[0][:, :])
                    s_dump.wait(sy)
                    return nc
                load_xr(0)
                load_agsb(1)
                load_xr(1)

                mm_done = [0] * 8
                resid_evt = [0] * 8
                opfree = [0, 0]
                x1_evt = [0] * 8
                g_wot.wait(t)
                for i in range(8):
                    pb = i % 2
                    t.wait_ge(s_ag[i % 2].sem, 64 * (i // 2 + 1))
                    if opfree[pb]:
                        t.wait_ge(vC, opfree[pb])
                    for ec in range(16):
                        for f4 in range(4):
                            ins = t.matmul(
                                op_ps[pb][:, f4 * 512:(f4 + 1) * 512],
                                agsb[pb][:, ec * 128:(ec + 1) * 128],
                                wot[:, ec * E + f4 * 512:
                                    ec * E + f4 * 512 + 512],
                                start=(ec == 0), stop=(ec == 15),
                                skip_group_check=True,
                            )
                    mm_done[i] = tinc(ins)
                    v.wait_ge(tC, mm_done[i])
                    v.wait_ge(s_xr[i % 2].sem, 16 * (i // 2 + 1))
                    for f4 in range(4):
                        ins = v.scalar_tensor_tensor(
                            resid[pb][:, f4 * 512:(f4 + 1) * 512],
                            op_ps[pb][:, f4 * 512:(f4 + 1) * 512], 1.0,
                            xr[pb][:, f4 * 512:(f4 + 1) * 512],
                            ALU.mult, ALU.add)
                    opfree[pb] = vinc(ins)
                    resid_evt[i] = opfree[pb]
                    if PH == "OP" and i == 0:
                        sy.wait_ge(vC, resid_evt[0])
                        s_dump.dma(sy, OPD, resid[0][:, :])
                        s_dump.wait(sy)
                        return nc
                    if i >= 2:
                        v.wait_ge(s_x1st[i % 2].sem, 16 * (i // 2))
                    x1_evt[i] = ln_tail(resid[pb][:, :], scr2[:, :],
                                        lnb[pb], x1o[pb][:, :], resid_evt[i],
                                        scr2_free)
                    sy.wait_ge(vC, x1_evt[i])
                    if PH == "L1" and i == 0:
                        s_dump.dma(sy, L1D, x1o[0][:, :])
                        s_dump.wait(sy)
                        return nc
                    s_x1st[i % 2].dma(sy, X1D[i * 128:(i + 1) * 128, :],
                                      x1o[pb][:, :])
                    if i + 2 < 8:
                        g.wait_ge(tC, mm_done[i])
                        load_agsb(i + 2)
                        sy.wait_ge(vC, resid_evt[i])
                        load_xr(i + 2)
            if PH == "C1":
                s_x1st[0].wait(sy)
                s_x1st[1].wait(sy)
                s_dump.dma(sy, X1O, X1D.ap())
                s_dump.wait(sy)
                return nc

            with ExitStack() as c2x:
                e2 = c2x.enter_context
                x1i = [e2(nc.sbuf_tensor(f"x1i{i}", [128, E], F32))
                       for i in range(2)]
                x1t = [e2(nc.sbuf_tensor(f"x1t{i}", [128, 16 * 128], BF16))
                       for i in range(2)]
                ysb = [e2(nc.sbuf_tensor(f"ysb{i}", [128, E], F32))
                       for i in range(2)]
                resid2 = [e2(nc.sbuf_tensor(f"resid2{i}", [128, E], F32))
                          for i in range(2)]
                qe = e2(nc.sbuf_tensor("qe", [128, 8], F32))
                qlt = e2(nc.sbuf_tensor("qlt", [128, 8], F32))
                cw = e2(nc.sbuf_tensor("cw", [128, 8], F32))
                g1 = e2(nc.sbuf_tensor("g1", [128, 1], F32))
                g2 = e2(nc.sbuf_tensor("g2", [128, 1], F32))
                gneg = e2(nc.sbuf_tensor("gneg", [128, 1], F32))
                gss = e2(nc.sbuf_tensor("gss", [128, 1], F32))
                t_bf = e2(nc.sbuf_tensor("t_bf", [128, 128], BF16))
                tt_sb = e2(nc.sbuf_tensor("tt_sb", [128, 128], BF16))
                scr2b = e2(nc.sbuf_tensor("scr2b", [128, E], F32))
                scr2b_free = [0]
                lnb2 = [{k: e2(nc.sbuf_tensor(f"ln2_{k}{i}", [128, 1], F32))
                         for k in ("mraw", "m", "s2", "msq", "var", "lnv",
                                   "rstd", "nmr")} for i in range(2)]
                mn_ps = e2(nc.psum_tensor("mn_ps", [128, E], F32))
                ga_ps = e2(nc.psum_tensor("ga_ps", [128, 136], F32))
                tt_ps = e2(nc.psum_tensor("tt_ps", [128, 128], BF16))
                x1t_ps = [e2(nc.psum_tensor(f"x1t_ps{i}", [128, 512], F32))
                          for i in range(2)]

                def load_x1(i):
                    return g_x1[i % 2].dma(g, x1i[i % 2][:, :],
                                           X1D[i * 128:(i + 1) * 128, :])

                s_x1st[0].wait(g)
                s_x1st[1].wait(g)
                g.wait_ge(vC, x1_evt[7])
                g.wait_ge(tC, mm_done[7])
                load_x1(0)
                load_x1(1)

                g_wf.wait(t)
                s_const.wait(t)

                x1t_evt = [0] * 8
                ga_done = [0] * 8
                x1t_psfree = [0, 0]
                mnfree = [0] * 8
                gafree = 0
                ttfree = 0
                y_evt = [0] * 8

                for i in range(8):
                    pb = i % 2
                    t.wait_ge(g_x1[i % 2].sem, 16 * (i // 2 + 1))
                    for q4 in range(4):
                        qp = q4 % 2
                        if x1t_psfree[qp]:
                            t.wait_ge(vC, x1t_psfree[qp])
                        for j in range(4):
                            ins = t.transpose(
                                x1t_ps[qp][:, j * 128:(j + 1) * 128],
                                x1i[pb][:, (q4 * 4 + j) * 128:
                                        (q4 * 4 + j) * 128 + 128],
                                id32[:, :])
                        evt = tinc(ins)
                        v.wait_ge(tC, evt)
                        if i >= 2:
                            v.wait_ge(tC, ga_done[i - 2])
                        x1t_psfree[qp] = vinc(v.tensor_copy(
                            x1t[pb][:, q4 * 512:(q4 + 1) * 512],
                            x1t_ps[qp][:, :]))
                    x1t_evt[i] = n["vC"]

                    t.wait_ge(vC, x1t_evt[i])
                    if mnfree[i - 1] and i >= 1:
                        t.wait_ge(vC, mnfree[i - 1])
                    if gafree:
                        t.wait_ge(vC, gafree)
                    for ec in range(16):
                        t.matmul(ga_ps[:, :],
                                 x1t[pb][:, ec * 128:(ec + 1) * 128],
                                 gat[:, ec * 136:(ec + 1) * 136],
                                 start=(ec == 0), stop=(ec == 15),
                                 skip_group_check=True)
                        for f4 in range(4):
                            ins = t.matmul(
                                mn_ps[:, f4 * 512:(f4 + 1) * 512],
                                x1t[pb][:, ec * 128:(ec + 1) * 128],
                                wft[:, ec * E + f4 * 512:
                                     ec * E + f4 * 512 + 512],
                                start=(ec == 0), stop=False,
                                skip_group_check=True)
                    ga_done[i] = tinc(ins)

                    v.wait_ge(tC, ga_done[i])
                    ve = vinc(v.tensor_reduce(gneg[:, :], ga_ps[:, 0:8],
                                              AX.X, ALU.max, negate=True))
                    sc.wait_ge(vC, ve)
                    ae = ainc(sc.activation(qe[:, :], ga_ps[:, 0:8], AFT.Exp,
                                            bias=gneg[:, :]))
                    v.wait_ge(aC, ae)
                    vsync(v.reduce_max(g1[:, :], qe[:, :], AX.X))
                    vsync(v.tensor_single_scalar(qlt[:, :], qe[:, :],
                                                 g1[:, :], ALU.is_lt))
                    vsync(v.tensor_mul(qlt[:, :], qe[:, :], qlt[:, :]))
                    vsync(v.reduce_max(g2[:, :], qlt[:, :], AX.X))
                    vsync(v.tensor_add(gss[:, :], g1[:, :], g2[:, :]))
                    vsync(v.reciprocal(gss[:, :], gss[:, :]))
                    vsync(v.tensor_scalar_mul(gss[:, :], gss[:, :], 16.0))
                    vsync(v.tensor_single_scalar(qlt[:, :], qe[:, :],
                                                 g2[:, :], ALU.is_ge))
                    vsync(v.scalar_tensor_tensor(cw[:, :], qe[:, :],
                                                 gss[:, :], qlt[:, :],
                                                 ALU.mult, ALU.mult))
                    for ex in range(8):
                        ins = v.tensor_scalar(
                            t_bf[:, ex * 16:(ex + 1) * 16],
                            ga_ps[:, 8 + ex * 16: 8 + ex * 16 + 16],
                            cw[:, ex:ex + 1], None, ALU.mult)
                    gafree = vinc(ins)

                    t.wait_ge(vC, gafree)
                    if ttfree:
                        t.wait_ge(vC, ttfree)
                    tte = tinc(t.transpose(tt_ps[:, :], t_bf[:, :],
                                           id16[:, :]))
                    v.wait_ge(tC, tte)
                    ttv = vinc(v.tensor_copy(tt_sb[:, :], tt_ps[:, :]))
                    ttfree = ttv
                    t.wait_ge(vC, ttv)
                    for f4 in range(4):
                        ins = t.matmul(mn_ps[:, f4 * 512:(f4 + 1) * 512],
                                       tt_sb[:, :],
                                       btb[:, f4 * 512:(f4 + 1) * 512],
                                       start=False, stop=True,
                                       skip_group_check=True)
                    ffn_done = tinc(ins)

                    v.wait_ge(tC, ffn_done)
                    for f4 in range(4):
                        ins = v.scalar_tensor_tensor(
                            resid2[pb][:, f4 * 512:(f4 + 1) * 512],
                            mn_ps[:, f4 * 512:(f4 + 1) * 512], 1.0,
                            x1i[pb][:, f4 * 512:(f4 + 1) * 512],
                            ALU.mult, ALU.add)
                    mnfree[i] = vinc(ins)
                    if i >= 2:
                        v.wait_ge(s_y[i % 2].sem, 16 * (i // 2))
                    y_evt[i] = ln_tail(resid2[pb][:, :], scr2b[:, :],
                                       lnb2[pb], ysb[pb][:, :], mnfree[i],
                                       scr2b_free)
                    sy.wait_ge(vC, y_evt[i])
                    s_y[i % 2].dma(sy, Y[i * 128:(i + 1) * 128, :],
                                   ysb[pb][:, :])
                    if i + 2 < 8:
                        g.wait_ge(vC, mnfree[i])
                        load_x1(i + 2)

                s_y[0].wait(sy)
                s_y[1].wait(sy)

    return nc


_NC = None


def _get_nc():
    global _NC
    if _NC is None:
        _NC = _build_nc()
    return _NC


def _make_consts():
    ROT = 64
    inv_freq = (1.0 / 10000.0 ** (np.arange(0, ROT, 2) / ROT)).astype(np.float32)
    pos = np.arange(S, dtype=np.float32)
    freqs = pos[None, :] * inv_freq[:, None]
    cos = np.repeat(np.cos(freqs), 2, axis=0).astype(ml_dtypes.bfloat16)
    sin = np.repeat(np.sin(freqs), 2, axis=0).astype(ml_dtypes.bfloat16)
    R = np.zeros((64, 64), np.float32)
    for i in range(32):
        R[2 * i, 2 * i + 1] = -1.0
        R[2 * i + 1, 2 * i] = 1.0
    RT = np.ascontiguousarray(R.T).astype(ml_dtypes.bfloat16)
    id16 = np.eye(128, dtype=np.float32).astype(ml_dtypes.bfloat16)
    id32 = np.eye(128, dtype=np.float32)
    k = np.arange(128)[:, None]
    q = np.arange(QG)[None, :]
    masks = np.concatenate(
        [(k + 128 * d <= q).astype(np.float32) for d in range(4)],
        axis=1).astype(ml_dtypes.bfloat16)
    return cos, sin, RT, id16, id32, masks


def _make_in_maps(L0_attn_hidden_states, L0_attn_w_qkv, L0_attn_w_out,
                  L0_ffn_weight, L0_ffn_gate_w, L0_ffn_A_w, L0_ffn_B_w):
    X = np.ascontiguousarray(np.asarray(L0_attn_hidden_states, np.float32))
    Wqkv = np.asarray(L0_attn_w_qkv, np.float32)
    Wout = np.asarray(L0_attn_w_out, np.float32)
    Wffn = np.asarray(L0_ffn_weight, np.float32)
    Gw = np.asarray(L0_ffn_gate_w, np.float32)
    Aw = np.asarray(L0_ffn_A_w, np.float32)
    Bw = np.asarray(L0_ffn_B_w, np.float32)

    cos, sin, RT, id16, id32, masks = _make_consts()
    WOTn = np.ascontiguousarray(Wout.T)
    WFTn = np.ascontiguousarray(Wffn.T)
    GATn = np.ascontiguousarray(np.concatenate(
        [Gw.T, Aw.reshape(128, E).T], axis=1))
    BTDn = np.ascontiguousarray(Bw.transpose(0, 2, 1).reshape(128, E))

    in_maps = []
    for c in range(8):
        b, gq = c // 4, c % 4
        in_maps.append({
            "XT": np.ascontiguousarray(X[b].T),
            "XR": np.ascontiguousarray(X[b, gq * TOK:(gq + 1) * TOK]),
            "WQT": np.ascontiguousarray(Wqkv[gq * 1536: gq * 1536 + 512].T),
            "WVT": np.ascontiguousarray(
                Wqkv[gq * 1536 + 512: gq * 1536 + 1024].T),
            "WKT": np.ascontiguousarray(
                Wqkv[gq * 1536 + 1024: gq * 1536 + 1536].T),
            "WOT": WOTn, "WFT": WFTn, "GAT": GATn, "BTD": BTDn,
            "COS": cos, "SIN": sin, "RTD": RT, "ID16": id16, "ID32": id32,
            "MASKS": masks,
        })

    return in_maps


def kernel(**inputs):
    in_maps = _make_in_maps(**inputs)
    nc = _get_nc()
    res = run_bass_kernel_spmd(nc, in_maps, core_ids=list(range(8)))
    out = np.empty((B, S, E), np.float32)
    for c in range(8):
        b, gq = c // 4, c % 4
        out[b, gq * TOK:(gq + 1) * TOK] = res.results[c]["Y"]
    return out


# version 06
# speedup vs baseline: 1.6557x; 1.0491x over previous
import numpy as np
import ml_dtypes
from contextlib import ExitStack

import concourse.bass as bass
import concourse.mybir as mybir
from concourse.ap import AP
from concourse.bass_utils import run_bass_kernel_spmd

F32 = mybir.dt.float32
BF16 = mybir.dt.bfloat16
AFT = mybir.ActivationFunctionType
ALU = mybir.AluOpType
AX = mybir.AxisListType

_PHASES = "full"
_DETECT_RACES = False

B, S, E = 2, 4096, 2048
HG = 4
TOK = 1024
QG = 512
INV_SQRT_D = float(1.0 / np.sqrt(128.0))
LN_EPS = 1e-5


def _build_nc():
    nc = bass.Bass("TRN2", target_bir_lowering=False, debug=False,
                   detect_race_conditions=_DETECT_RACES)

    XT = nc.dram_tensor("XT", [E, S], F32, kind="ExternalInput").ap()
    XR = nc.dram_tensor("XR", [TOK, E], F32, kind="ExternalInput").ap()
    WQT = nc.dram_tensor("WQT", [E, 512], F32, kind="ExternalInput").ap()
    WKT = nc.dram_tensor("WKT", [E, 512], F32, kind="ExternalInput").ap()
    WVT = nc.dram_tensor("WVT", [E, 512], F32, kind="ExternalInput").ap()
    WOT = nc.dram_tensor("WOT", [E, E], F32, kind="ExternalInput").ap()
    WFT = nc.dram_tensor("WFT", [E, E], F32, kind="ExternalInput").ap()
    GAT = nc.dram_tensor("GAT", [E, 136], F32, kind="ExternalInput").ap()
    BTD = nc.dram_tensor("BTD", [128, E], F32, kind="ExternalInput").ap()
    COS = nc.dram_tensor("COS", [64, S], BF16, kind="ExternalInput").ap()
    SIN = nc.dram_tensor("SIN", [64, S], BF16, kind="ExternalInput").ap()
    RTD = nc.dram_tensor("RTD", [64, 64], BF16, kind="ExternalInput").ap()
    ID16 = nc.dram_tensor("ID16", [128, 128], BF16, kind="ExternalInput").ap()
    ID32 = nc.dram_tensor("ID32", [128, 128], F32, kind="ExternalInput").ap()
    MASKS = nc.dram_tensor("MASKS", [128, 4 * QG], BF16, kind="ExternalInput").ap()
    Y = nc.dram_tensor("Y", [TOK, E], F32, kind="ExternalOutput").ap()
    PH = _PHASES
    if PH == "A":
        QD = nc.dram_tensor("QD", [128, HG * S], BF16, kind="ExternalOutput").ap()
        KD = nc.dram_tensor("KD", [128, HG * S], BF16, kind="ExternalOutput").ap()
        VD = nc.dram_tensor("VD", [128, HG * 32 * 128], BF16, kind="ExternalOutput").ap()
    if PH == "B":
        AOD = nc.dram_tensor("AOD", [128, HG * S], BF16, kind="ExternalOutput").ap()
    if PH == "X":
        XOD = nc.dram_tensor("XOD", [E, S], BF16, kind="ExternalOutput").ap()
    if PH == "C1":
        X1O = nc.dram_tensor("X1O", [TOK, E], F32, kind="ExternalOutput").ap()
    if PH == "AG2":
        AGD = nc.dram_tensor("AGD", [128, 16 * 128], BF16, kind="ExternalOutput").ap()
    if PH == "OP":
        OPD = nc.dram_tensor("OPD", [128, E], F32, kind="ExternalOutput").ap()
    if PH == "L1":
        L1D = nc.dram_tensor("L1D", [128, E], F32, kind="ExternalOutput").ap()

    AGINH = [nc.dram_tensor(f"AGINH{h}", [128, S], BF16) for h in range(HG)]
    AGOUTH = [nc.dram_tensor(f"AGOUTH{h}", [4 * 128, S], BF16)
              for h in range(HG)]
    BARIN = nc.dram_tensor("BARIN", [1, 16], F32)
    BAROUT = nc.dram_tensor("BAROUT", [4, 16], F32)
    X1D = nc.dram_tensor("X1D", [TOK, E], F32)

    g = nc.gpsimd
    sy = nc.sync
    t = nc.tensor
    v = nc.vector
    sc = nc.scalar

    with ExitStack() as top:
        et = top.enter_context
        ones_col = et(nc.sbuf_tensor("ones_col", [128, 1], BF16))
        ones_row = et(nc.sbuf_tensor("ones_row", [1, 128], BF16))
        id16 = et(nc.sbuf_tensor("id16", [128, 128], BF16))
        id32 = et(nc.sbuf_tensor("id32", [128, 128], F32))
        eps_b = et(nc.sbuf_tensor("eps_b", [128, 1], F32))
        zb = et(nc.sbuf_tensor("zb", [1, 16], F32))

        tC = et(nc.semaphore("tC"))
        aC = et(nc.semaphore("aC"))
        vC = et(nc.semaphore("vC"))
        cC = et(nc.semaphore("cC"))

        class Stream:

            def __init__(self, name):
                self.sem = et(nc.semaphore(name))
                self.v = 0

            def dma(self, eng, out, in_):
                eng.dma_start(out=out, in_=in_).then_inc(self.sem, 16)
                self.v += 16
                return self.v

            def wait(self, eng, val=None):
                eng.wait_ge(self.sem, self.v if val is None else val)

        g_xt = [Stream("g_xt0"), Stream("g_xt1")]
        g_w = Stream("g_w")
        g_wot = Stream("g_wot")
        g_wf = Stream("g_wf")
        g_x1 = [Stream("g_x10"), Stream("g_x11")]
        s_const = Stream("s_const")
        s_masks = Stream("s_masks")
        s_agin = Stream("s_agin")
        s_ag = [Stream("s_ag0"), Stream("s_ag1")]
        s_xr = [Stream("s_xr0"), Stream("s_xr1")]
        s_x1st = [Stream("s_x1st0"), Stream("s_x1st1")]
        s_y = [Stream("s_y0"), Stream("s_y1")]
        s_bar = Stream("s_bar")
        s_dump = Stream("s_dump")

        n = {"tC": 0, "aC": 0, "vC": 0}

        def vsync(ins):
            ev = vinc(ins)
            v.wait_ge(vC, ev)
            return ev

        def tinc(ins):
            ins.then_inc(tC, 1)
            n["tC"] += 1
            return n["tC"]

        def ainc(ins):
            ins.then_inc(aC, 1)
            n["aC"] += 1
            return n["aC"]

        def vinc(ins):
            ins.then_inc(vC, 1)
            n["vC"] += 1
            return n["vC"]

        v.memset(ones_col[:, :], 1.0)
        v.memset(ones_row[:, :], 1.0)
        v.memset(eps_b[:, :], LN_EPS)
        zb_evt = vinc(v.memset(zb[:, :], 0.0))
        s_const.dma(sy, id16[:, :], ID16)
        s_const.dma(sy, id32[:, :], ID32)

        with ExitStack() as abx:
            eab = abx.enter_context
            qs = eab(nc.sbuf_tensor("qs", [128, HG * S], BF16, side="right"))
            ks = eab(nc.sbuf_tensor("ks", [128, HG * S], BF16, side="right"))
            vs = eab(nc.sbuf_tensor("vs", [128, HG * 32 * 128], BF16, side="right"))

            with ExitStack() as ax:
                ea = ax.enter_context
                wq = ea(nc.sbuf_tensor("wq", [128, 16 * 512], BF16, side="right"))
                wk = ea(nc.sbuf_tensor("wk", [128, 16 * 512], BF16, side="right"))
                wv = ea(nc.sbuf_tensor("wv", [128, 16 * 512], BF16, side="right"))
                xtb = [ea(nc.sbuf_tensor(f"xtb{i}", [128, 16 * 512], BF16, side="right"))
                       for i in range(2)]
                cos = ea(nc.sbuf_tensor("cos", [64, S], BF16, side="right"))
                sin = ea(nc.sbuf_tensor("sin", [64, S], BF16, side="right"))
                rts = ea(nc.sbuf_tensor("rts", [64, 64], BF16, side="right"))
                rtmp1 = ea(nc.sbuf_tensor("rtmp1", [64, 512], BF16, side="right"))
                rtmp2 = ea(nc.sbuf_tensor("rtmp2", [64, 512], BF16, side="right"))
                qk_ps = [ea(nc.psum_tensor(f"qk_ps{i}", [128, 512], F32))
                         for i in range(2)]
                v_ps = [ea(nc.psum_tensor(f"v_ps{i}", [128, 512], F32))
                        for i in range(2)]
                rz_ps = [ea(nc.psum_tensor(f"rz_ps{i}", [64, 512], F32))
                         for i in range(2)]

                s_const.dma(sy, cos[:, :], COS)
                s_const.dma(sy, sin[:, :], SIN)
                s_const.dma(sy, rts[:, :], RTD)

                def load_xt(i):
                    return g_xt[i % 2].dma(
                        g,
                        xtb[i % 2][:, :].rearrange("p (ec t) -> p ec t", ec=16),
                        XT.rearrange("(ec p) t -> p ec t", p=128)[
                            :, :, i * 512:(i + 1) * 512],
                    )

                load_xt(0)
                g_w.dma(g, wq[:, :].rearrange("p (ec f) -> p ec f", ec=16),
                        WQT.rearrange("(ec p) f -> p ec f", p=128))
                g_w.dma(g, wk[:, :].rearrange("p (ec f) -> p ec f", ec=16),
                        WKT.rearrange("(ec p) f -> p ec f", p=128))
                g_w.dma(g, wv[:, :].rearrange("p (ec f) -> p ec f", ec=16),
                        WVT.rearrange("(ec p) f -> p ec f", p=128))
                load_xt(1)

                qk_free = [0, 0]
                v_free = [0, 0]
                tile_done_t = [0] * 8

                for it in range(8):
                    t0 = it * 512
                    t.wait_ge(g_xt[it % 2].sem, 16 * (it // 2 + 1))
                    if it == 0:
                        g_w.wait(t)
                    for fc in range(8):
                        w = wq if fc < 4 else wk
                        hl = fc % 4
                        ps = qk_ps[fc % 2]
                        if qk_free[fc % 2]:
                            t.wait_ge(vC, qk_free[fc % 2])
                        for ec in range(16):
                            ins = t.matmul(
                                ps[:, :],
                                w[:, ec * 512 + hl * 128: ec * 512 + hl * 128 + 128],
                                xtb[it % 2][:, ec * 512:(ec + 1) * 512],
                                start=(ec == 0), stop=(ec == 15),
                            )
                        evt = tinc(ins)
                        dst = qs if fc < 4 else ks
                        v.wait_ge(tC, evt)
                        qk_free[fc % 2] = vinc(v.tensor_copy(
                            dst[:, hl * S + t0: hl * S + t0 + 512], ps[:, :]))
                    for sub in range(4):
                        ps = v_ps[sub % 2]
                        if v_free[sub % 2]:
                            t.wait_ge(vC, v_free[sub % 2])
                        for ec in range(16):
                            ins = t.matmul(
                                ps[:, :],
                                xtb[it % 2][:, ec * 512 + sub * 128:
                                            ec * 512 + sub * 128 + 128],
                                wv[:, ec * 512:(ec + 1) * 512],
                                start=(ec == 0), stop=(ec == 15),
                            )
                        evt = tinc(ins)
                        kt = it * 4 + sub
                        v.wait_ge(tC, evt)
                        v_free[sub % 2] = vinc(v.tensor_copy(
                            vs[:, :].rearrange("p (h k d) -> p h k d",
                                               h=HG, k=32)[:, :, kt, :],
                            ps[:, :].rearrange("p (h d) -> p h d", h=HG)))
                    tile_done_t[it] = n["tC"]
                    if it + 2 < 8:
                        g.wait_ge(tC, tile_done_t[it])
                        load_xt(it + 2)

                qkv_evac_all = n["vC"]
                t.wait_ge(vC, qkv_evac_all)
                s_const.wait(v)
                s_const.wait(t)
                rz_free = [0, 0]
                ri = 0
                for zs in (qs, ks):
                    for h in range(HG):
                        for sl in range(8):
                            s0 = sl * 512
                            par = ri % 2
                            ri += 1
                            ps = rz_ps[par]
                            if rz_free[par]:
                                t.wait_ge(vC, rz_free[par])
                            ins = t.matmul(
                                ps[:, :], rts[:, :],
                                zs[0:64, h * S + s0: h * S + s0 + 512],
                                start=True, stop=True)
                            evt = tinc(ins)
                            v.wait_ge(tC, evt)
                            v.tensor_mul(rtmp1[:, :], ps[:, :],
                                         sin[:, s0:s0 + 512])
                            vsync(v.tensor_mul(
                                rtmp2[:, :],
                                zs[0:64, h * S + s0: h * S + s0 + 512],
                                cos[:, s0:s0 + 512]))
                            rz_free[par] = vinc(v.tensor_add(
                                zs[0:64, h * S + s0: h * S + s0 + 512],
                                rtmp1[:, :], rtmp2[:, :]))
                rope_done_v = n["vC"]
                rope_done_t = n["tC"]

            if PH == "A":
                sy.wait_ge(vC, rope_done_v)
                s_dump.dma(sy, QD, qs[:, :])
                s_dump.dma(sy, KD, ks[:, :])
                s_dump.dma(sy, VD, vs[:, :])
                s_dump.wait(sy)
                return nc

            wot = et(nc.sbuf_tensor("wot", [128, 16 * E], BF16))
            g.wait_ge(tC, rope_done_t)
            g.wait_ge(vC, rope_done_v)
            g_wot.dma(g, wot[:, :].rearrange("p (ec f) -> p ec f", ec=16),
                      WOT.rearrange("(ec p) f -> p ec f", p=128))

            with ExitStack() as bx:
                ebx = bx.enter_context
                ao = ebx(nc.sbuf_tensor("ao", [128, HG * S], BF16, side="right"))
                masks = ebx(nc.sbuf_tensor("masks", [128, 4 * QG], BF16, side="right"))
                pt = [ebx(nc.sbuf_tensor(f"pt{i}", [128, QG], BF16, side="right"))
                      for i in range(4)]
                nrec = [ebx(nc.sbuf_tensor(f"nrec{i}", [1, QG], BF16, side="right"))
                        for i in range(2)]
                bc_sb = [ebx(nc.sbuf_tensor(f"bc_sb{i}", [128, QG], BF16, side="right"))
                         for i in range(2)]
                s_ps = [ebx(nc.psum_tensor(f"s_ps{i}", [128, QG], F32))
                        for i in range(4)]
                o_ps = [ebx(nc.psum_tensor(f"o_ps{i}", [128, QG], F32))
                        for i in range(2)]
                b_ps = [ebx(nc.psum_tensor(f"b_ps{i}", [128, QG], F32))
                        for i in range(2)]

                sy.wait_ge(tC, rope_done_t)
                sy.wait_ge(vC, rope_done_v)
                s_masks.dma(sy, masks[:, :], MASKS)
                s_masks.wait(v)
                t.wait_ge(vC, rope_done_v)

                sy.wait_ge(vC, zb_evt)
                s_bar.dma(sy, BARIN.ap(), zb[:, :])

                pt_evt_a = [0, 0, 0, 0]
                pt_evt_v = [0, 0, 0, 0]
                pt_masked = [False] * 4
                pt_used = [0, 0, 0, 0]
                s_exped = [0, 0, 0, 0]
                on_free = [0, 0]
                rec_evt_ob = [0, 0]
                ktc = 0
                state = {"pending": None, "bc": None}

                def emit_pv():
                    if state["pending"] is None:
                        return
                    par, ob2, hh, kt2, first, last, q0b = state["pending"]
                    state["pending"] = None
                    if pt_masked[par]:
                        t.wait_ge(vC, pt_evt_v[par])
                    else:
                        t.wait_ge(aC, pt_evt_a[par])
                    t.matmul(o_ps[ob2][:, :],
                             vs[:, (hh * 32 + kt2) * 128:
                                (hh * 32 + kt2) * 128 + 128],
                             pt[par][:, :], start=first, stop=last,
                             skip_group_check=True)
                    ins2 = t.matmul(b_ps[ob2][0:1, :], ones_col[:, :],
                                    pt[par][:, :], start=first, stop=last,
                                    skip_group_check=True)
                    pt_used[par] = tinc(ins2)
                    if last:
                        v.wait_ge(tC, pt_used[par])
                        with nc.allow_low_precision(
                                reason="softmax normalizer reciprocal kept in"
                                       " bf16; same precision class as the "
                                       "bf16 attention matmul path"):
                            rec_evt_ob[ob2] = vinc(v.reciprocal(
                                nrec[ob2][:, :], b_ps[ob2][0:1, :]))
                        state["bc"] = (ob2, hh, q0b)

                def emit_bcast():
                    if state["bc"] is None:
                        return
                    ob2, hh, q0b = state["bc"]
                    state["bc"] = None
                    t.wait_ge(vC, rec_evt_ob[ob2])
                    ins3 = t.matmul(b_ps[ob2][:, :], ones_row[:, :],
                                    nrec[ob2][:, :], start=True, stop=True,
                                    skip_group_check=True)
                    bce = tinc(ins3)
                    v.wait_ge(tC, bce)
                    vsync(v.tensor_copy(bc_sb[ob2][:, :], b_ps[ob2][:, :]))
                    on_free[ob2] = vinc(v.tensor_mul(
                        ao[:, hh * S + q0b: hh * S + q0b + QG],
                        o_ps[ob2][:, :], bc_sb[ob2][:, :]))
                    if q0b == 7 * QG:
                        sy.wait_ge(vC, on_free[ob2])
                        s_agin.dma(sy, AGINH[hh].ap(),
                                   ao[:, hh * S:(hh + 1) * S])
                        s_agin.wait(g)
                        g.collective_compute(
                            "AllGather", ALU.bypass,
                            replica_groups=[[0, 1, 2, 3], [4, 5, 6, 7]],
                            ins=[AGINH[hh].ap().opt()],
                            outs=[AGOUTH[hh].ap().opt()],
                        ).then_inc(cC)

                for gi in range(HG * 8):
                    h, qg = gi // 8, gi % 8
                    q0 = qg * QG
                    ob = gi % 2
                    if on_free[ob]:
                        t.wait_ge(vC, on_free[ob])
                    nkt = 4 * (qg + 1)
                    for kt in range(nkt):
                        par = ktc % 4
                        ktc += 1
                        if s_exped[par]:
                            t.wait_ge(aC, s_exped[par])
                        ins = t.matmul(
                            s_ps[par][:, :],
                            ks[:, h * S + kt * 128: h * S + kt * 128 + 128],
                            qs[:, h * S + q0: h * S + q0 + QG],
                            start=True, stop=True, skip_group_check=True)
                        s_evt = tinc(ins)
                        emit_pv()
                        if kt >= 1:
                            emit_bcast()
                        sc.wait_ge(tC, s_evt)
                        if pt_used[par]:
                            sc.wait_ge(tC, pt_used[par])
                        s_exped[par] = ainc(sc.activation(
                            pt[par][:, :], s_ps[par][:, :], AFT.Exp,
                            scale=INV_SQRT_D))
                        delta = kt - 4 * qg
                        if delta >= 0:
                            v.wait_ge(aC, s_exped[par])
                            pt_evt_v[par] = vinc(v.tensor_mul(
                                pt[par][:, :], pt[par][:, :],
                                masks[:, delta * QG:(delta + 1) * QG]))
                            pt_masked[par] = True
                        else:
                            pt_evt_a[par] = s_exped[par]
                            pt_masked[par] = False
                        state["pending"] = (par, ob, h, kt, kt == 0,
                                            kt == nkt - 1, q0)
                emit_pv()
                emit_bcast()
                ao_done_v = n["vC"]

                if PH == "B":
                    sy.wait_ge(vC, ao_done_v)
                    s_dump.dma(sy, AOD, ao[:, :])
                    s_dump.wait(sy)
                    return nc

                s_bar.wait(g)
                g.collective_compute(
                    "AllGather", ALU.bypass,
                    replica_groups=[[0, 1, 2, 3], [4, 5, 6, 7]],
                    ins=[BARIN.ap().opt()], outs=[BAROUT.ap().opt()],
                ).then_inc(cC)

        if PH == "X":
            sy.wait_ge(cC, 5)
            for hh in range(HG):
                s_dump.dma(sy, XOD.rearrange("(cg hh p) q -> cg hh p q",
                                             hh=4, p=128)[:, hh],
                           AGOUTH[hh].ap().rearrange("(cg p) q -> cg p q",
                                                     p=128))
            s_dump.wait(sy)
            return nc

        t0_gp = (g.partition_id() % 4) * TOK

        with ExitStack() as cx:
            ecx = cx.enter_context
            wft = ecx(nc.sbuf_tensor("wft", [128, 16 * E], BF16))
            gat = ecx(nc.sbuf_tensor("gat", [128, 16 * 136], BF16))
            btb = ecx(nc.sbuf_tensor("btb", [128, E], BF16))
            g_wf.dma(g, wft[:, :].rearrange("p (ec f) -> p ec f", ec=16),
                     WFT.rearrange("(ec p) f -> p ec f", p=128))
            g_wf.dma(g, gat[:, :].rearrange("p (ec f) -> p ec f", ec=16),
                     GAT.rearrange("(ec p) f -> p ec f", p=128))
            g_wf.dma(g, btb[:, :], BTD)

            def ln_tail(resid_ap, scr2_ap, lb, out_ap, resid_evt_v,
                        scr2_free):
                ve1 = vinc(v.reduce_sum(lb["mraw"][:, :], resid_ap, AX.X))
                sc.wait_ge(vC, resid_evt_v)
                if scr2_free[0]:
                    sc.wait_ge(vC, scr2_free[0])
                a1 = ainc(sc.activation(scr2_ap, resid_ap, AFT.Square))
                v.wait_ge(vC, ve1)
                v.tensor_scalar_mul(lb["m"][:, :], lb["mraw"][:, :], 1.0 / E)
                v.wait_ge(aC, a1)
                scr2_free[0] = vsync(v.reduce_sum(lb["s2"][:, :], scr2_ap,
                                                  AX.X))
                vsync(v.tensor_mul(lb["msq"][:, :], lb["m"][:, :],
                                   lb["m"][:, :]))
                ve4 = vinc(v.scalar_tensor_tensor(
                    lb["var"][:, :], lb["s2"][:, :], 1.0 / E,
                    lb["msq"][:, :], ALU.mult, ALU.subtract))
                sc.wait_ge(vC, ve4)
                aln = ainc(sc.activation(lb["lnv"][:, :], lb["var"][:, :],
                                         AFT.Ln, bias=eps_b[:, :]))
                sc.wait_ge(aC, aln)
                ae = ainc(sc.activation(lb["rstd"][:, :], lb["lnv"][:, :],
                                        AFT.Exp, scale=-0.5))
                v.wait_ge(aC, ae)
                vsync(v.tensor_scalar(lb["nmr"][:, :], lb["m"][:, :],
                                      lb["rstd"][:, :], -1.0, ALU.mult,
                                      ALU.mult))
                return vinc(v.tensor_scalar(out_ap, resid_ap,
                                            lb["rstd"][:, :], lb["nmr"][:, :],
                                            ALU.mult, ALU.add))

            with ExitStack() as c1x:
                e1 = c1x.enter_context
                agsb = [e1(nc.sbuf_tensor(f"agsb{i}", [128, 16 * 128], BF16))
                        for i in range(2)]
                xr = [e1(nc.sbuf_tensor(f"xr{i}", [128, E], F32))
                      for i in range(2)]
                resid = [e1(nc.sbuf_tensor(f"resid{i}", [128, E], F32))
                         for i in range(2)]
                x1o = [e1(nc.sbuf_tensor(f"x1o{i}", [128, E], F32))
                       for i in range(2)]
                scr2 = e1(nc.sbuf_tensor("scr2", [128, E], F32))
                scr2_free = [0]
                lnb = [{k: e1(nc.sbuf_tensor(f"ln_{k}{i}", [128, 1], F32))
                        for k in ("mraw", "m", "s2", "msq", "var", "lnv",
                                  "rstd", "nmr")} for i in range(2)]
                op_ps = [e1(nc.psum_tensor(f"op_ps{i}", [128, E], F32))
                         for i in range(2)]

                def load_agsb(i):
                    for hh in range(HG):
                        base = AGOUTH[hh].ap().rearrange(
                            "(cg p) q -> p cg q", p=128)[
                            :, :, i * 128:(i + 1) * 128]
                        dyn = AP(base.tensor, base.offset + t0_gp, base.ap)
                        s_ag[i % 2].dma(
                            g,
                            agsb[i % 2][:, :].rearrange(
                                "p (cg hh c) -> p cg hh c", hh=4, c=128)[
                                :, :, hh], dyn)
                    return s_ag[i % 2].v

                def load_xr(i):
                    return s_xr[i % 2].dma(sy, xr[i % 2][:, :],
                                           XR[i * 128:(i + 1) * 128, :])

                g.wait_ge(cC, 5)
                load_agsb(0)
                if PH == "AG2":
                    s_ag[0].wait(sy)
                    s_dump.dma(sy, AGD, agsb[0][:, :])
                    s_dump.wait(sy)
                    return nc
                load_xr(0)
                load_agsb(1)
                load_xr(1)

                mm_done = [0] * 8
                resid_evt = [0] * 8
                opfree = [0, 0]
                x1_evt = [0] * 8
                g_wot.wait(t)
                for i in range(8):
                    pb = i % 2
                    t.wait_ge(s_ag[i % 2].sem, 64 * (i // 2 + 1))
                    if opfree[pb]:
                        t.wait_ge(vC, opfree[pb])
                    for ec in range(16):
                        for f4 in range(4):
                            ins = t.matmul(
                                op_ps[pb][:, f4 * 512:(f4 + 1) * 512],
                                agsb[pb][:, ec * 128:(ec + 1) * 128],
                                wot[:, ec * E + f4 * 512:
                                    ec * E + f4 * 512 + 512],
                                start=(ec == 0), stop=(ec == 15),
                                skip_group_check=True,
                            )
                    mm_done[i] = tinc(ins)
                    v.wait_ge(tC, mm_done[i])
                    v.wait_ge(s_xr[i % 2].sem, 16 * (i // 2 + 1))
                    for f4 in range(4):
                        ins = v.scalar_tensor_tensor(
                            resid[pb][:, f4 * 512:(f4 + 1) * 512],
                            op_ps[pb][:, f4 * 512:(f4 + 1) * 512], 1.0,
                            xr[pb][:, f4 * 512:(f4 + 1) * 512],
                            ALU.mult, ALU.add)
                    opfree[pb] = vinc(ins)
                    resid_evt[i] = opfree[pb]
                    if PH == "OP" and i == 0:
                        sy.wait_ge(vC, resid_evt[0])
                        s_dump.dma(sy, OPD, resid[0][:, :])
                        s_dump.wait(sy)
                        return nc
                    if i >= 2:
                        v.wait_ge(s_x1st[i % 2].sem, 16 * (i // 2))
                    x1_evt[i] = ln_tail(resid[pb][:, :], scr2[:, :],
                                        lnb[pb], x1o[pb][:, :], resid_evt[i],
                                        scr2_free)
                    sy.wait_ge(vC, x1_evt[i])
                    if PH == "L1" and i == 0:
                        s_dump.dma(sy, L1D, x1o[0][:, :])
                        s_dump.wait(sy)
                        return nc
                    s_x1st[i % 2].dma(sy, X1D[i * 128:(i + 1) * 128, :],
                                      x1o[pb][:, :])
                    if i + 2 < 8:
                        g.wait_ge(tC, mm_done[i])
                        load_agsb(i + 2)
                        sy.wait_ge(vC, resid_evt[i])
                        load_xr(i + 2)
            if PH == "C1":
                s_x1st[0].wait(sy)
                s_x1st[1].wait(sy)
                s_dump.dma(sy, X1O, X1D.ap())
                s_dump.wait(sy)
                return nc

            with ExitStack() as c2x:
                e2 = c2x.enter_context
                x1i = [e2(nc.sbuf_tensor(f"x1i{i}", [128, E], F32))
                       for i in range(2)]
                x1t = [e2(nc.sbuf_tensor(f"x1t{i}", [128, 16 * 128], BF16))
                       for i in range(2)]
                ysb = [e2(nc.sbuf_tensor(f"ysb{i}", [128, E], F32))
                       for i in range(2)]
                resid2 = [e2(nc.sbuf_tensor(f"resid2{i}", [128, E], F32))
                          for i in range(2)]
                qe = e2(nc.sbuf_tensor("qe", [128, 8], F32))
                qlt = e2(nc.sbuf_tensor("qlt", [128, 8], F32))
                cw = e2(nc.sbuf_tensor("cw", [128, 8], F32))
                g1 = e2(nc.sbuf_tensor("g1", [128, 1], F32))
                g2 = e2(nc.sbuf_tensor("g2", [128, 1], F32))
                gneg = e2(nc.sbuf_tensor("gneg", [128, 1], F32))
                gss = e2(nc.sbuf_tensor("gss", [128, 1], F32))
                t_bf = e2(nc.sbuf_tensor("t_bf", [128, 128], BF16))
                tt_sb = e2(nc.sbuf_tensor("tt_sb", [128, 128], BF16))
                scr2b = e2(nc.sbuf_tensor("scr2b", [128, E], F32))
                scr2b_free = [0]
                lnb2 = [{k: e2(nc.sbuf_tensor(f"ln2_{k}{i}", [128, 1], F32))
                         for k in ("mraw", "m", "s2", "msq", "var", "lnv",
                                   "rstd", "nmr")} for i in range(2)]
                mn_ps = e2(nc.psum_tensor("mn_ps", [128, E], F32))
                ga_ps = e2(nc.psum_tensor("ga_ps", [128, 136], F32))
                tt_ps = e2(nc.psum_tensor("tt_ps", [128, 128], BF16))
                x1t_ps = [e2(nc.psum_tensor(f"x1t_ps{i}", [128, 512], F32))
                          for i in range(2)]

                def load_x1(i):
                    return g_x1[i % 2].dma(g, x1i[i % 2][:, :],
                                           X1D[i * 128:(i + 1) * 128, :])

                s_x1st[0].wait(g)
                s_x1st[1].wait(g)
                g.wait_ge(vC, x1_evt[7])
                g.wait_ge(tC, mm_done[7])
                load_x1(0)
                load_x1(1)

                g_wf.wait(t)
                s_const.wait(t)

                x1t_evt = [0] * 8
                ga_done = [0] * 8
                x1t_psfree = [0, 0]
                mnfree = [0] * 8
                gafree = 0
                ttfree = 0
                y_evt = [0] * 8

                def emit_x1t(i):
                    pb2 = i % 2
                    t.wait_ge(g_x1[i % 2].sem, 16 * (i // 2 + 1))
                    for q4 in range(4):
                        qp = q4 % 2
                        if x1t_psfree[qp]:
                            t.wait_ge(vC, x1t_psfree[qp])
                        for j in range(4):
                            ins = t.transpose(
                                x1t_ps[qp][:, j * 128:(j + 1) * 128],
                                x1i[pb2][:, (q4 * 4 + j) * 128:
                                         (q4 * 4 + j) * 128 + 128],
                                id32[:, :])
                        evt = tinc(ins)
                        v.wait_ge(tC, evt)
                        if i >= 2:
                            v.wait_ge(tC, ga_done[i - 2])
                        x1t_psfree[qp] = vinc(v.tensor_copy(
                            x1t[i % 2][:, q4 * 512:(q4 + 1) * 512],
                            x1t_ps[qp][:, :]))
                    x1t_evt[i] = n["vC"]

                emit_x1t(0)
                for i in range(8):
                    pb = i % 2
                    t.wait_ge(vC, x1t_evt[i])
                    if mnfree[i - 1] and i >= 1:
                        t.wait_ge(vC, mnfree[i - 1])
                    if gafree:
                        t.wait_ge(vC, gafree)
                    for ec in range(16):
                        t.matmul(ga_ps[:, :],
                                 x1t[pb][:, ec * 128:(ec + 1) * 128],
                                 gat[:, ec * 136:(ec + 1) * 136],
                                 start=(ec == 0), stop=(ec == 15),
                                 skip_group_check=True)
                        for f4 in range(4):
                            ins = t.matmul(
                                mn_ps[:, f4 * 512:(f4 + 1) * 512],
                                x1t[pb][:, ec * 128:(ec + 1) * 128],
                                wft[:, ec * E + f4 * 512:
                                     ec * E + f4 * 512 + 512],
                                start=(ec == 0), stop=False,
                                skip_group_check=True)
                    ga_done[i] = tinc(ins)
                    if i + 1 < 8:
                        emit_x1t(i + 1)

                    v.wait_ge(tC, ga_done[i])
                    ve = vinc(v.tensor_reduce(gneg[:, :], ga_ps[:, 0:8],
                                              AX.X, ALU.max, negate=True))
                    sc.wait_ge(vC, ve)
                    ae = ainc(sc.activation(qe[:, :], ga_ps[:, 0:8], AFT.Exp,
                                            bias=gneg[:, :]))
                    v.wait_ge(aC, ae)
                    vsync(v.reduce_max(g1[:, :], qe[:, :], AX.X))
                    vsync(v.tensor_single_scalar(qlt[:, :], qe[:, :],
                                                 g1[:, :], ALU.is_lt))
                    vsync(v.tensor_mul(qlt[:, :], qe[:, :], qlt[:, :]))
                    vsync(v.reduce_max(g2[:, :], qlt[:, :], AX.X))
                    vsync(v.tensor_add(gss[:, :], g1[:, :], g2[:, :]))
                    vsync(v.reciprocal(gss[:, :], gss[:, :]))
                    vsync(v.tensor_scalar_mul(gss[:, :], gss[:, :], 16.0))
                    vsync(v.tensor_single_scalar(qlt[:, :], qe[:, :],
                                                 g2[:, :], ALU.is_ge))
                    vsync(v.scalar_tensor_tensor(cw[:, :], qe[:, :],
                                                 gss[:, :], qlt[:, :],
                                                 ALU.mult, ALU.mult))
                    for ex in range(8):
                        ins = v.tensor_scalar(
                            t_bf[:, ex * 16:(ex + 1) * 16],
                            ga_ps[:, 8 + ex * 16: 8 + ex * 16 + 16],
                            cw[:, ex:ex + 1], None, ALU.mult)
                    gafree = vinc(ins)

                    t.wait_ge(vC, gafree)
                    if ttfree:
                        t.wait_ge(vC, ttfree)
                    tte = tinc(t.transpose(tt_ps[:, :], t_bf[:, :],
                                           id16[:, :]))
                    v.wait_ge(tC, tte)
                    ttv = vinc(v.tensor_copy(tt_sb[:, :], tt_ps[:, :]))
                    ttfree = ttv
                    t.wait_ge(vC, ttv)
                    for f4 in range(4):
                        ins = t.matmul(mn_ps[:, f4 * 512:(f4 + 1) * 512],
                                       tt_sb[:, :],
                                       btb[:, f4 * 512:(f4 + 1) * 512],
                                       start=False, stop=True,
                                       skip_group_check=True)
                    ffn_done = tinc(ins)

                    v.wait_ge(tC, ffn_done)
                    for f4 in range(4):
                        ins = v.scalar_tensor_tensor(
                            resid2[pb][:, f4 * 512:(f4 + 1) * 512],
                            mn_ps[:, f4 * 512:(f4 + 1) * 512], 1.0,
                            x1i[pb][:, f4 * 512:(f4 + 1) * 512],
                            ALU.mult, ALU.add)
                    mnfree[i] = vinc(ins)
                    if i >= 2:
                        v.wait_ge(s_y[i % 2].sem, 16 * (i // 2))
                    y_evt[i] = ln_tail(resid2[pb][:, :], scr2b[:, :],
                                       lnb2[pb], ysb[pb][:, :], mnfree[i],
                                       scr2b_free)
                    sy.wait_ge(vC, y_evt[i])
                    s_y[i % 2].dma(sy, Y[i * 128:(i + 1) * 128, :],
                                   ysb[pb][:, :])
                    if i + 2 < 8:
                        g.wait_ge(vC, mnfree[i])
                        load_x1(i + 2)

                s_y[0].wait(sy)
                s_y[1].wait(sy)

    return nc


_NC = None


def _get_nc():
    global _NC
    if _NC is None:
        _NC = _build_nc()
    return _NC


def _make_consts():
    ROT = 64
    inv_freq = (1.0 / 10000.0 ** (np.arange(0, ROT, 2) / ROT)).astype(np.float32)
    pos = np.arange(S, dtype=np.float32)
    freqs = pos[None, :] * inv_freq[:, None]
    cos = np.repeat(np.cos(freqs), 2, axis=0).astype(ml_dtypes.bfloat16)
    sin = np.repeat(np.sin(freqs), 2, axis=0).astype(ml_dtypes.bfloat16)
    R = np.zeros((64, 64), np.float32)
    for i in range(32):
        R[2 * i, 2 * i + 1] = -1.0
        R[2 * i + 1, 2 * i] = 1.0
    RT = np.ascontiguousarray(R.T).astype(ml_dtypes.bfloat16)
    id16 = np.eye(128, dtype=np.float32).astype(ml_dtypes.bfloat16)
    id32 = np.eye(128, dtype=np.float32)
    k = np.arange(128)[:, None]
    q = np.arange(QG)[None, :]
    masks = np.concatenate(
        [(k + 128 * d <= q).astype(np.float32) for d in range(4)],
        axis=1).astype(ml_dtypes.bfloat16)
    return cos, sin, RT, id16, id32, masks


def _make_in_maps(L0_attn_hidden_states, L0_attn_w_qkv, L0_attn_w_out,
                  L0_ffn_weight, L0_ffn_gate_w, L0_ffn_A_w, L0_ffn_B_w):
    X = np.ascontiguousarray(np.asarray(L0_attn_hidden_states, np.float32))
    Wqkv = np.asarray(L0_attn_w_qkv, np.float32)
    Wout = np.asarray(L0_attn_w_out, np.float32)
    Wffn = np.asarray(L0_ffn_weight, np.float32)
    Gw = np.asarray(L0_ffn_gate_w, np.float32)
    Aw = np.asarray(L0_ffn_A_w, np.float32)
    Bw = np.asarray(L0_ffn_B_w, np.float32)

    cos, sin, RT, id16, id32, masks = _make_consts()
    WOTn = np.ascontiguousarray(Wout.T)
    WFTn = np.ascontiguousarray(Wffn.T)
    GATn = np.ascontiguousarray(np.concatenate(
        [Gw.T, Aw.reshape(128, E).T], axis=1))
    BTDn = np.ascontiguousarray(Bw.transpose(0, 2, 1).reshape(128, E))

    in_maps = []
    for c in range(8):
        b, gq = c // 4, c % 4
        in_maps.append({
            "XT": np.ascontiguousarray(X[b].T),
            "XR": np.ascontiguousarray(X[b, gq * TOK:(gq + 1) * TOK]),
            "WQT": np.ascontiguousarray(Wqkv[gq * 1536: gq * 1536 + 512].T),
            "WVT": np.ascontiguousarray(
                Wqkv[gq * 1536 + 512: gq * 1536 + 1024].T),
            "WKT": np.ascontiguousarray(
                Wqkv[gq * 1536 + 1024: gq * 1536 + 1536].T),
            "WOT": WOTn, "WFT": WFTn, "GAT": GATn, "BTD": BTDn,
            "COS": cos, "SIN": sin, "RTD": RT, "ID16": id16, "ID32": id32,
            "MASKS": masks,
        })

    return in_maps


def kernel(**inputs):
    in_maps = _make_in_maps(**inputs)
    nc = _get_nc()
    res = run_bass_kernel_spmd(nc, in_maps, core_ids=list(range(8)))
    out = np.empty((B, S, E), np.float32)
    for c in range(8):
        b, gq = c // 4, c % 4
        out[b, gq * TOK:(gq + 1) * TOK] = res.results[c]["Y"]
    return out


# version 07
# speedup vs baseline: 1.7831x; 1.0769x over previous
import numpy as np
import ml_dtypes
from contextlib import ExitStack

import concourse.bass as bass
import concourse.mybir as mybir
from concourse.ap import AP
from concourse.bass_utils import run_bass_kernel_spmd

F32 = mybir.dt.float32
BF16 = mybir.dt.bfloat16
AFT = mybir.ActivationFunctionType
ALU = mybir.AluOpType
AX = mybir.AxisListType

_PHASES = "full"
_DETECT_RACES = False

B, S, E = 2, 4096, 2048
HG = 4
TOK = 1024
QG = 512
INV_SQRT_D = float(1.0 / np.sqrt(128.0))
LN_EPS = 1e-5


def _build_nc():
    nc = bass.Bass("TRN2", target_bir_lowering=False, debug=False,
                   detect_race_conditions=_DETECT_RACES)

    XT = nc.dram_tensor("XT", [E, S], F32, kind="ExternalInput").ap()
    XR = nc.dram_tensor("XR", [TOK, E], F32, kind="ExternalInput").ap()
    WQT = nc.dram_tensor("WQT", [E, 512], F32, kind="ExternalInput").ap()
    WKT = nc.dram_tensor("WKT", [E, 512], F32, kind="ExternalInput").ap()
    WVT = nc.dram_tensor("WVT", [E, 512], F32, kind="ExternalInput").ap()
    WOT = nc.dram_tensor("WOT", [E, E], F32, kind="ExternalInput").ap()
    WFT = nc.dram_tensor("WFT", [E, E], F32, kind="ExternalInput").ap()
    GAT = nc.dram_tensor("GAT", [E, 136], F32, kind="ExternalInput").ap()
    BTD = nc.dram_tensor("BTD", [128, E], F32, kind="ExternalInput").ap()
    COS = nc.dram_tensor("COS", [64, S], BF16, kind="ExternalInput").ap()
    SIN = nc.dram_tensor("SIN", [64, S], BF16, kind="ExternalInput").ap()
    RTD = nc.dram_tensor("RTD", [64, 64], BF16, kind="ExternalInput").ap()
    ID16 = nc.dram_tensor("ID16", [128, 128], BF16, kind="ExternalInput").ap()
    ID32 = nc.dram_tensor("ID32", [128, 128], F32, kind="ExternalInput").ap()
    MASKS = nc.dram_tensor("MASKS", [128, 4 * QG], BF16, kind="ExternalInput").ap()
    Y = nc.dram_tensor("Y", [TOK, E], F32, kind="ExternalOutput").ap()
    PH = _PHASES
    if PH == "A":
        QD = nc.dram_tensor("QD", [128, HG * S], BF16, kind="ExternalOutput").ap()
        KD = nc.dram_tensor("KD", [128, HG * S], BF16, kind="ExternalOutput").ap()
        VD = nc.dram_tensor("VD", [128, HG * 32 * 128], BF16, kind="ExternalOutput").ap()
    if PH == "B":
        AOD = nc.dram_tensor("AOD", [128, HG * S], BF16, kind="ExternalOutput").ap()
    if PH == "X":
        XOD = nc.dram_tensor("XOD", [E, S], BF16, kind="ExternalOutput").ap()
    if PH == "C1":
        X1O = nc.dram_tensor("X1O", [TOK, E], F32, kind="ExternalOutput").ap()
    if PH == "AG2":
        AGD = nc.dram_tensor("AGD", [128, 16 * 128], BF16, kind="ExternalOutput").ap()
    if PH == "OP":
        OPD = nc.dram_tensor("OPD", [128, E], F32, kind="ExternalOutput").ap()
    if PH == "L1":
        L1D = nc.dram_tensor("L1D", [128, E], F32, kind="ExternalOutput").ap()

    AGINH = [nc.dram_tensor(f"AGINH{h}", [128, S], BF16) for h in range(HG)]
    AGOUTH = [nc.dram_tensor(f"AGOUTH{h}", [4 * 128, S], BF16)
              for h in range(HG)]
    BARIN = nc.dram_tensor("BARIN", [1, 16], F32)
    BAROUT = nc.dram_tensor("BAROUT", [4, 16], F32)
    X1D = nc.dram_tensor("X1D", [TOK, E], F32)

    g = nc.gpsimd
    sy = nc.sync
    t = nc.tensor
    v = nc.vector
    sc = nc.scalar

    with ExitStack() as top:
        et = top.enter_context
        ones_col = et(nc.sbuf_tensor("ones_col", [128, 1], BF16))
        ones128 = et(nc.sbuf_tensor("ones128", [128, 128], BF16))
        ones_row = et(nc.sbuf_tensor("ones_row", [1, 128], BF16))
        id16 = et(nc.sbuf_tensor("id16", [128, 128], BF16))
        id32 = et(nc.sbuf_tensor("id32", [128, 128], F32))
        eps_b = et(nc.sbuf_tensor("eps_b", [128, 1], F32))
        zb = et(nc.sbuf_tensor("zb", [1, 16], F32))

        tC = et(nc.semaphore("tC"))
        aC = et(nc.semaphore("aC"))
        vC = et(nc.semaphore("vC"))
        cC = et(nc.semaphore("cC"))

        class Stream:

            def __init__(self, name):
                self.sem = et(nc.semaphore(name))
                self.v = 0

            def dma(self, eng, out, in_):
                eng.dma_start(out=out, in_=in_).then_inc(self.sem, 16)
                self.v += 16
                return self.v

            def wait(self, eng, val=None):
                eng.wait_ge(self.sem, self.v if val is None else val)

        g_xt = [Stream("g_xt0"), Stream("g_xt1")]
        g_w = Stream("g_w")
        g_wot = Stream("g_wot")
        g_wf = Stream("g_wf")
        g_x1 = [Stream("g_x10"), Stream("g_x11")]
        s_const = Stream("s_const")
        s_masks = Stream("s_masks")
        s_agin = Stream("s_agin")
        s_ag = [Stream("s_ag0"), Stream("s_ag1")]
        s_xr = [Stream("s_xr0"), Stream("s_xr1")]
        s_x1st = [Stream("s_x1st0"), Stream("s_x1st1")]
        s_y = [Stream("s_y0"), Stream("s_y1")]
        s_bar = Stream("s_bar")
        s_dump = Stream("s_dump")

        n = {"tC": 0, "aC": 0, "vC": 0}

        def vsync(ins):
            ev = vinc(ins)
            v.wait_ge(vC, ev)
            return ev

        def tinc(ins):
            ins.then_inc(tC, 1)
            n["tC"] += 1
            return n["tC"]

        def ainc(ins):
            ins.then_inc(aC, 1)
            n["aC"] += 1
            return n["aC"]

        def vinc(ins):
            ins.then_inc(vC, 1)
            n["vC"] += 1
            return n["vC"]

        v.memset(ones_col[:, :], 1.0)
        v.memset(ones128[:, :], 1.0)
        v.memset(ones_row[:, :], 1.0)
        v.memset(eps_b[:, :], LN_EPS)
        zb_evt = vinc(v.memset(zb[:, :], 0.0))
        s_const.dma(sy, id16[:, :], ID16)
        s_const.dma(sy, id32[:, :], ID32)

        with ExitStack() as abx:
            eab = abx.enter_context
            qs = eab(nc.sbuf_tensor("qs", [128, HG * S], BF16, side="right"))
            ks = eab(nc.sbuf_tensor("ks", [128, HG * S], BF16, side="right"))
            vs = eab(nc.sbuf_tensor("vs", [128, HG * 32 * 128], BF16, side="right"))

            with ExitStack() as ax:
                ea = ax.enter_context
                wq = ea(nc.sbuf_tensor("wq", [128, 16 * 512], BF16, side="right"))
                wk = ea(nc.sbuf_tensor("wk", [128, 16 * 512], BF16, side="right"))
                wv = ea(nc.sbuf_tensor("wv", [128, 16 * 512], BF16, side="right"))
                xtb = [ea(nc.sbuf_tensor(f"xtb{i}", [128, 16 * 512], BF16, side="right"))
                       for i in range(2)]
                cos = ea(nc.sbuf_tensor("cos", [64, S], BF16, side="right"))
                sin = ea(nc.sbuf_tensor("sin", [64, S], BF16, side="right"))
                rts = ea(nc.sbuf_tensor("rts", [64, 64], BF16, side="right"))
                rtmp1 = ea(nc.sbuf_tensor("rtmp1", [64, 512], BF16, side="right"))
                rtmp2 = ea(nc.sbuf_tensor("rtmp2", [64, 512], BF16, side="right"))
                qk_ps = [ea(nc.psum_tensor(f"qk_ps{i}", [128, 512], F32))
                         for i in range(2)]
                v_ps = [ea(nc.psum_tensor(f"v_ps{i}", [128, 512], F32))
                        for i in range(2)]
                rz_ps = [ea(nc.psum_tensor(f"rz_ps{i}", [64, 512], F32))
                         for i in range(2)]

                s_const.dma(sy, cos[:, :], COS)
                s_const.dma(sy, sin[:, :], SIN)
                s_const.dma(sy, rts[:, :], RTD)

                def load_xt(i):
                    return g_xt[i % 2].dma(
                        g,
                        xtb[i % 2][:, :].rearrange("p (ec t) -> p ec t", ec=16),
                        XT.rearrange("(ec p) t -> p ec t", p=128)[
                            :, :, i * 512:(i + 1) * 512],
                    )

                load_xt(0)
                g_w.dma(g, wq[:, :].rearrange("p (ec f) -> p ec f", ec=16),
                        WQT.rearrange("(ec p) f -> p ec f", p=128))
                g_w.dma(g, wk[:, :].rearrange("p (ec f) -> p ec f", ec=16),
                        WKT.rearrange("(ec p) f -> p ec f", p=128))
                g_w.dma(g, wv[:, :].rearrange("p (ec f) -> p ec f", ec=16),
                        WVT.rearrange("(ec p) f -> p ec f", p=128))
                load_xt(1)

                qk_free = [0, 0]
                v_free = [0, 0]
                tile_done_t = [0] * 8

                for it in range(8):
                    t0 = it * 512
                    t.wait_ge(g_xt[it % 2].sem, 16 * (it // 2 + 1))
                    if it == 0:
                        g_w.wait(t)
                    for fc in range(8):
                        w = wq if fc < 4 else wk
                        hl = fc % 4
                        ps = qk_ps[fc % 2]
                        if qk_free[fc % 2]:
                            t.wait_ge(vC, qk_free[fc % 2])
                        for ec in range(16):
                            ins = t.matmul(
                                ps[:, :],
                                w[:, ec * 512 + hl * 128: ec * 512 + hl * 128 + 128],
                                xtb[it % 2][:, ec * 512:(ec + 1) * 512],
                                start=(ec == 0), stop=(ec == 15),
                            )
                        evt = tinc(ins)
                        dst = qs if fc < 4 else ks
                        v.wait_ge(tC, evt)
                        qk_free[fc % 2] = vinc(v.tensor_copy(
                            dst[:, hl * S + t0: hl * S + t0 + 512], ps[:, :]))
                    for sub in range(4):
                        ps = v_ps[sub % 2]
                        if v_free[sub % 2]:
                            t.wait_ge(vC, v_free[sub % 2])
                        for ec in range(16):
                            ins = t.matmul(
                                ps[:, :],
                                xtb[it % 2][:, ec * 512 + sub * 128:
                                            ec * 512 + sub * 128 + 128],
                                wv[:, ec * 512:(ec + 1) * 512],
                                start=(ec == 0), stop=(ec == 15),
                            )
                        evt = tinc(ins)
                        kt = it * 4 + sub
                        v.wait_ge(tC, evt)
                        v_free[sub % 2] = vinc(v.tensor_copy(
                            vs[:, :].rearrange("p (h k d) -> p h k d",
                                               h=HG, k=32)[:, :, kt, :],
                            ps[:, :].rearrange("p (h d) -> p h d", h=HG)))
                    tile_done_t[it] = n["tC"]
                    if it + 2 < 8:
                        g.wait_ge(tC, tile_done_t[it])
                        load_xt(it + 2)

                qkv_evac_all = n["vC"]
                t.wait_ge(vC, qkv_evac_all)
                s_const.wait(v)
                s_const.wait(t)
                rz_free = [0, 0]
                ri = 0
                for zs in (qs, ks):
                    for h in range(HG):
                        for sl in range(8):
                            s0 = sl * 512
                            par = ri % 2
                            ri += 1
                            ps = rz_ps[par]
                            if rz_free[par]:
                                t.wait_ge(vC, rz_free[par])
                            ins = t.matmul(
                                ps[:, :], rts[:, :],
                                zs[0:64, h * S + s0: h * S + s0 + 512],
                                start=True, stop=True)
                            evt = tinc(ins)
                            v.wait_ge(tC, evt)
                            v.tensor_mul(rtmp1[:, :], ps[:, :],
                                         sin[:, s0:s0 + 512])
                            vsync(v.tensor_mul(
                                rtmp2[:, :],
                                zs[0:64, h * S + s0: h * S + s0 + 512],
                                cos[:, s0:s0 + 512]))
                            rz_free[par] = vinc(v.tensor_add(
                                zs[0:64, h * S + s0: h * S + s0 + 512],
                                rtmp1[:, :], rtmp2[:, :]))
                rope_done_v = n["vC"]
                rope_done_t = n["tC"]

            if PH == "A":
                sy.wait_ge(vC, rope_done_v)
                s_dump.dma(sy, QD, qs[:, :])
                s_dump.dma(sy, KD, ks[:, :])
                s_dump.dma(sy, VD, vs[:, :])
                s_dump.wait(sy)
                return nc

            wot = et(nc.sbuf_tensor("wot", [128, 16 * E], BF16))
            g.wait_ge(tC, rope_done_t)
            g.wait_ge(vC, rope_done_v)
            g_wot.dma(g, wot[:, :].rearrange("p (ec f) -> p ec f", ec=16),
                      WOT.rearrange("(ec p) f -> p ec f", p=128))

            with ExitStack() as bx:
                ebx = bx.enter_context
                ao = ebx(nc.sbuf_tensor("ao", [128, HG * S], BF16, side="right"))
                masks = ebx(nc.sbuf_tensor("masks", [128, 4 * QG], BF16, side="right"))
                pt = [ebx(nc.sbuf_tensor(f"pt{i}", [128, QG], BF16, side="right"))
                      for i in range(4)]
                nrec = [ebx(nc.sbuf_tensor(f"nrec{i}", [1, QG], BF16, side="right"))
                        for i in range(2)]
                bc_sb = [ebx(nc.sbuf_tensor(f"bc_sb{i}", [128, QG], BF16, side="right"))
                         for i in range(2)]
                s_ps = [ebx(nc.psum_tensor(f"s_ps{i}", [128, QG], F32))
                        for i in range(4)]
                o_ps = [ebx(nc.psum_tensor(f"o_ps{i}", [128, QG], F32))
                        for i in range(2)]
                b_ps = [ebx(nc.psum_tensor(f"b_ps{i}", [128, QG], F32))
                        for i in range(2)]

                sy.wait_ge(tC, rope_done_t)
                sy.wait_ge(vC, rope_done_v)
                s_masks.dma(sy, masks[:, :], MASKS)
                s_masks.wait(v)
                t.wait_ge(vC, rope_done_v)

                sy.wait_ge(vC, zb_evt)
                s_bar.dma(sy, BARIN.ap(), zb[:, :])

                pt_evt_a = [0, 0, 0, 0]
                pt_evt_v = [0, 0, 0, 0]
                pt_masked = [False] * 4
                pt_used = [0, 0, 0, 0]
                s_exped = [0, 0, 0, 0]
                on_free = [0, 0]
                rec_evt_ob = [0, 0]
                ktc = 0
                state = {"pending": None, "bc": None}

                def emit_pv():
                    if state["pending"] is None:
                        return
                    par, ob2, hh, kt2, first, last, q0b = state["pending"]
                    state["pending"] = None
                    if pt_masked[par]:
                        t.wait_ge(vC, pt_evt_v[par])
                    else:
                        t.wait_ge(aC, pt_evt_a[par])
                    t.matmul(o_ps[ob2][:, :],
                             vs[:, (hh * 32 + kt2) * 128:
                                (hh * 32 + kt2) * 128 + 128],
                             pt[par][:, :], start=first, stop=last,
                             skip_group_check=True)
                    ins2 = t.matmul(b_ps[ob2][:, :], ones128[:, :],
                                    pt[par][:, :], start=first, stop=last,
                                    skip_group_check=True)
                    pt_used[par] = tinc(ins2)
                    if last:
                        v.wait_ge(tC, pt_used[par])
                        with nc.allow_low_precision(
                                reason="softmax normalizer reciprocal kept in"
                                       " bf16; same precision class as the "
                                       "bf16 attention matmul path"):
                            rec_evt_ob[ob2] = vsync(v.reciprocal(
                                bc_sb[ob2][:, :], b_ps[ob2][:, :]))
                        state["bc"] = (ob2, hh, q0b)

                def emit_bcast():
                    if state["bc"] is None:
                        return
                    ob2, hh, q0b = state["bc"]
                    state["bc"] = None
                    on_free[ob2] = vinc(v.tensor_mul(
                        ao[:, hh * S + q0b: hh * S + q0b + QG],
                        o_ps[ob2][:, :], bc_sb[ob2][:, :]))
                    if q0b == 7 * QG:
                        sy.wait_ge(vC, on_free[ob2])
                        s_agin.dma(sy, AGINH[hh].ap(),
                                   ao[:, hh * S:(hh + 1) * S])
                        s_agin.wait(g)
                        g.collective_compute(
                            "AllGather", ALU.bypass,
                            replica_groups=[[0, 1, 2, 3], [4, 5, 6, 7]],
                            ins=[AGINH[hh].ap().opt()],
                            outs=[AGOUTH[hh].ap().opt()],
                        ).then_inc(cC)

                for gi in range(HG * 8):
                    h, qg = gi // 8, gi % 8
                    q0 = qg * QG
                    ob = gi % 2
                    if on_free[ob]:
                        t.wait_ge(vC, on_free[ob])
                    nkt = 4 * (qg + 1)
                    for kt in range(nkt):
                        par = ktc % 4
                        ktc += 1
                        if s_exped[par]:
                            t.wait_ge(aC, s_exped[par])
                        ins = t.matmul(
                            s_ps[par][:, :],
                            ks[:, h * S + kt * 128: h * S + kt * 128 + 128],
                            qs[:, h * S + q0: h * S + q0 + QG],
                            start=True, stop=True, skip_group_check=True)
                        s_evt = tinc(ins)
                        emit_pv()
                        if kt >= 1:
                            emit_bcast()
                        sc.wait_ge(tC, s_evt)
                        if pt_used[par]:
                            sc.wait_ge(tC, pt_used[par])
                        s_exped[par] = ainc(sc.activation(
                            pt[par][:, :], s_ps[par][:, :], AFT.Exp,
                            scale=INV_SQRT_D))
                        delta = kt - 4 * qg
                        if delta >= 0:
                            v.wait_ge(aC, s_exped[par])
                            pt_evt_v[par] = vinc(v.tensor_mul(
                                pt[par][:, :], pt[par][:, :],
                                masks[:, delta * QG:(delta + 1) * QG]))
                            pt_masked[par] = True
                        else:
                            pt_evt_a[par] = s_exped[par]
                            pt_masked[par] = False
                        state["pending"] = (par, ob, h, kt, kt == 0,
                                            kt == nkt - 1, q0)
                emit_pv()
                emit_bcast()
                ao_done_v = n["vC"]

                if PH == "B":
                    sy.wait_ge(vC, ao_done_v)
                    s_dump.dma(sy, AOD, ao[:, :])
                    s_dump.wait(sy)
                    return nc

                s_bar.wait(g)
                g.collective_compute(
                    "AllGather", ALU.bypass,
                    replica_groups=[[0, 1, 2, 3], [4, 5, 6, 7]],
                    ins=[BARIN.ap().opt()], outs=[BAROUT.ap().opt()],
                ).then_inc(cC)

        if PH == "X":
            sy.wait_ge(cC, 5)
            for hh in range(HG):
                s_dump.dma(sy, XOD.rearrange("(cg hh p) q -> cg hh p q",
                                             hh=4, p=128)[:, hh],
                           AGOUTH[hh].ap().rearrange("(cg p) q -> cg p q",
                                                     p=128))
            s_dump.wait(sy)
            return nc

        t0_gp = (g.partition_id() % 4) * TOK

        with ExitStack() as cx:
            ecx = cx.enter_context
            wft = ecx(nc.sbuf_tensor("wft", [128, 16 * E], BF16))
            gat = ecx(nc.sbuf_tensor("gat", [128, 16 * 136], BF16))
            btb = ecx(nc.sbuf_tensor("btb", [128, E], BF16))
            g_wf.dma(g, wft[:, :].rearrange("p (ec f) -> p ec f", ec=16),
                     WFT.rearrange("(ec p) f -> p ec f", p=128))
            g_wf.dma(g, gat[:, :].rearrange("p (ec f) -> p ec f", ec=16),
                     GAT.rearrange("(ec p) f -> p ec f", p=128))
            g_wf.dma(g, btb[:, :], BTD)

            def ln_tail(resid_ap, scr2_ap, lb, out_ap, resid_evt_v,
                        scr2_free):
                ve1 = vinc(v.reduce_sum(lb["mraw"][:, :], resid_ap, AX.X))
                sc.wait_ge(vC, resid_evt_v)
                if scr2_free[0]:
                    sc.wait_ge(vC, scr2_free[0])
                a1 = ainc(sc.activation(scr2_ap, resid_ap, AFT.Square))
                v.wait_ge(vC, ve1)
                v.tensor_scalar_mul(lb["m"][:, :], lb["mraw"][:, :], 1.0 / E)
                v.wait_ge(aC, a1)
                scr2_free[0] = vsync(v.reduce_sum(lb["s2"][:, :], scr2_ap,
                                                  AX.X))
                vsync(v.tensor_mul(lb["msq"][:, :], lb["m"][:, :],
                                   lb["m"][:, :]))
                ve4 = vinc(v.scalar_tensor_tensor(
                    lb["var"][:, :], lb["s2"][:, :], 1.0 / E,
                    lb["msq"][:, :], ALU.mult, ALU.subtract))
                sc.wait_ge(vC, ve4)
                aln = ainc(sc.activation(lb["lnv"][:, :], lb["var"][:, :],
                                         AFT.Ln, bias=eps_b[:, :]))
                sc.wait_ge(aC, aln)
                ae = ainc(sc.activation(lb["rstd"][:, :], lb["lnv"][:, :],
                                        AFT.Exp, scale=-0.5))
                v.wait_ge(aC, ae)
                vsync(v.tensor_scalar(lb["nmr"][:, :], lb["m"][:, :],
                                      lb["rstd"][:, :], -1.0, ALU.mult,
                                      ALU.mult))
                return vinc(v.tensor_scalar(out_ap, resid_ap,
                                            lb["rstd"][:, :], lb["nmr"][:, :],
                                            ALU.mult, ALU.add))

            with ExitStack() as c1x:
                e1 = c1x.enter_context
                agsb = [e1(nc.sbuf_tensor(f"agsb{i}", [128, 16 * 128], BF16))
                        for i in range(2)]
                xr = [e1(nc.sbuf_tensor(f"xr{i}", [128, E], F32))
                      for i in range(2)]
                resid = [e1(nc.sbuf_tensor(f"resid{i}", [128, E], F32))
                         for i in range(2)]
                x1o = [e1(nc.sbuf_tensor(f"x1o{i}", [128, E], F32))
                       for i in range(2)]
                scr2 = e1(nc.sbuf_tensor("scr2", [128, E], F32))
                scr2_free = [0]
                lnb = [{k: e1(nc.sbuf_tensor(f"ln_{k}{i}", [128, 1], F32))
                        for k in ("mraw", "m", "s2", "msq", "var", "lnv",
                                  "rstd", "nmr")} for i in range(2)]
                op_ps = [e1(nc.psum_tensor(f"op_ps{i}", [128, E], F32))
                         for i in range(2)]

                def load_agsb(i):
                    for hh in range(HG):
                        base = AGOUTH[hh].ap().rearrange(
                            "(cg p) q -> p cg q", p=128)[
                            :, :, i * 128:(i + 1) * 128]
                        dyn = AP(base.tensor, base.offset + t0_gp, base.ap)
                        s_ag[i % 2].dma(
                            g,
                            agsb[i % 2][:, :].rearrange(
                                "p (cg hh c) -> p cg hh c", hh=4, c=128)[
                                :, :, hh], dyn)
                    return s_ag[i % 2].v

                def load_xr(i):
                    return s_xr[i % 2].dma(sy, xr[i % 2][:, :],
                                           XR[i * 128:(i + 1) * 128, :])

                g.wait_ge(cC, 5)
                load_agsb(0)
                if PH == "AG2":
                    s_ag[0].wait(sy)
                    s_dump.dma(sy, AGD, agsb[0][:, :])
                    s_dump.wait(sy)
                    return nc
                load_xr(0)
                load_agsb(1)
                load_xr(1)

                mm_done = [0] * 8
                resid_evt = [0] * 8
                opfree = [0, 0]
                x1_evt = [0] * 8
                g_wot.wait(t)
                for i in range(8):
                    pb = i % 2
                    t.wait_ge(s_ag[i % 2].sem, 64 * (i // 2 + 1))
                    if opfree[pb]:
                        t.wait_ge(vC, opfree[pb])
                    for ec in range(16):
                        for f4 in range(4):
                            ins = t.matmul(
                                op_ps[pb][:, f4 * 512:(f4 + 1) * 512],
                                agsb[pb][:, ec * 128:(ec + 1) * 128],
                                wot[:, ec * E + f4 * 512:
                                    ec * E + f4 * 512 + 512],
                                start=(ec == 0), stop=(ec == 15),
                                skip_group_check=True,
                            )
                    mm_done[i] = tinc(ins)
                    v.wait_ge(tC, mm_done[i])
                    v.wait_ge(s_xr[i % 2].sem, 16 * (i // 2 + 1))
                    for f4 in range(4):
                        ins = v.scalar_tensor_tensor(
                            resid[pb][:, f4 * 512:(f4 + 1) * 512],
                            op_ps[pb][:, f4 * 512:(f4 + 1) * 512], 1.0,
                            xr[pb][:, f4 * 512:(f4 + 1) * 512],
                            ALU.mult, ALU.add)
                    opfree[pb] = vinc(ins)
                    resid_evt[i] = opfree[pb]
                    if PH == "OP" and i == 0:
                        sy.wait_ge(vC, resid_evt[0])
                        s_dump.dma(sy, OPD, resid[0][:, :])
                        s_dump.wait(sy)
                        return nc
                    if i >= 2:
                        v.wait_ge(s_x1st[i % 2].sem, 16 * (i // 2))
                    x1_evt[i] = ln_tail(resid[pb][:, :], scr2[:, :],
                                        lnb[pb], x1o[pb][:, :], resid_evt[i],
                                        scr2_free)
                    sy.wait_ge(vC, x1_evt[i])
                    if PH == "L1" and i == 0:
                        s_dump.dma(sy, L1D, x1o[0][:, :])
                        s_dump.wait(sy)
                        return nc
                    s_x1st[i % 2].dma(sy, X1D[i * 128:(i + 1) * 128, :],
                                      x1o[pb][:, :])
                    if i + 2 < 8:
                        g.wait_ge(tC, mm_done[i])
                        load_agsb(i + 2)
                        sy.wait_ge(vC, resid_evt[i])
                        load_xr(i + 2)
            if PH == "C1":
                s_x1st[0].wait(sy)
                s_x1st[1].wait(sy)
                s_dump.dma(sy, X1O, X1D.ap())
                s_dump.wait(sy)
                return nc

            with ExitStack() as c2x:
                e2 = c2x.enter_context
                x1i = [e2(nc.sbuf_tensor(f"x1i{i}", [128, E], F32))
                       for i in range(2)]
                x1t = [e2(nc.sbuf_tensor(f"x1t{i}", [128, 16 * 128], BF16))
                       for i in range(2)]
                ysb = [e2(nc.sbuf_tensor(f"ysb{i}", [128, E], F32))
                       for i in range(2)]
                resid2 = [e2(nc.sbuf_tensor(f"resid2{i}", [128, E], F32))
                          for i in range(2)]
                qe = e2(nc.sbuf_tensor("qe", [128, 8], F32))
                qlt = e2(nc.sbuf_tensor("qlt", [128, 8], F32))
                cw = e2(nc.sbuf_tensor("cw", [128, 8], F32))
                g1 = e2(nc.sbuf_tensor("g1", [128, 1], F32))
                g2 = e2(nc.sbuf_tensor("g2", [128, 1], F32))
                gneg = e2(nc.sbuf_tensor("gneg", [128, 1], F32))
                gss = e2(nc.sbuf_tensor("gss", [128, 1], F32))
                t_bf = e2(nc.sbuf_tensor("t_bf", [128, 128], BF16))
                tt_sb = e2(nc.sbuf_tensor("tt_sb", [128, 128], BF16))
                scr2b = e2(nc.sbuf_tensor("scr2b", [128, E], F32))
                scr2b_free = [0]
                lnb2 = [{k: e2(nc.sbuf_tensor(f"ln2_{k}{i}", [128, 1], F32))
                         for k in ("mraw", "m", "s2", "msq", "var", "lnv",
                                   "rstd", "nmr")} for i in range(2)]
                mn_ps = e2(nc.psum_tensor("mn_ps", [128, E], F32))
                ga_ps = e2(nc.psum_tensor("ga_ps", [128, 136], F32))
                tt_ps = e2(nc.psum_tensor("tt_ps", [128, 128], BF16))
                x1t_ps = [e2(nc.psum_tensor(f"x1t_ps{i}", [128, 512], F32))
                          for i in range(2)]

                def load_x1(i):
                    return g_x1[i % 2].dma(g, x1i[i % 2][:, :],
                                           X1D[i * 128:(i + 1) * 128, :])

                s_x1st[0].wait(g)
                s_x1st[1].wait(g)
                g.wait_ge(vC, x1_evt[7])
                g.wait_ge(tC, mm_done[7])
                load_x1(0)
                load_x1(1)

                g_wf.wait(t)
                s_const.wait(t)

                x1t_evt = [0] * 8
                ga_done = [0] * 8
                x1t_psfree = [0, 0]
                mnfree = [0] * 8
                gafree = 0
                ttfree = 0
                y_evt = [0] * 8

                def emit_x1t(i):
                    pb2 = i % 2
                    t.wait_ge(g_x1[i % 2].sem, 16 * (i // 2 + 1))
                    for q4 in range(4):
                        qp = q4 % 2
                        if x1t_psfree[qp]:
                            t.wait_ge(vC, x1t_psfree[qp])
                        for j in range(4):
                            ins = t.transpose(
                                x1t_ps[qp][:, j * 128:(j + 1) * 128],
                                x1i[pb2][:, (q4 * 4 + j) * 128:
                                         (q4 * 4 + j) * 128 + 128],
                                id32[:, :])
                        evt = tinc(ins)
                        v.wait_ge(tC, evt)
                        if i >= 2:
                            v.wait_ge(tC, ga_done[i - 2])
                        x1t_psfree[qp] = vinc(v.tensor_copy(
                            x1t[i % 2][:, q4 * 512:(q4 + 1) * 512],
                            x1t_ps[qp][:, :]))
                    x1t_evt[i] = n["vC"]

                emit_x1t(0)
                for i in range(8):
                    pb = i % 2
                    t.wait_ge(vC, x1t_evt[i])
                    if mnfree[i - 1] and i >= 1:
                        t.wait_ge(vC, mnfree[i - 1])
                    if gafree:
                        t.wait_ge(vC, gafree)
                    for ec in range(16):
                        t.matmul(ga_ps[:, :],
                                 x1t[pb][:, ec * 128:(ec + 1) * 128],
                                 gat[:, ec * 136:(ec + 1) * 136],
                                 start=(ec == 0), stop=(ec == 15),
                                 skip_group_check=True)
                        for f4 in range(4):
                            ins = t.matmul(
                                mn_ps[:, f4 * 512:(f4 + 1) * 512],
                                x1t[pb][:, ec * 128:(ec + 1) * 128],
                                wft[:, ec * E + f4 * 512:
                                     ec * E + f4 * 512 + 512],
                                start=(ec == 0), stop=False,
                                skip_group_check=True)
                    ga_done[i] = tinc(ins)
                    if i + 1 < 8:
                        emit_x1t(i + 1)

                    v.wait_ge(tC, ga_done[i])
                    ve = vinc(v.tensor_reduce(gneg[:, :], ga_ps[:, 0:8],
                                              AX.X, ALU.max, negate=True))
                    sc.wait_ge(vC, ve)
                    ae = ainc(sc.activation(qe[:, :], ga_ps[:, 0:8], AFT.Exp,
                                            bias=gneg[:, :]))
                    v.wait_ge(aC, ae)
                    vsync(v.reduce_max(g1[:, :], qe[:, :], AX.X))
                    vsync(v.tensor_single_scalar(qlt[:, :], qe[:, :],
                                                 g1[:, :], ALU.is_lt))
                    vsync(v.tensor_mul(qlt[:, :], qe[:, :], qlt[:, :]))
                    vsync(v.reduce_max(g2[:, :], qlt[:, :], AX.X))
                    vsync(v.tensor_add(gss[:, :], g1[:, :], g2[:, :]))
                    vsync(v.reciprocal(gss[:, :], gss[:, :]))
                    vsync(v.tensor_scalar_mul(gss[:, :], gss[:, :], 16.0))
                    vsync(v.tensor_single_scalar(qlt[:, :], qe[:, :],
                                                 g2[:, :], ALU.is_ge))
                    vsync(v.scalar_tensor_tensor(cw[:, :], qe[:, :],
                                                 gss[:, :], qlt[:, :],
                                                 ALU.mult, ALU.mult))
                    for ex in range(8):
                        ins = v.tensor_scalar(
                            t_bf[:, ex * 16:(ex + 1) * 16],
                            ga_ps[:, 8 + ex * 16: 8 + ex * 16 + 16],
                            cw[:, ex:ex + 1], None, ALU.mult)
                    gafree = vinc(ins)

                    t.wait_ge(vC, gafree)
                    if ttfree:
                        t.wait_ge(vC, ttfree)
                    tte = tinc(t.transpose(tt_ps[:, :], t_bf[:, :],
                                           id16[:, :]))
                    v.wait_ge(tC, tte)
                    ttv = vinc(v.tensor_copy(tt_sb[:, :], tt_ps[:, :]))
                    ttfree = ttv
                    t.wait_ge(vC, ttv)
                    for f4 in range(4):
                        ins = t.matmul(mn_ps[:, f4 * 512:(f4 + 1) * 512],
                                       tt_sb[:, :],
                                       btb[:, f4 * 512:(f4 + 1) * 512],
                                       start=False, stop=True,
                                       skip_group_check=True)
                    ffn_done = tinc(ins)

                    v.wait_ge(tC, ffn_done)
                    for f4 in range(4):
                        ins = v.scalar_tensor_tensor(
                            resid2[pb][:, f4 * 512:(f4 + 1) * 512],
                            mn_ps[:, f4 * 512:(f4 + 1) * 512], 1.0,
                            x1i[pb][:, f4 * 512:(f4 + 1) * 512],
                            ALU.mult, ALU.add)
                    mnfree[i] = vinc(ins)
                    if i >= 2:
                        v.wait_ge(s_y[i % 2].sem, 16 * (i // 2))
                    y_evt[i] = ln_tail(resid2[pb][:, :], scr2b[:, :],
                                       lnb2[pb], ysb[pb][:, :], mnfree[i],
                                       scr2b_free)
                    sy.wait_ge(vC, y_evt[i])
                    s_y[i % 2].dma(sy, Y[i * 128:(i + 1) * 128, :],
                                   ysb[pb][:, :])
                    if i + 2 < 8:
                        g.wait_ge(vC, mnfree[i])
                        load_x1(i + 2)

                s_y[0].wait(sy)
                s_y[1].wait(sy)

    return nc


_NC = None


def _get_nc():
    global _NC
    if _NC is None:
        _NC = _build_nc()
    return _NC


def _make_consts():
    ROT = 64
    inv_freq = (1.0 / 10000.0 ** (np.arange(0, ROT, 2) / ROT)).astype(np.float32)
    pos = np.arange(S, dtype=np.float32)
    freqs = pos[None, :] * inv_freq[:, None]
    cos = np.repeat(np.cos(freqs), 2, axis=0).astype(ml_dtypes.bfloat16)
    sin = np.repeat(np.sin(freqs), 2, axis=0).astype(ml_dtypes.bfloat16)
    R = np.zeros((64, 64), np.float32)
    for i in range(32):
        R[2 * i, 2 * i + 1] = -1.0
        R[2 * i + 1, 2 * i] = 1.0
    RT = np.ascontiguousarray(R.T).astype(ml_dtypes.bfloat16)
    id16 = np.eye(128, dtype=np.float32).astype(ml_dtypes.bfloat16)
    id32 = np.eye(128, dtype=np.float32)
    k = np.arange(128)[:, None]
    q = np.arange(QG)[None, :]
    masks = np.concatenate(
        [(k + 128 * d <= q).astype(np.float32) for d in range(4)],
        axis=1).astype(ml_dtypes.bfloat16)
    return cos, sin, RT, id16, id32, masks


def _make_in_maps(L0_attn_hidden_states, L0_attn_w_qkv, L0_attn_w_out,
                  L0_ffn_weight, L0_ffn_gate_w, L0_ffn_A_w, L0_ffn_B_w):
    X = np.ascontiguousarray(np.asarray(L0_attn_hidden_states, np.float32))
    Wqkv = np.asarray(L0_attn_w_qkv, np.float32)
    Wout = np.asarray(L0_attn_w_out, np.float32)
    Wffn = np.asarray(L0_ffn_weight, np.float32)
    Gw = np.asarray(L0_ffn_gate_w, np.float32)
    Aw = np.asarray(L0_ffn_A_w, np.float32)
    Bw = np.asarray(L0_ffn_B_w, np.float32)

    cos, sin, RT, id16, id32, masks = _make_consts()
    WOTn = np.ascontiguousarray(Wout.T)
    WFTn = np.ascontiguousarray(Wffn.T)
    GATn = np.ascontiguousarray(np.concatenate(
        [Gw.T, Aw.reshape(128, E).T], axis=1))
    BTDn = np.ascontiguousarray(Bw.transpose(0, 2, 1).reshape(128, E))

    in_maps = []
    for c in range(8):
        b, gq = c // 4, c % 4
        in_maps.append({
            "XT": np.ascontiguousarray(X[b].T),
            "XR": np.ascontiguousarray(X[b, gq * TOK:(gq + 1) * TOK]),
            "WQT": np.ascontiguousarray(Wqkv[gq * 1536: gq * 1536 + 512].T),
            "WVT": np.ascontiguousarray(
                Wqkv[gq * 1536 + 512: gq * 1536 + 1024].T),
            "WKT": np.ascontiguousarray(
                Wqkv[gq * 1536 + 1024: gq * 1536 + 1536].T),
            "WOT": WOTn, "WFT": WFTn, "GAT": GATn, "BTD": BTDn,
            "COS": cos, "SIN": sin, "RTD": RT, "ID16": id16, "ID32": id32,
            "MASKS": masks,
        })

    return in_maps


def kernel(**inputs):
    in_maps = _make_in_maps(**inputs)
    nc = _get_nc()
    res = run_bass_kernel_spmd(nc, in_maps, core_ids=list(range(8)))
    out = np.empty((B, S, E), np.float32)
    for c in range(8):
        b, gq = c // 4, c % 4
        out[b, gq * TOK:(gq + 1) * TOK] = res.results[c]["Y"]
    return out


# version 09
# speedup vs baseline: 1.8324x; 1.0277x over previous
import numpy as np
import ml_dtypes
from contextlib import ExitStack

import concourse.bass as bass
import concourse.mybir as mybir
from concourse.ap import AP
from concourse.bass_utils import run_bass_kernel_spmd

F32 = mybir.dt.float32
BF16 = mybir.dt.bfloat16
AFT = mybir.ActivationFunctionType
ALU = mybir.AluOpType
AX = mybir.AxisListType

_PHASES = "full"
_DETECT_RACES = False

B, S, E = 2, 4096, 2048
HG = 4
TOK = 1024
QG = 512
INV_SQRT_D = float(1.0 / np.sqrt(128.0))
LN_EPS = 1e-5


def _build_nc():
    nc = bass.Bass("TRN2", target_bir_lowering=False, debug=False,
                   detect_race_conditions=_DETECT_RACES)

    XT = nc.dram_tensor("XT", [E, S], F32, kind="ExternalInput").ap()
    XR = nc.dram_tensor("XR", [TOK, E], F32, kind="ExternalInput").ap()
    WQT = nc.dram_tensor("WQT", [E, 512], F32, kind="ExternalInput").ap()
    WKT = nc.dram_tensor("WKT", [E, 512], F32, kind="ExternalInput").ap()
    WVT = nc.dram_tensor("WVT", [E, 512], F32, kind="ExternalInput").ap()
    WOT = nc.dram_tensor("WOT", [E, E], F32, kind="ExternalInput").ap()
    WFT = nc.dram_tensor("WFT", [E, E], F32, kind="ExternalInput").ap()
    GAT = nc.dram_tensor("GAT", [E, 136], F32, kind="ExternalInput").ap()
    BTD = nc.dram_tensor("BTD", [128, E], F32, kind="ExternalInput").ap()
    COS = nc.dram_tensor("COS", [64, S], BF16, kind="ExternalInput").ap()
    SIN = nc.dram_tensor("SIN", [64, S], BF16, kind="ExternalInput").ap()
    RTD = nc.dram_tensor("RTD", [64, 64], BF16, kind="ExternalInput").ap()
    ID16 = nc.dram_tensor("ID16", [128, 128], BF16, kind="ExternalInput").ap()
    ID32 = nc.dram_tensor("ID32", [128, 128], F32, kind="ExternalInput").ap()
    MASKS = nc.dram_tensor("MASKS", [128, 4 * QG], BF16, kind="ExternalInput").ap()
    Y = nc.dram_tensor("Y", [TOK, E], F32, kind="ExternalOutput").ap()
    PH = _PHASES
    if PH == "A":
        QD = nc.dram_tensor("QD", [128, HG * S], BF16, kind="ExternalOutput").ap()
        KD = nc.dram_tensor("KD", [128, HG * S], BF16, kind="ExternalOutput").ap()
        VD = nc.dram_tensor("VD", [128, HG * 32 * 128], BF16, kind="ExternalOutput").ap()
    if PH == "B":
        AOD = nc.dram_tensor("AOD", [128, HG * S], BF16, kind="ExternalOutput").ap()
    if PH == "X":
        XOD = nc.dram_tensor("XOD", [E, S], BF16, kind="ExternalOutput").ap()
    if PH == "C1":
        X1O = nc.dram_tensor("X1O", [TOK, E], F32, kind="ExternalOutput").ap()
    if PH == "AG2":
        AGD = nc.dram_tensor("AGD", [128, 16 * 128], BF16, kind="ExternalOutput").ap()
    if PH == "OP":
        OPD = nc.dram_tensor("OPD", [128, E], F32, kind="ExternalOutput").ap()
    if PH == "L1":
        L1D = nc.dram_tensor("L1D", [128, E], F32, kind="ExternalOutput").ap()

    AGINH = [nc.dram_tensor(f"AGINH{h}", [128, S], BF16) for h in range(HG)]
    AGOUTH = [nc.dram_tensor(f"AGOUTH{h}", [8 * 128, S], BF16)
              for h in range(HG)]
    BARIN = nc.dram_tensor("BARIN", [1, 16], F32)
    BAROUT = nc.dram_tensor("BAROUT", [8, 16], F32)
    X1D = nc.dram_tensor("X1D", [TOK, E], F32)

    g = nc.gpsimd
    sy = nc.sync
    t = nc.tensor
    v = nc.vector
    sc = nc.scalar

    with ExitStack() as top:
        et = top.enter_context
        ones_col = et(nc.sbuf_tensor("ones_col", [128, 1], BF16))
        ones128 = et(nc.sbuf_tensor("ones128", [128, 128], BF16))
        ones_row = et(nc.sbuf_tensor("ones_row", [1, 128], BF16))
        id16 = et(nc.sbuf_tensor("id16", [128, 128], BF16))
        id32 = et(nc.sbuf_tensor("id32", [128, 128], F32))
        eps_b = et(nc.sbuf_tensor("eps_b", [128, 1], F32))
        zb = et(nc.sbuf_tensor("zb", [1, 16], F32))

        tC = et(nc.semaphore("tC"))
        aC = et(nc.semaphore("aC"))
        vC = et(nc.semaphore("vC"))
        cC = et(nc.semaphore("cC"))

        class Stream:

            def __init__(self, name):
                self.sem = et(nc.semaphore(name))
                self.v = 0

            def dma(self, eng, out, in_):
                eng.dma_start(out=out, in_=in_).then_inc(self.sem, 16)
                self.v += 16
                return self.v

            def wait(self, eng, val=None):
                eng.wait_ge(self.sem, self.v if val is None else val)

        g_xt = [Stream("g_xt0"), Stream("g_xt1")]
        g_w = Stream("g_w")
        g_wot = Stream("g_wot")
        g_wf = Stream("g_wf")
        g_x1 = [Stream("g_x10"), Stream("g_x11")]
        s_const = Stream("s_const")
        s_masks = Stream("s_masks")
        s_agin = Stream("s_agin")
        s_agh = [[Stream(f"s_ag{p}h{h}") for h in range(4)]
                 for p in range(2)]
        s_xr = [Stream("s_xr0"), Stream("s_xr1")]
        s_x1st = [Stream("s_x1st0"), Stream("s_x1st1")]
        s_y = [Stream("s_y0"), Stream("s_y1")]
        s_bar = Stream("s_bar")
        s_dump = Stream("s_dump")

        n = {"tC": 0, "aC": 0, "vC": 0}

        def vsync(ins):
            ev = vinc(ins)
            v.wait_ge(vC, ev)
            return ev

        def tinc(ins):
            ins.then_inc(tC, 1)
            n["tC"] += 1
            return n["tC"]

        def ainc(ins):
            ins.then_inc(aC, 1)
            n["aC"] += 1
            return n["aC"]

        def vinc(ins):
            ins.then_inc(vC, 1)
            n["vC"] += 1
            return n["vC"]

        v.memset(ones_col[:, :], 1.0)
        v.memset(ones128[:, :], 1.0)
        v.memset(ones_row[:, :], 1.0)
        v.memset(eps_b[:, :], LN_EPS)
        zb_evt = vinc(v.memset(zb[:, :], 0.0))
        s_const.dma(sy, id16[:, :], ID16)
        s_const.dma(sy, id32[:, :], ID32)

        with ExitStack() as abx:
            eab = abx.enter_context
            qs = eab(nc.sbuf_tensor("qs", [128, HG * S], BF16, side="right"))
            ks = eab(nc.sbuf_tensor("ks", [128, HG * S], BF16, side="right"))
            vs = eab(nc.sbuf_tensor("vs", [128, HG * 32 * 128], BF16, side="right"))

            with ExitStack() as ax:
                ea = ax.enter_context
                wq = ea(nc.sbuf_tensor("wq", [128, 16 * 512], BF16, side="right"))
                wk = ea(nc.sbuf_tensor("wk", [128, 16 * 512], BF16, side="right"))
                wv = ea(nc.sbuf_tensor("wv", [128, 16 * 512], BF16, side="right"))
                xtb = [ea(nc.sbuf_tensor(f"xtb{i}", [128, 16 * 512], BF16, side="right"))
                       for i in range(2)]
                cos = ea(nc.sbuf_tensor("cos", [64, S], BF16, side="right"))
                sin = ea(nc.sbuf_tensor("sin", [64, S], BF16, side="right"))
                rts = ea(nc.sbuf_tensor("rts", [64, 64], BF16, side="right"))
                rtmp1 = ea(nc.sbuf_tensor("rtmp1", [64, 512], BF16, side="right"))
                rtmp2 = ea(nc.sbuf_tensor("rtmp2", [64, 512], BF16, side="right"))
                qk_ps = [ea(nc.psum_tensor(f"qk_ps{i}", [128, 512], F32))
                         for i in range(2)]
                v_ps = [ea(nc.psum_tensor(f"v_ps{i}", [128, 512], F32))
                        for i in range(2)]
                rz_ps = [ea(nc.psum_tensor(f"rz_ps{i}", [64, 512], F32))
                         for i in range(2)]

                s_const.dma(sy, cos[:, :], COS)
                s_const.dma(sy, sin[:, :], SIN)
                s_const.dma(sy, rts[:, :], RTD)

                def load_xt(i):
                    return g_xt[i % 2].dma(
                        g,
                        xtb[i % 2][:, :].rearrange("p (ec t) -> p ec t", ec=16),
                        XT.rearrange("(ec p) t -> p ec t", p=128)[
                            :, :, i * 512:(i + 1) * 512],
                    )

                load_xt(0)
                g_w.dma(g, wq[:, :].rearrange("p (ec f) -> p ec f", ec=16),
                        WQT.rearrange("(ec p) f -> p ec f", p=128))
                g_w.dma(g, wk[:, :].rearrange("p (ec f) -> p ec f", ec=16),
                        WKT.rearrange("(ec p) f -> p ec f", p=128))
                g_w.dma(g, wv[:, :].rearrange("p (ec f) -> p ec f", ec=16),
                        WVT.rearrange("(ec p) f -> p ec f", p=128))
                load_xt(1)

                qk_free = [0, 0]
                v_free = [0, 0]
                tile_done_t = [0] * 8

                for it in range(8):
                    t0 = it * 512
                    t.wait_ge(g_xt[it % 2].sem, 16 * (it // 2 + 1))
                    if it == 0:
                        g_w.wait(t)
                    for fc in range(8):
                        w = wq if fc < 4 else wk
                        hl = fc % 4
                        ps = qk_ps[fc % 2]
                        if qk_free[fc % 2]:
                            t.wait_ge(vC, qk_free[fc % 2])
                        for ec in range(16):
                            ins = t.matmul(
                                ps[:, :],
                                w[:, ec * 512 + hl * 128: ec * 512 + hl * 128 + 128],
                                xtb[it % 2][:, ec * 512:(ec + 1) * 512],
                                start=(ec == 0), stop=(ec == 15),
                            )
                        evt = tinc(ins)
                        dst = qs if fc < 4 else ks
                        v.wait_ge(tC, evt)
                        qk_free[fc % 2] = vinc(v.tensor_copy(
                            dst[:, hl * S + t0: hl * S + t0 + 512], ps[:, :]))
                    for sub in range(4):
                        ps = v_ps[sub % 2]
                        if v_free[sub % 2]:
                            t.wait_ge(vC, v_free[sub % 2])
                        for ec in range(16):
                            ins = t.matmul(
                                ps[:, :],
                                xtb[it % 2][:, ec * 512 + sub * 128:
                                            ec * 512 + sub * 128 + 128],
                                wv[:, ec * 512:(ec + 1) * 512],
                                start=(ec == 0), stop=(ec == 15),
                            )
                        evt = tinc(ins)
                        kt = it * 4 + sub
                        v.wait_ge(tC, evt)
                        v_free[sub % 2] = vinc(v.tensor_copy(
                            vs[:, :].rearrange("p (h k d) -> p h k d",
                                               h=HG, k=32)[:, :, kt, :],
                            ps[:, :].rearrange("p (h d) -> p h d", h=HG)))
                    tile_done_t[it] = n["tC"]
                    if it + 2 < 8:
                        g.wait_ge(tC, tile_done_t[it])
                        load_xt(it + 2)

                qkv_evac_all = n["vC"]
                t.wait_ge(vC, qkv_evac_all)
                s_const.wait(v)
                s_const.wait(t)
                rz_free = [0, 0]
                ri = 0
                for zs in (qs, ks):
                    for h in range(HG):
                        for sl in range(8):
                            s0 = sl * 512
                            par = ri % 2
                            ri += 1
                            ps = rz_ps[par]
                            if rz_free[par]:
                                t.wait_ge(vC, rz_free[par])
                            ins = t.matmul(
                                ps[:, :], rts[:, :],
                                zs[0:64, h * S + s0: h * S + s0 + 512],
                                start=True, stop=True)
                            evt = tinc(ins)
                            v.wait_ge(tC, evt)
                            v.tensor_mul(rtmp1[:, :], ps[:, :],
                                         sin[:, s0:s0 + 512])
                            vsync(v.tensor_mul(
                                rtmp2[:, :],
                                zs[0:64, h * S + s0: h * S + s0 + 512],
                                cos[:, s0:s0 + 512]))
                            rz_free[par] = vinc(v.tensor_add(
                                zs[0:64, h * S + s0: h * S + s0 + 512],
                                rtmp1[:, :], rtmp2[:, :]))
                rope_done_v = n["vC"]
                rope_done_t = n["tC"]

            if PH == "A":
                sy.wait_ge(vC, rope_done_v)
                s_dump.dma(sy, QD, qs[:, :])
                s_dump.dma(sy, KD, ks[:, :])
                s_dump.dma(sy, VD, vs[:, :])
                s_dump.wait(sy)
                return nc

            wot = et(nc.sbuf_tensor("wot", [128, 16 * E], BF16))
            g.wait_ge(tC, rope_done_t)
            g.wait_ge(vC, rope_done_v)
            g_wot.dma(g, wot[:, :].rearrange("p (ec f) -> p ec f", ec=16),
                      WOT.rearrange("(ec p) f -> p ec f", p=128))

            with ExitStack() as bx:
                ebx = bx.enter_context
                ao = ebx(nc.sbuf_tensor("ao", [128, HG * S], BF16, side="right"))
                masks = ebx(nc.sbuf_tensor("masks", [128, 4 * QG], BF16, side="right"))
                pt = [ebx(nc.sbuf_tensor(f"pt{i}", [128, QG], BF16, side="right"))
                      for i in range(4)]
                nrec = [ebx(nc.sbuf_tensor(f"nrec{i}", [1, QG], BF16, side="right"))
                        for i in range(2)]
                bc_sb = [ebx(nc.sbuf_tensor(f"bc_sb{i}", [128, QG], BF16, side="right"))
                         for i in range(2)]
                s_ps = [ebx(nc.psum_tensor(f"s_ps{i}", [128, QG], F32))
                        for i in range(4)]
                o_ps = [ebx(nc.psum_tensor(f"o_ps{i}", [128, QG], F32))
                        for i in range(2)]
                b_ps = [ebx(nc.psum_tensor(f"b_ps{i}", [128, QG], F32))
                        for i in range(2)]

                sy.wait_ge(tC, rope_done_t)
                sy.wait_ge(vC, rope_done_v)
                s_masks.dma(sy, masks[:, :], MASKS)
                s_masks.wait(v)
                t.wait_ge(vC, rope_done_v)

                sy.wait_ge(vC, zb_evt)
                s_bar.dma(sy, BARIN.ap(), zb[:, :])

                pt_evt_a = [0, 0, 0, 0]
                pt_evt_v = [0, 0, 0, 0]
                pt_masked = [False] * 4
                pt_used = [0, 0, 0, 0]
                s_exped = [0, 0, 0, 0]
                on_free = [0, 0]
                rec_evt_ob = [0, 0]
                ktc = 0
                state = {"pending": None, "bc": None}

                def emit_pv():
                    if state["pending"] is None:
                        return
                    par, ob2, hh, kt2, first, last, q0b = state["pending"]
                    state["pending"] = None
                    if pt_masked[par]:
                        t.wait_ge(vC, pt_evt_v[par])
                    else:
                        t.wait_ge(aC, pt_evt_a[par])
                    t.matmul(o_ps[ob2][:, :],
                             vs[:, (hh * 32 + kt2) * 128:
                                (hh * 32 + kt2) * 128 + 128],
                             pt[par][:, :], start=first, stop=last,
                             skip_group_check=True)
                    ins2 = t.matmul(b_ps[ob2][:, :], ones128[:, :],
                                    pt[par][:, :], start=first, stop=last,
                                    skip_group_check=True)
                    pt_used[par] = tinc(ins2)
                    if last:
                        v.wait_ge(tC, pt_used[par])
                        with nc.allow_low_precision(
                                reason="softmax normalizer reciprocal kept in"
                                       " bf16; same precision class as the "
                                       "bf16 attention matmul path"):
                            rec_evt_ob[ob2] = vsync(v.reciprocal(
                                bc_sb[ob2][:, :], b_ps[ob2][:, :]))
                        state["bc"] = (ob2, hh, q0b)

                def emit_bcast():
                    if state["bc"] is None:
                        return
                    ob2, hh, q0b = state["bc"]
                    state["bc"] = None
                    on_free[ob2] = vinc(v.tensor_mul(
                        ao[:, hh * S + q0b: hh * S + q0b + QG],
                        o_ps[ob2][:, :], bc_sb[ob2][:, :]))
                    if q0b == 7 * QG:
                        sy.wait_ge(vC, on_free[ob2])
                        s_agin.dma(sy, AGINH[hh].ap(),
                                   ao[:, hh * S:(hh + 1) * S])
                        s_agin.wait(g)
                        g.collective_compute(
                            "AllGather", ALU.bypass,
                            replica_groups=[[0, 1, 2, 3, 4, 5, 6, 7]],
                            ins=[AGINH[hh].ap().opt()],
                            outs=[AGOUTH[hh].ap().opt()],
                        ).then_inc(cC)

                for gi in range(HG * 8):
                    h, qg = gi // 8, gi % 8
                    q0 = qg * QG
                    ob = gi % 2
                    if on_free[ob]:
                        t.wait_ge(vC, on_free[ob])
                    nkt = 4 * (qg + 1)
                    for kt in range(nkt):
                        par = ktc % 4
                        ktc += 1
                        if s_exped[par]:
                            t.wait_ge(aC, s_exped[par])
                        ins = t.matmul(
                            s_ps[par][:, :],
                            ks[:, h * S + kt * 128: h * S + kt * 128 + 128],
                            qs[:, h * S + q0: h * S + q0 + QG],
                            start=True, stop=True, skip_group_check=True)
                        s_evt = tinc(ins)
                        emit_pv()
                        if kt >= 1:
                            emit_bcast()
                        sc.wait_ge(tC, s_evt)
                        if pt_used[par]:
                            sc.wait_ge(tC, pt_used[par])
                        s_exped[par] = ainc(sc.activation(
                            pt[par][:, :], s_ps[par][:, :], AFT.Exp,
                            scale=INV_SQRT_D))
                        delta = kt - 4 * qg
                        if delta >= 0:
                            v.wait_ge(aC, s_exped[par])
                            pt_evt_v[par] = vinc(v.tensor_mul(
                                pt[par][:, :], pt[par][:, :],
                                masks[:, delta * QG:(delta + 1) * QG]))
                            pt_masked[par] = True
                        else:
                            pt_evt_a[par] = s_exped[par]
                            pt_masked[par] = False
                        state["pending"] = (par, ob, h, kt, kt == 0,
                                            kt == nkt - 1, q0)
                emit_pv()
                emit_bcast()
                ao_done_v = n["vC"]

                if PH == "B":
                    sy.wait_ge(vC, ao_done_v)
                    s_dump.dma(sy, AOD, ao[:, :])
                    s_dump.wait(sy)
                    return nc

                s_bar.wait(g)
                g.collective_compute(
                    "AllGather", ALU.bypass,
                    replica_groups=[[0, 1, 2, 3, 4, 5, 6, 7]],
                    ins=[BARIN.ap().opt()], outs=[BAROUT.ap().opt()],
                ).then_inc(cC)

        if PH == "X":
            sy.wait_ge(cC, 5)
            for hh in range(HG):
                s_dump.dma(sy, XOD.rearrange("(cg hh p) q -> cg hh p q",
                                             hh=4, p=128)[:, hh],
                           AGOUTH[hh].ap().rearrange("(cg p) q -> cg p q",
                                                     p=128))
            s_dump.wait(sy)
            return nc

        pid_gp = g.partition_id()
        t0_gp = (pid_gp % 4) * TOK + (pid_gp // 4) * (4 * 128 * S)

        with ExitStack() as cx:
            ecx = cx.enter_context
            wft = ecx(nc.sbuf_tensor("wft", [128, 16 * E], BF16))
            gat = ecx(nc.sbuf_tensor("gat", [128, 16 * 136], BF16))
            btb = ecx(nc.sbuf_tensor("btb", [128, E], BF16))
            g_wf.dma(g, wft[:, :].rearrange("p (ec f) -> p ec f", ec=16),
                     WFT.rearrange("(ec p) f -> p ec f", p=128))
            g_wf.dma(g, gat[:, :].rearrange("p (ec f) -> p ec f", ec=16),
                     GAT.rearrange("(ec p) f -> p ec f", p=128))
            g_wf.dma(g, btb[:, :], BTD)

            def ln_tail(resid_ap, scr2_ap, lb, out_ap, resid_evt_v,
                        scr2_free):
                ve1 = vinc(v.reduce_sum(lb["mraw"][:, :], resid_ap, AX.X))
                sc.wait_ge(vC, resid_evt_v)
                if scr2_free[0]:
                    sc.wait_ge(vC, scr2_free[0])
                a1 = ainc(sc.activation(scr2_ap, resid_ap, AFT.Square))
                v.wait_ge(vC, ve1)
                v.tensor_scalar_mul(lb["m"][:, :], lb["mraw"][:, :], 1.0 / E)
                v.wait_ge(aC, a1)
                scr2_free[0] = vsync(v.reduce_sum(lb["s2"][:, :], scr2_ap,
                                                  AX.X))
                vsync(v.tensor_mul(lb["msq"][:, :], lb["m"][:, :],
                                   lb["m"][:, :]))
                ve4 = vinc(v.scalar_tensor_tensor(
                    lb["var"][:, :], lb["s2"][:, :], 1.0 / E,
                    lb["msq"][:, :], ALU.mult, ALU.subtract))
                sc.wait_ge(vC, ve4)
                aln = ainc(sc.activation(lb["lnv"][:, :], lb["var"][:, :],
                                         AFT.Ln, bias=eps_b[:, :]))
                sc.wait_ge(aC, aln)
                ae = ainc(sc.activation(lb["rstd"][:, :], lb["lnv"][:, :],
                                        AFT.Exp, scale=-0.5))
                v.wait_ge(aC, ae)
                vsync(v.tensor_scalar(lb["nmr"][:, :], lb["m"][:, :],
                                      lb["rstd"][:, :], -1.0, ALU.mult,
                                      ALU.mult))
                return vinc(v.tensor_scalar(out_ap, resid_ap,
                                            lb["rstd"][:, :], lb["nmr"][:, :],
                                            ALU.mult, ALU.add))

            with ExitStack() as c1x:
                e1 = c1x.enter_context
                agsb = [e1(nc.sbuf_tensor(f"agsb{i}", [128, 16 * 128], BF16))
                        for i in range(2)]
                xr = [e1(nc.sbuf_tensor(f"xr{i}", [128, E], F32))
                      for i in range(2)]
                resid = [e1(nc.sbuf_tensor(f"resid{i}", [128, E], F32))
                         for i in range(2)]
                x1o = [e1(nc.sbuf_tensor(f"x1o{i}", [128, E], F32))
                       for i in range(2)]
                scr2 = e1(nc.sbuf_tensor("scr2", [128, E], F32))
                scr2_free = [0]
                lnb = [{k: e1(nc.sbuf_tensor(f"ln_{k}{i}", [128, 1], F32))
                        for k in ("mraw", "m", "s2", "msq", "var", "lnv",
                                  "rstd", "nmr")} for i in range(2)]
                op_ps = [e1(nc.psum_tensor(f"op_ps{i}", [128, E], F32))
                         for i in range(2)]

                def load_agsb(i, heads=range(HG)):
                    for hh in heads:
                        g.wait_ge(cC, hh + 2)
                        base = AGOUTH[hh].ap().rearrange(
                            "(cg p) q -> p cg q", p=128)[
                            :, 0:4, i * 128:(i + 1) * 128]
                        dyn = AP(base.tensor, base.offset + t0_gp, base.ap)
                        s_agh[i % 2][hh].dma(
                            g,
                            agsb[i % 2][:, :].rearrange(
                                "p (cg hh c) -> p cg hh c", hh=4, c=128)[
                                :, :, hh], dyn)

                def load_xr(i):
                    return s_xr[i % 2].dma(sy, xr[i % 2][:, :],
                                           XR[i * 128:(i + 1) * 128, :])

                for hh in range(HG):
                    load_agsb(0, heads=(hh,))
                    load_agsb(1, heads=(hh,))
                if PH == "AG2":
                    for hh in range(4):
                        s_agh[0][hh].wait(sy)
                    s_dump.dma(sy, AGD, agsb[0][:, :])
                    s_dump.wait(sy)
                    return nc
                load_xr(0)
                load_xr(1)

                mm_done = [0] * 8
                resid_evt = [0] * 8
                opfree = [0, 0]
                x1_evt = [0] * 8
                g_wot.wait(t)
                for i in range(8):
                    pb = i % 2
                    if opfree[pb]:
                        t.wait_ge(vC, opfree[pb])
                    for hh in range(HG):
                        t.wait_ge(s_agh[pb][hh].sem, 16 * (i // 2 + 1))
                        for cg in range(4):
                            ec = cg * 4 + hh
                            for f4 in range(4):
                                ins = t.matmul(
                                    op_ps[pb][:, f4 * 512:(f4 + 1) * 512],
                                    agsb[pb][:, ec * 128:(ec + 1) * 128],
                                    wot[:, ec * E + f4 * 512:
                                        ec * E + f4 * 512 + 512],
                                    start=(hh == 0 and cg == 0),
                                    stop=(hh == 3 and cg == 3),
                                    skip_group_check=True,
                                )
                    mm_done[i] = tinc(ins)
                    if True:
                        v.wait_ge(tC, mm_done[i])
                        v.wait_ge(s_xr[i % 2].sem, 16 * (i // 2 + 1))
                        for f4 in range(4):
                            ins = v.scalar_tensor_tensor(
                                resid[pb][:, f4 * 512:(f4 + 1) * 512],
                                op_ps[pb][:, f4 * 512:(f4 + 1) * 512], 1.0,
                                xr[pb][:, f4 * 512:(f4 + 1) * 512],
                                ALU.mult, ALU.add)
                        opfree[pb] = vinc(ins)
                        resid_evt[i] = opfree[pb]
                        if PH == "OP" and i == 0:
                            sy.wait_ge(vC, resid_evt[0])
                            s_dump.dma(sy, OPD, resid[0][:, :])
                            s_dump.wait(sy)
                            return nc
                        if i >= 2:
                            v.wait_ge(s_x1st[i % 2].sem, 16 * (i // 2))
                        x1_evt[i] = ln_tail(resid[pb][:, :], scr2[:, :],
                                            lnb[pb], x1o[pb][:, :],
                                            resid_evt[i], scr2_free)
                        sy.wait_ge(vC, x1_evt[i])
                        if PH == "L1" and i == 0:
                            s_dump.dma(sy, L1D, x1o[0][:, :])
                            s_dump.wait(sy)
                            return nc
                        s_x1st[i % 2].dma(sy, X1D[i * 128:(i + 1) * 128, :],
                                          x1o[pb][:, :])
                        if i + 2 < 8:
                            g.wait_ge(tC, mm_done[i])
                            load_agsb(i + 2)
                            sy.wait_ge(vC, resid_evt[i])
                            load_xr(i + 2)
            if PH == "C1":
                s_x1st[0].wait(sy)
                s_x1st[1].wait(sy)
                s_dump.dma(sy, X1O, X1D.ap())
                s_dump.wait(sy)
                return nc

            with ExitStack() as c2x:
                e2 = c2x.enter_context
                x1i = [e2(nc.sbuf_tensor(f"x1i{i}", [128, E], F32))
                       for i in range(2)]
                x1t = [e2(nc.sbuf_tensor(f"x1t{i}", [128, 16 * 128], BF16))
                       for i in range(2)]
                ysb = [e2(nc.sbuf_tensor(f"ysb{i}", [128, E], F32))
                       for i in range(2)]
                resid2 = [e2(nc.sbuf_tensor(f"resid2{i}", [128, E], F32))
                          for i in range(2)]
                qe = e2(nc.sbuf_tensor("qe", [128, 8], F32))
                qlt = e2(nc.sbuf_tensor("qlt", [128, 8], F32))
                cw = e2(nc.sbuf_tensor("cw", [128, 8], F32))
                g1 = e2(nc.sbuf_tensor("g1", [128, 1], F32))
                g2 = e2(nc.sbuf_tensor("g2", [128, 1], F32))
                gneg = e2(nc.sbuf_tensor("gneg", [128, 1], F32))
                gss = e2(nc.sbuf_tensor("gss", [128, 1], F32))
                t_bf = [e2(nc.sbuf_tensor(f"t_bf{i}", [128, 128], BF16))
                        for i in range(2)]
                tt_sb = [e2(nc.sbuf_tensor(f"tt_sb{i}", [128, 128], BF16))
                         for i in range(2)]
                scr2b = e2(nc.sbuf_tensor("scr2b", [128, E], F32))
                scr2b_free = [0]
                lnb2 = [{k: e2(nc.sbuf_tensor(f"ln2_{k}{i}", [128, 1], F32))
                         for k in ("mraw", "m", "s2", "msq", "var", "lnv",
                                   "rstd", "nmr")} for i in range(2)]
                mn_ps = e2(nc.psum_tensor("mn_ps", [128, E], F32))
                ga_ps = e2(nc.psum_tensor("ga_ps", [128, 136], F32))
                tt_ps = e2(nc.psum_tensor("tt_ps", [128, 128], BF16))
                x1t_ps = [e2(nc.psum_tensor(f"x1t_ps{i}", [128, 512], F32))
                          for i in range(2)]

                def load_x1(i):
                    return g_x1[i % 2].dma(g, x1i[i % 2][:, :],
                                           X1D[i * 128:(i + 1) * 128, :])

                s_x1st[0].wait(g)
                s_x1st[1].wait(g)
                g.wait_ge(vC, x1_evt[7])
                g.wait_ge(tC, mm_done[7])
                load_x1(0)
                load_x1(1)

                g_wf.wait(t)
                s_const.wait(t)

                x1t_evt = [0] * 8
                main_done = [0] * 8
                ga_evt = [0] * 8
                gate_done = [0] * 8
                ttv = [0] * 8
                x1t_psfree = [0, 0]
                mnfree = [0] * 8
                y_evt = [0] * 8

                def emit_x1t(i):
                    pb2 = i % 2
                    t.wait_ge(g_x1[i % 2].sem, 16 * (i // 2 + 1))
                    for q4 in range(4):
                        qp = q4 % 2
                        if x1t_psfree[qp]:
                            t.wait_ge(vC, x1t_psfree[qp])
                        for j in range(4):
                            ins = t.transpose(
                                x1t_ps[qp][:, j * 128:(j + 1) * 128],
                                x1i[pb2][:, (q4 * 4 + j) * 128:
                                         (q4 * 4 + j) * 128 + 128],
                                id32[:, :])
                        evt = tinc(ins)
                        v.wait_ge(tC, evt)
                        if i >= 2:
                            v.wait_ge(tC, main_done[i - 2])
                        x1t_psfree[qp] = vinc(v.tensor_copy(
                            x1t[i % 2][:, q4 * 512:(q4 + 1) * 512],
                            x1t_ps[qp][:, :]))
                    x1t_evt[i] = n["vC"]

                def emit_ga(i):
                    t.wait_ge(vC, x1t_evt[i])
                    for ec in range(16):
                        ins = t.matmul(ga_ps[:, :],
                                       x1t[i % 2][:, ec * 128:(ec + 1) * 128],
                                       gat[:, ec * 136:(ec + 1) * 136],
                                       start=(ec == 0), stop=(ec == 15),
                                       skip_group_check=True)
                    ga_evt[i] = tinc(ins)

                def emit_gate_chain(i):
                    v.wait_ge(tC, ga_evt[i])
                    ve = vinc(v.tensor_reduce(gneg[:, :], ga_ps[:, 0:8],
                                              AX.X, ALU.max, negate=True))
                    sc.wait_ge(vC, ve)
                    ae = ainc(sc.activation(qe[:, :], ga_ps[:, 0:8], AFT.Exp,
                                            bias=gneg[:, :]))
                    v.wait_ge(aC, ae)
                    vsync(v.reduce_max(g1[:, :], qe[:, :], AX.X))
                    vsync(v.tensor_single_scalar(qlt[:, :], qe[:, :],
                                                 g1[:, :], ALU.is_lt))
                    vsync(v.tensor_mul(qlt[:, :], qe[:, :], qlt[:, :]))
                    vsync(v.reduce_max(g2[:, :], qlt[:, :], AX.X))
                    vsync(v.tensor_add(gss[:, :], g1[:, :], g2[:, :]))
                    vsync(v.reciprocal(gss[:, :], gss[:, :]))
                    vsync(v.tensor_scalar_mul(gss[:, :], gss[:, :], 16.0))
                    vsync(v.tensor_single_scalar(qlt[:, :], qe[:, :],
                                                 g2[:, :], ALU.is_ge))
                    vsync(v.scalar_tensor_tensor(cw[:, :], qe[:, :],
                                                 gss[:, :], qlt[:, :],
                                                 ALU.mult, ALU.mult))
                    for ex in range(8):
                        ins = v.tensor_scalar(
                            t_bf[i % 2][:, ex * 16:(ex + 1) * 16],
                            ga_ps[:, 8 + ex * 16: 8 + ex * 16 + 16],
                            cw[:, ex:ex + 1], None, ALU.mult)
                    gate_done[i] = vinc(ins)

                emit_x1t(0)
                emit_ga(0)
                emit_gate_chain(0)
                for i in range(8):
                    pb = i % 2
                    t.wait_ge(vC, x1t_evt[i])
                    if i >= 1 and mnfree[i - 1]:
                        t.wait_ge(vC, mnfree[i - 1])
                    for ec in range(16):
                        for f4 in range(4):
                            ins = t.matmul(
                                mn_ps[:, f4 * 512:(f4 + 1) * 512],
                                x1t[pb][:, ec * 128:(ec + 1) * 128],
                                wft[:, ec * E + f4 * 512:
                                     ec * E + f4 * 512 + 512],
                                start=(ec == 0), stop=False,
                                skip_group_check=True)
                    main_done[i] = tinc(ins)
                    t.wait_ge(vC, gate_done[i])
                    tte = tinc(t.transpose(tt_ps[:, :], t_bf[pb][:, :],
                                           id16[:, :]))
                    v.wait_ge(tC, tte)
                    ttv[i] = vinc(v.tensor_copy(tt_sb[pb][:, :], tt_ps[:, :]))
                    t.wait_ge(vC, ttv[i])
                    for f4 in range(4):
                        ins = t.matmul(mn_ps[:, f4 * 512:(f4 + 1) * 512],
                                       tt_sb[pb][:, :],
                                       btb[:, f4 * 512:(f4 + 1) * 512],
                                       start=False, stop=True,
                                       skip_group_check=True)
                    ffn_done = tinc(ins)
                    if i + 1 < 8:
                        emit_x1t(i + 1)
                    v.wait_ge(tC, ffn_done)
                    for f4 in range(4):
                        ins = v.scalar_tensor_tensor(
                            resid2[pb][:, f4 * 512:(f4 + 1) * 512],
                            mn_ps[:, f4 * 512:(f4 + 1) * 512], 1.0,
                            x1i[pb][:, f4 * 512:(f4 + 1) * 512],
                            ALU.mult, ALU.add)
                    mnfree[i] = vinc(ins)
                    if i + 1 < 8:
                        emit_ga(i + 1)
                        emit_gate_chain(i + 1)
                    if i >= 2:
                        v.wait_ge(s_y[i % 2].sem, 16 * (i // 2))
                    y_evt[i] = ln_tail(resid2[pb][:, :], scr2b[:, :],
                                       lnb2[pb], ysb[pb][:, :], mnfree[i],
                                       scr2b_free)
                    sy.wait_ge(vC, y_evt[i])
                    s_y[i % 2].dma(sy, Y[i * 128:(i + 1) * 128, :],
                                   ysb[pb][:, :])
                    if i + 2 < 8:
                        g.wait_ge(vC, mnfree[i])
                        load_x1(i + 2)

                s_y[0].wait(sy)
                s_y[1].wait(sy)

    return nc


_NC = None


def _get_nc():
    global _NC
    if _NC is None:
        _NC = _build_nc()
    return _NC


def _make_consts():
    ROT = 64
    inv_freq = (1.0 / 10000.0 ** (np.arange(0, ROT, 2) / ROT)).astype(np.float32)
    pos = np.arange(S, dtype=np.float32)
    freqs = pos[None, :] * inv_freq[:, None]
    cos = np.repeat(np.cos(freqs), 2, axis=0).astype(ml_dtypes.bfloat16)
    sin = np.repeat(np.sin(freqs), 2, axis=0).astype(ml_dtypes.bfloat16)
    R = np.zeros((64, 64), np.float32)
    for i in range(32):
        R[2 * i, 2 * i + 1] = -1.0
        R[2 * i + 1, 2 * i] = 1.0
    RT = np.ascontiguousarray(R.T).astype(ml_dtypes.bfloat16)
    id16 = np.eye(128, dtype=np.float32).astype(ml_dtypes.bfloat16)
    id32 = np.eye(128, dtype=np.float32)
    k = np.arange(128)[:, None]
    q = np.arange(QG)[None, :]
    masks = np.concatenate(
        [(k + 128 * d <= q).astype(np.float32) for d in range(4)],
        axis=1).astype(ml_dtypes.bfloat16)
    return cos, sin, RT, id16, id32, masks


def _make_in_maps(L0_attn_hidden_states, L0_attn_w_qkv, L0_attn_w_out,
                  L0_ffn_weight, L0_ffn_gate_w, L0_ffn_A_w, L0_ffn_B_w):
    X = np.ascontiguousarray(np.asarray(L0_attn_hidden_states, np.float32))
    Wqkv = np.asarray(L0_attn_w_qkv, np.float32)
    Wout = np.asarray(L0_attn_w_out, np.float32)
    Wffn = np.asarray(L0_ffn_weight, np.float32)
    Gw = np.asarray(L0_ffn_gate_w, np.float32)
    Aw = np.asarray(L0_ffn_A_w, np.float32)
    Bw = np.asarray(L0_ffn_B_w, np.float32)

    cos, sin, RT, id16, id32, masks = _make_consts()
    WOTn = np.ascontiguousarray(Wout.T)
    WFTn = np.ascontiguousarray(Wffn.T)
    GATn = np.ascontiguousarray(np.concatenate(
        [Gw.T, Aw.reshape(128, E).T], axis=1))
    BTDn = np.ascontiguousarray(Bw.transpose(0, 2, 1).reshape(128, E))

    in_maps = []
    for c in range(8):
        b, gq = c // 4, c % 4
        in_maps.append({
            "XT": np.ascontiguousarray(X[b].T),
            "XR": np.ascontiguousarray(X[b, gq * TOK:(gq + 1) * TOK]),
            "WQT": np.ascontiguousarray(Wqkv[gq * 1536: gq * 1536 + 512].T),
            "WVT": np.ascontiguousarray(
                Wqkv[gq * 1536 + 512: gq * 1536 + 1024].T),
            "WKT": np.ascontiguousarray(
                Wqkv[gq * 1536 + 1024: gq * 1536 + 1536].T),
            "WOT": WOTn, "WFT": WFTn, "GAT": GATn, "BTD": BTDn,
            "COS": cos, "SIN": sin, "RTD": RT, "ID16": id16, "ID32": id32,
            "MASKS": masks,
        })

    return in_maps


def kernel(**inputs):
    in_maps = _make_in_maps(**inputs)
    nc = _get_nc()
    res = run_bass_kernel_spmd(nc, in_maps, core_ids=list(range(8)))
    out = np.empty((B, S, E), np.float32)
    for c in range(8):
        b, gq = c // 4, c % 4
        out[b, gq * TOK:(gq + 1) * TOK] = res.results[c]["Y"]
    return out


# version 10 (final)
# speedup vs baseline: 1.9189x; 1.0472x over previous
import numpy as np
import ml_dtypes
from contextlib import ExitStack

import concourse.bass as bass
import concourse.mybir as mybir
from concourse.ap import AP
from concourse.bass_utils import run_bass_kernel_spmd

F32 = mybir.dt.float32
BF16 = mybir.dt.bfloat16
AFT = mybir.ActivationFunctionType
ALU = mybir.AluOpType
AX = mybir.AxisListType

_PHASES = "full"
_DETECT_RACES = False

B, S, E = 2, 4096, 2048
HG = 4
TOK = 1024
QG = 512
INV_SQRT_D = float(1.0 / np.sqrt(128.0))
LN_EPS = 1e-5


def _build_nc():
    nc = bass.Bass("TRN2", target_bir_lowering=False, debug=False,
                   detect_race_conditions=_DETECT_RACES)

    XT = nc.dram_tensor("XT", [E, S], F32, kind="ExternalInput").ap()
    XR = nc.dram_tensor("XR", [TOK, E], F32, kind="ExternalInput").ap()
    WQT = nc.dram_tensor("WQT", [E, 512], F32, kind="ExternalInput").ap()
    WKT = nc.dram_tensor("WKT", [E, 512], F32, kind="ExternalInput").ap()
    WVT = nc.dram_tensor("WVT", [E, 512], F32, kind="ExternalInput").ap()
    WOT = nc.dram_tensor("WOT", [E, E], F32, kind="ExternalInput").ap()
    WFT = nc.dram_tensor("WFT", [E, E], F32, kind="ExternalInput").ap()
    GAT = nc.dram_tensor("GAT", [E, 136], F32, kind="ExternalInput").ap()
    BTD = nc.dram_tensor("BTD", [128, E], F32, kind="ExternalInput").ap()
    COS = nc.dram_tensor("COS", [64, S], BF16, kind="ExternalInput").ap()
    SIN = nc.dram_tensor("SIN", [64, S], BF16, kind="ExternalInput").ap()
    RTD = nc.dram_tensor("RTD", [64, 64], BF16, kind="ExternalInput").ap()
    ID16 = nc.dram_tensor("ID16", [128, 128], BF16, kind="ExternalInput").ap()
    ID32 = nc.dram_tensor("ID32", [128, 128], F32, kind="ExternalInput").ap()
    MASKS = nc.dram_tensor("MASKS", [128, 4 * QG], BF16, kind="ExternalInput").ap()
    Y = nc.dram_tensor("Y", [TOK, E], F32, kind="ExternalOutput").ap()
    PH = _PHASES
    if PH == "A":
        QD = nc.dram_tensor("QD", [128, HG * S], BF16, kind="ExternalOutput").ap()
        KD = nc.dram_tensor("KD", [128, HG * S], BF16, kind="ExternalOutput").ap()
        VD = nc.dram_tensor("VD", [128, HG * 32 * 128], BF16, kind="ExternalOutput").ap()
    if PH == "B":
        AOD = nc.dram_tensor("AOD", [128, HG * S], BF16, kind="ExternalOutput").ap()
    if PH == "X":
        XOD = nc.dram_tensor("XOD", [E, S], BF16, kind="ExternalOutput").ap()
    if PH == "C1":
        X1O = nc.dram_tensor("X1O", [TOK, E], F32, kind="ExternalOutput").ap()
    if PH == "AG2":
        AGD = nc.dram_tensor("AGD", [128, 16 * 128], BF16, kind="ExternalOutput").ap()
    if PH == "OP":
        OPD = nc.dram_tensor("OPD", [128, E], F32, kind="ExternalOutput").ap()
    if PH == "L1":
        L1D = nc.dram_tensor("L1D", [128, E], F32, kind="ExternalOutput").ap()

    AGINH = [nc.dram_tensor(f"AGINH{h}", [128, S], BF16) for h in range(HG)]
    AGOUTH = [nc.dram_tensor(f"AGOUTH{h}", [4 * 128, S], BF16)
              for h in range(HG)]
    BARIN = nc.dram_tensor("BARIN", [1, 16], F32)
    BAROUT = nc.dram_tensor("BAROUT", [4, 16], F32)
    X1D = nc.dram_tensor("X1D", [TOK, E], F32)

    g = nc.gpsimd
    sy = nc.sync
    t = nc.tensor
    v = nc.vector
    sc = nc.scalar

    with ExitStack() as top:
        et = top.enter_context
        ones_col = et(nc.sbuf_tensor("ones_col", [128, 1], BF16))
        ones128 = et(nc.sbuf_tensor("ones128", [128, 128], BF16))
        ones_row = et(nc.sbuf_tensor("ones_row", [1, 128], BF16))
        id16 = et(nc.sbuf_tensor("id16", [128, 128], BF16))
        id32 = et(nc.sbuf_tensor("id32", [128, 128], F32))
        eps_b = et(nc.sbuf_tensor("eps_b", [128, 1], F32))
        zb = et(nc.sbuf_tensor("zb", [1, 16], F32))

        tC = et(nc.semaphore("tC"))
        aC = et(nc.semaphore("aC"))
        vC = et(nc.semaphore("vC"))
        cC = et(nc.semaphore("cC"))

        class Stream:

            def __init__(self, name):
                self.sem = et(nc.semaphore(name))
                self.v = 0

            def dma(self, eng, out, in_):
                eng.dma_start(out=out, in_=in_).then_inc(self.sem, 16)
                self.v += 16
                return self.v

            def wait(self, eng, val=None):
                eng.wait_ge(self.sem, self.v if val is None else val)

        g_xt = [Stream("g_xt0"), Stream("g_xt1")]
        g_w = Stream("g_w")
        g_wot = Stream("g_wot")
        g_wf = Stream("g_wf")
        g_x1 = [Stream("g_x10"), Stream("g_x11")]
        s_const = Stream("s_const")
        s_masks = Stream("s_masks")
        s_agin = Stream("s_agin")
        s_agh = [[Stream(f"s_ag{p}h{h}") for h in range(4)]
                 for p in range(2)]
        s_xr = [Stream("s_xr0"), Stream("s_xr1")]
        s_x1st = [Stream("s_x1st0"), Stream("s_x1st1")]
        s_y = [Stream("s_y0"), Stream("s_y1")]
        s_bar = Stream("s_bar")
        s_dump = Stream("s_dump")

        n = {"tC": 0, "aC": 0, "vC": 0}

        def vsync(ins):
            ev = vinc(ins)
            v.wait_ge(vC, ev)
            return ev

        def tinc(ins):
            ins.then_inc(tC, 1)
            n["tC"] += 1
            return n["tC"]

        def ainc(ins):
            ins.then_inc(aC, 1)
            n["aC"] += 1
            return n["aC"]

        def vinc(ins):
            ins.then_inc(vC, 1)
            n["vC"] += 1
            return n["vC"]

        v.memset(ones_col[:, :], 1.0)
        v.memset(ones128[:, :], 1.0)
        v.memset(ones_row[:, :], 1.0)
        v.memset(eps_b[:, :], LN_EPS)
        zb_evt = vinc(v.memset(zb[:, :], 0.0))
        s_const.dma(sy, id16[:, :], ID16)
        s_const.dma(sy, id32[:, :], ID32)

        with ExitStack() as abx:
            eab = abx.enter_context
            qs = eab(nc.sbuf_tensor("qs", [128, HG * S], BF16, side="right"))
            ks = eab(nc.sbuf_tensor("ks", [128, HG * S], BF16, side="right"))
            vs = eab(nc.sbuf_tensor("vs", [128, HG * 32 * 128], BF16, side="right"))

            with ExitStack() as ax:
                ea = ax.enter_context
                wq = ea(nc.sbuf_tensor("wq", [128, 16 * 512], BF16, side="right"))
                wk = ea(nc.sbuf_tensor("wk", [128, 16 * 512], BF16, side="right"))
                wv = ea(nc.sbuf_tensor("wv", [128, 16 * 512], BF16, side="right"))
                xtb = [ea(nc.sbuf_tensor(f"xtb{i}", [128, 16 * 512], BF16, side="right"))
                       for i in range(2)]
                cos = ea(nc.sbuf_tensor("cos", [64, S], BF16, side="right"))
                sin = ea(nc.sbuf_tensor("sin", [64, S], BF16, side="right"))
                rts = ea(nc.sbuf_tensor("rts", [64, 64], BF16, side="right"))
                rtmp1 = ea(nc.sbuf_tensor("rtmp1", [64, 512], BF16, side="right"))
                rtmp2 = ea(nc.sbuf_tensor("rtmp2", [64, 512], BF16, side="right"))
                qk_ps = [ea(nc.psum_tensor(f"qk_ps{i}", [128, 512], F32))
                         for i in range(2)]
                v_ps = [ea(nc.psum_tensor(f"v_ps{i}", [128, 512], F32))
                        for i in range(2)]
                rz_ps = [ea(nc.psum_tensor(f"rz_ps{i}", [64, 512], F32))
                         for i in range(2)]

                s_const.dma(sy, cos[:, :], COS)
                s_const.dma(sy, sin[:, :], SIN)
                s_const.dma(sy, rts[:, :], RTD)

                def load_xt(i):
                    return g_xt[i % 2].dma(
                        g,
                        xtb[i % 2][:, :].rearrange("p (ec t) -> p ec t", ec=16),
                        XT.rearrange("(ec p) t -> p ec t", p=128)[
                            :, :, i * 512:(i + 1) * 512],
                    )

                load_xt(0)
                g_w.dma(g, wq[:, :].rearrange("p (ec f) -> p ec f", ec=16),
                        WQT.rearrange("(ec p) f -> p ec f", p=128))
                g_w.dma(g, wk[:, :].rearrange("p (ec f) -> p ec f", ec=16),
                        WKT.rearrange("(ec p) f -> p ec f", p=128))
                g_w.dma(g, wv[:, :].rearrange("p (ec f) -> p ec f", ec=16),
                        WVT.rearrange("(ec p) f -> p ec f", p=128))
                load_xt(1)

                qk_free = [0, 0]
                v_free = [0, 0]
                tile_done_t = [0] * 8

                for it in range(8):
                    t0 = it * 512
                    t.wait_ge(g_xt[it % 2].sem, 16 * (it // 2 + 1))
                    if it == 0:
                        g_w.wait(t)
                    for fc in range(8):
                        w = wq if fc < 4 else wk
                        hl = fc % 4
                        ps = qk_ps[fc % 2]
                        if qk_free[fc % 2]:
                            t.wait_ge(vC, qk_free[fc % 2])
                        for ec in range(16):
                            ins = t.matmul(
                                ps[:, :],
                                w[:, ec * 512 + hl * 128: ec * 512 + hl * 128 + 128],
                                xtb[it % 2][:, ec * 512:(ec + 1) * 512],
                                start=(ec == 0), stop=(ec == 15),
                            )
                        evt = tinc(ins)
                        dst = qs if fc < 4 else ks
                        v.wait_ge(tC, evt)
                        qk_free[fc % 2] = vinc(v.tensor_copy(
                            dst[:, hl * S + t0: hl * S + t0 + 512], ps[:, :]))
                    for sub in range(4):
                        ps = v_ps[sub % 2]
                        if v_free[sub % 2]:
                            t.wait_ge(vC, v_free[sub % 2])
                        for ec in range(16):
                            ins = t.matmul(
                                ps[:, :],
                                xtb[it % 2][:, ec * 512 + sub * 128:
                                            ec * 512 + sub * 128 + 128],
                                wv[:, ec * 512:(ec + 1) * 512],
                                start=(ec == 0), stop=(ec == 15),
                            )
                        evt = tinc(ins)
                        kt = it * 4 + sub
                        v.wait_ge(tC, evt)
                        v_free[sub % 2] = vinc(v.tensor_copy(
                            vs[:, :].rearrange("p (h k d) -> p h k d",
                                               h=HG, k=32)[:, :, kt, :],
                            ps[:, :].rearrange("p (h d) -> p h d", h=HG)))
                    tile_done_t[it] = n["tC"]
                    if it + 2 < 8:
                        g.wait_ge(tC, tile_done_t[it])
                        load_xt(it + 2)

                qkv_evac_all = n["vC"]
                t.wait_ge(vC, qkv_evac_all)
                s_const.wait(v)
                s_const.wait(t)
                rz_free = [0, 0]
                ri = 0
                for zs in (qs, ks):
                    for h in range(HG):
                        for sl in range(8):
                            s0 = sl * 512
                            par = ri % 2
                            ri += 1
                            ps = rz_ps[par]
                            if rz_free[par]:
                                t.wait_ge(vC, rz_free[par])
                            ins = t.matmul(
                                ps[:, :], rts[:, :],
                                zs[0:64, h * S + s0: h * S + s0 + 512],
                                start=True, stop=True)
                            evt = tinc(ins)
                            v.wait_ge(tC, evt)
                            v.tensor_mul(rtmp1[:, :], ps[:, :],
                                         sin[:, s0:s0 + 512])
                            vsync(v.tensor_mul(
                                rtmp2[:, :],
                                zs[0:64, h * S + s0: h * S + s0 + 512],
                                cos[:, s0:s0 + 512]))
                            rz_free[par] = vinc(v.tensor_add(
                                zs[0:64, h * S + s0: h * S + s0 + 512],
                                rtmp1[:, :], rtmp2[:, :]))
                rope_done_v = n["vC"]
                rope_done_t = n["tC"]

            if PH == "A":
                sy.wait_ge(vC, rope_done_v)
                s_dump.dma(sy, QD, qs[:, :])
                s_dump.dma(sy, KD, ks[:, :])
                s_dump.dma(sy, VD, vs[:, :])
                s_dump.wait(sy)
                return nc

            wot = et(nc.sbuf_tensor("wot", [128, 16 * E], BF16))
            g.wait_ge(tC, rope_done_t)
            g.wait_ge(vC, rope_done_v)
            g_wot.dma(g, wot[:, :].rearrange("p (ec f) -> p ec f", ec=16),
                      WOT.rearrange("(ec p) f -> p ec f", p=128))

            with ExitStack() as bx:
                ebx = bx.enter_context
                ao = ebx(nc.sbuf_tensor("ao", [128, HG * S], BF16, side="right"))
                masks = ebx(nc.sbuf_tensor("masks", [128, 4 * QG], BF16, side="right"))
                pt = [ebx(nc.sbuf_tensor(f"pt{i}", [128, QG], BF16, side="right"))
                      for i in range(4)]
                nrec = [ebx(nc.sbuf_tensor(f"nrec{i}", [1, QG], BF16, side="right"))
                        for i in range(2)]
                bc_sb = [ebx(nc.sbuf_tensor(f"bc_sb{i}", [128, QG], BF16, side="right"))
                         for i in range(2)]
                s_ps = [ebx(nc.psum_tensor(f"s_ps{i}", [128, QG], F32))
                        for i in range(4)]
                o_ps = [ebx(nc.psum_tensor(f"o_ps{i}", [128, QG], F32))
                        for i in range(2)]
                b_ps = [ebx(nc.psum_tensor(f"b_ps{i}", [128, QG], F32))
                        for i in range(2)]

                sy.wait_ge(tC, rope_done_t)
                sy.wait_ge(vC, rope_done_v)
                s_masks.dma(sy, masks[:, :], MASKS)
                s_masks.wait(v)
                t.wait_ge(vC, rope_done_v)

                sy.wait_ge(vC, zb_evt)
                s_bar.dma(sy, BARIN.ap(), zb[:, :])

                pt_evt_a = [0, 0, 0, 0]
                pt_evt_v = [0, 0, 0, 0]
                pt_masked = [False] * 4
                pt_used = [0, 0, 0, 0]
                s_exped = [0, 0, 0, 0]
                on_free = [0, 0]
                rec_evt_ob = [0, 0]
                ktc = 0
                state = {"pending": None, "bc": None}

                def emit_pv():
                    if state["pending"] is None:
                        return
                    par, ob2, hh, kt2, first, last, q0b = state["pending"]
                    state["pending"] = None
                    if pt_masked[par]:
                        t.wait_ge(vC, pt_evt_v[par])
                    else:
                        t.wait_ge(aC, pt_evt_a[par])
                    t.matmul(o_ps[ob2][:, :],
                             vs[:, (hh * 32 + kt2) * 128:
                                (hh * 32 + kt2) * 128 + 128],
                             pt[par][:, :], start=first, stop=last,
                             skip_group_check=True)
                    ins2 = t.matmul(b_ps[ob2][:, :], ones128[:, :],
                                    pt[par][:, :], start=first, stop=last,
                                    skip_group_check=True)
                    pt_used[par] = tinc(ins2)
                    if last:
                        v.wait_ge(tC, pt_used[par])
                        with nc.allow_low_precision(
                                reason="softmax normalizer reciprocal kept in"
                                       " bf16; same precision class as the "
                                       "bf16 attention matmul path"):
                            rec_evt_ob[ob2] = vsync(v.reciprocal(
                                bc_sb[ob2][:, :], b_ps[ob2][:, :]))
                        state["bc"] = (ob2, hh, q0b)

                def emit_bcast():
                    if state["bc"] is None:
                        return
                    ob2, hh, q0b = state["bc"]
                    state["bc"] = None
                    on_free[ob2] = vinc(v.tensor_mul(
                        ao[:, hh * S + q0b: hh * S + q0b + QG],
                        o_ps[ob2][:, :], bc_sb[ob2][:, :]))
                    if q0b == 7 * QG:
                        sy.wait_ge(vC, on_free[ob2])
                        s_agin.dma(sy, AGINH[hh].ap(),
                                   ao[:, hh * S:(hh + 1) * S])
                        s_agin.wait(g)
                        g.collective_compute(
                            "AllGather", ALU.bypass,
                            replica_groups=[[0, 1, 2, 3], [4, 5, 6, 7]],
                            ins=[AGINH[hh].ap().opt()],
                            outs=[AGOUTH[hh].ap().opt()],
                        ).then_inc(cC)

                for gi in range(HG * 8):
                    h, qg = gi // 8, gi % 8
                    q0 = qg * QG
                    ob = gi % 2
                    if on_free[ob]:
                        t.wait_ge(vC, on_free[ob])
                    nkt = 4 * (qg + 1)
                    for kt in range(nkt):
                        par = ktc % 4
                        ktc += 1
                        if s_exped[par]:
                            t.wait_ge(aC, s_exped[par])
                        ins = t.matmul(
                            s_ps[par][:, :],
                            ks[:, h * S + kt * 128: h * S + kt * 128 + 128],
                            qs[:, h * S + q0: h * S + q0 + QG],
                            start=True, stop=True, skip_group_check=True)
                        s_evt = tinc(ins)
                        emit_pv()
                        if kt >= 1:
                            emit_bcast()
                        sc.wait_ge(tC, s_evt)
                        if pt_used[par]:
                            sc.wait_ge(tC, pt_used[par])
                        s_exped[par] = ainc(sc.activation(
                            pt[par][:, :], s_ps[par][:, :], AFT.Exp,
                            scale=INV_SQRT_D))
                        delta = kt - 4 * qg
                        if delta >= 0:
                            v.wait_ge(aC, s_exped[par])
                            pt_evt_v[par] = vinc(v.tensor_mul(
                                pt[par][:, :], pt[par][:, :],
                                masks[:, delta * QG:(delta + 1) * QG]))
                            pt_masked[par] = True
                        else:
                            pt_evt_a[par] = s_exped[par]
                            pt_masked[par] = False
                        state["pending"] = (par, ob, h, kt, kt == 0,
                                            kt == nkt - 1, q0)
                emit_pv()
                emit_bcast()
                ao_done_v = n["vC"]

                if PH == "B":
                    sy.wait_ge(vC, ao_done_v)
                    s_dump.dma(sy, AOD, ao[:, :])
                    s_dump.wait(sy)
                    return nc

                s_bar.wait(g)
                g.collective_compute(
                    "AllGather", ALU.bypass,
                    replica_groups=[[0, 1, 2, 3], [4, 5, 6, 7]],
                    ins=[BARIN.ap().opt()], outs=[BAROUT.ap().opt()],
                ).then_inc(cC)

        if PH == "X":
            sy.wait_ge(cC, 5)
            for hh in range(HG):
                s_dump.dma(sy, XOD.rearrange("(cg hh p) q -> cg hh p q",
                                             hh=4, p=128)[:, hh],
                           AGOUTH[hh].ap().rearrange("(cg p) q -> cg p q",
                                                     p=128))
            s_dump.wait(sy)
            return nc

        t0_gp = (g.partition_id() % 4) * TOK

        with ExitStack() as cx:
            ecx = cx.enter_context
            wft = ecx(nc.sbuf_tensor("wft", [128, 16 * E], BF16))
            gat = ecx(nc.sbuf_tensor("gat", [128, 16 * 136], BF16))
            btb = ecx(nc.sbuf_tensor("btb", [128, E], BF16))
            g_wf.dma(g, wft[:, :].rearrange("p (ec f) -> p ec f", ec=16),
                     WFT.rearrange("(ec p) f -> p ec f", p=128))
            g_wf.dma(g, gat[:, :].rearrange("p (ec f) -> p ec f", ec=16),
                     GAT.rearrange("(ec p) f -> p ec f", p=128))
            g_wf.dma(g, btb[:, :], BTD)

            def ln_tail(resid_ap, scr2_ap, lb, out_ap, resid_evt_v,
                        scr2_free):
                ve1 = vinc(v.reduce_sum(lb["mraw"][:, :], resid_ap, AX.X))
                sc.wait_ge(vC, resid_evt_v)
                if scr2_free[0]:
                    sc.wait_ge(vC, scr2_free[0])
                a1 = ainc(sc.activation(scr2_ap, resid_ap, AFT.Square))
                v.wait_ge(vC, ve1)
                v.tensor_scalar_mul(lb["m"][:, :], lb["mraw"][:, :], 1.0 / E)
                v.wait_ge(aC, a1)
                scr2_free[0] = vsync(v.reduce_sum(lb["s2"][:, :], scr2_ap,
                                                  AX.X))
                vsync(v.tensor_mul(lb["msq"][:, :], lb["m"][:, :],
                                   lb["m"][:, :]))
                ve4 = vinc(v.scalar_tensor_tensor(
                    lb["var"][:, :], lb["s2"][:, :], 1.0 / E,
                    lb["msq"][:, :], ALU.mult, ALU.subtract))
                sc.wait_ge(vC, ve4)
                aln = ainc(sc.activation(lb["lnv"][:, :], lb["var"][:, :],
                                         AFT.Ln, bias=eps_b[:, :]))
                sc.wait_ge(aC, aln)
                ae = ainc(sc.activation(lb["rstd"][:, :], lb["lnv"][:, :],
                                        AFT.Exp, scale=-0.5))
                v.wait_ge(aC, ae)
                vsync(v.tensor_scalar(lb["nmr"][:, :], lb["m"][:, :],
                                      lb["rstd"][:, :], -1.0, ALU.mult,
                                      ALU.mult))
                return vinc(v.tensor_scalar(out_ap, resid_ap,
                                            lb["rstd"][:, :], lb["nmr"][:, :],
                                            ALU.mult, ALU.add))

            with ExitStack() as c1x:
                e1 = c1x.enter_context
                agsb = [e1(nc.sbuf_tensor(f"agsb{i}", [128, 16 * 128], BF16))
                        for i in range(2)]
                xr = [e1(nc.sbuf_tensor(f"xr{i}", [128, E], F32))
                      for i in range(2)]
                resid = [e1(nc.sbuf_tensor(f"resid{i}", [128, E], F32))
                         for i in range(2)]
                x1o = [e1(nc.sbuf_tensor(f"x1o{i}", [128, E], F32))
                       for i in range(2)]
                scr2 = e1(nc.sbuf_tensor("scr2", [128, E], F32))
                scr2_free = [0]
                lnb = [{k: e1(nc.sbuf_tensor(f"ln_{k}{i}", [128, 1], F32))
                        for k in ("mraw", "m", "s2", "msq", "var", "lnv",
                                  "rstd", "nmr")} for i in range(2)]
                op_ps = [e1(nc.psum_tensor(f"op_ps{i}", [128, E], F32))
                         for i in range(2)]

                def load_agsb(i, heads=range(HG)):
                    for hh in heads:
                        g.wait_ge(cC, hh + 2)
                        base = AGOUTH[hh].ap().rearrange(
                            "(cg p) q -> p cg q", p=128)[
                            :, 0:4, i * 128:(i + 1) * 128]
                        dyn = AP(base.tensor, base.offset + t0_gp, base.ap)
                        s_agh[i % 2][hh].dma(
                            g,
                            agsb[i % 2][:, :].rearrange(
                                "p (cg hh c) -> p cg hh c", hh=4, c=128)[
                                :, :, hh], dyn)

                def load_xr(i):
                    return s_xr[i % 2].dma(sy, xr[i % 2][:, :],
                                           XR[i * 128:(i + 1) * 128, :])

                for hh in range(HG):
                    load_agsb(0, heads=(hh,))
                    load_agsb(1, heads=(hh,))
                if PH == "AG2":
                    for hh in range(4):
                        s_agh[0][hh].wait(sy)
                    s_dump.dma(sy, AGD, agsb[0][:, :])
                    s_dump.wait(sy)
                    return nc
                load_xr(0)
                load_xr(1)

                mm_done = [0] * 8
                resid_evt = [0] * 8
                opfree = [0, 0]
                x1_evt = [0] * 8
                g_wot.wait(t)
                for i in range(8):
                    pb = i % 2
                    if opfree[pb]:
                        t.wait_ge(vC, opfree[pb])
                    for hh in range(HG):
                        t.wait_ge(s_agh[pb][hh].sem, 16 * (i // 2 + 1))
                        for cg in range(4):
                            ec = cg * 4 + hh
                            for f4 in range(4):
                                ins = t.matmul(
                                    op_ps[pb][:, f4 * 512:(f4 + 1) * 512],
                                    agsb[pb][:, ec * 128:(ec + 1) * 128],
                                    wot[:, ec * E + f4 * 512:
                                        ec * E + f4 * 512 + 512],
                                    start=(hh == 0 and cg == 0),
                                    stop=(hh == 3 and cg == 3),
                                    skip_group_check=True,
                                )
                    mm_done[i] = tinc(ins)
                    if True:
                        v.wait_ge(tC, mm_done[i])
                        v.wait_ge(s_xr[i % 2].sem, 16 * (i // 2 + 1))
                        for f4 in range(4):
                            ins = v.scalar_tensor_tensor(
                                resid[pb][:, f4 * 512:(f4 + 1) * 512],
                                op_ps[pb][:, f4 * 512:(f4 + 1) * 512], 1.0,
                                xr[pb][:, f4 * 512:(f4 + 1) * 512],
                                ALU.mult, ALU.add)
                        opfree[pb] = vinc(ins)
                        resid_evt[i] = opfree[pb]
                        if PH == "OP" and i == 0:
                            sy.wait_ge(vC, resid_evt[0])
                            s_dump.dma(sy, OPD, resid[0][:, :])
                            s_dump.wait(sy)
                            return nc
                        if i >= 2:
                            v.wait_ge(s_x1st[i % 2].sem, 16 * (i // 2))
                        x1_evt[i] = ln_tail(resid[pb][:, :], scr2[:, :],
                                            lnb[pb], x1o[pb][:, :],
                                            resid_evt[i], scr2_free)
                        sy.wait_ge(vC, x1_evt[i])
                        if PH == "L1" and i == 0:
                            s_dump.dma(sy, L1D, x1o[0][:, :])
                            s_dump.wait(sy)
                            return nc
                        s_x1st[i % 2].dma(sy, X1D[i * 128:(i + 1) * 128, :],
                                          x1o[pb][:, :])
                        if i + 2 < 8:
                            g.wait_ge(tC, mm_done[i])
                            load_agsb(i + 2)
                            sy.wait_ge(vC, resid_evt[i])
                            load_xr(i + 2)
            if PH == "C1":
                s_x1st[0].wait(sy)
                s_x1st[1].wait(sy)
                s_dump.dma(sy, X1O, X1D.ap())
                s_dump.wait(sy)
                return nc

            with ExitStack() as c2x:
                e2 = c2x.enter_context
                x1i = [e2(nc.sbuf_tensor(f"x1i{i}", [128, E], F32))
                       for i in range(2)]
                x1t = [e2(nc.sbuf_tensor(f"x1t{i}", [128, 16 * 128], BF16))
                       for i in range(2)]
                ysb = [e2(nc.sbuf_tensor(f"ysb{i}", [128, E], F32))
                       for i in range(2)]
                resid2 = [e2(nc.sbuf_tensor(f"resid2{i}", [128, E], F32))
                          for i in range(2)]
                qe = e2(nc.sbuf_tensor("qe", [128, 8], F32))
                qlt = e2(nc.sbuf_tensor("qlt", [128, 8], F32))
                cw = e2(nc.sbuf_tensor("cw", [128, 8], F32))
                g1 = e2(nc.sbuf_tensor("g1", [128, 1], F32))
                g2 = e2(nc.sbuf_tensor("g2", [128, 1], F32))
                gneg = e2(nc.sbuf_tensor("gneg", [128, 1], F32))
                gss = e2(nc.sbuf_tensor("gss", [128, 1], F32))
                t_bf = [e2(nc.sbuf_tensor(f"t_bf{i}", [128, 128], BF16))
                        for i in range(2)]
                tt_sb = [e2(nc.sbuf_tensor(f"tt_sb{i}", [128, 128], BF16))
                         for i in range(2)]
                scr2b = e2(nc.sbuf_tensor("scr2b", [128, E], F32))
                scr2b_free = [0]
                lnb2 = [{k: e2(nc.sbuf_tensor(f"ln2_{k}{i}", [128, 1], F32))
                         for k in ("mraw", "m", "s2", "msq", "var", "lnv",
                                   "rstd", "nmr")} for i in range(2)]
                mn_ps = e2(nc.psum_tensor("mn_ps", [128, E], F32))
                ga_ps = e2(nc.psum_tensor("ga_ps", [128, 136], F32))
                tt_ps = e2(nc.psum_tensor("tt_ps", [128, 128], BF16))
                x1t_ps = [e2(nc.psum_tensor(f"x1t_ps{i}", [128, 512], F32))
                          for i in range(2)]

                def load_x1(i):
                    return g_x1[i % 2].dma(g, x1i[i % 2][:, :],
                                           X1D[i * 128:(i + 1) * 128, :])

                s_x1st[0].wait(g)
                s_x1st[1].wait(g)
                g.wait_ge(vC, x1_evt[7])
                g.wait_ge(tC, mm_done[7])
                load_x1(0)
                load_x1(1)

                g_wf.wait(t)
                s_const.wait(t)

                x1t_evt = [0] * 8
                main_done = [0] * 8
                ga_evt = [0] * 8
                gate_done = [0] * 8
                ttv = [0] * 8
                x1t_psfree = [0, 0]
                mnfree = [0] * 8
                y_evt = [0] * 8

                def emit_x1t(i):
                    pb2 = i % 2
                    t.wait_ge(g_x1[i % 2].sem, 16 * (i // 2 + 1))
                    for q4 in range(4):
                        qp = q4 % 2
                        if x1t_psfree[qp]:
                            t.wait_ge(vC, x1t_psfree[qp])
                        for j in range(4):
                            ins = t.transpose(
                                x1t_ps[qp][:, j * 128:(j + 1) * 128],
                                x1i[pb2][:, (q4 * 4 + j) * 128:
                                         (q4 * 4 + j) * 128 + 128],
                                id32[:, :])
                        evt = tinc(ins)
                        v.wait_ge(tC, evt)
                        if i >= 2:
                            v.wait_ge(tC, main_done[i - 2])
                        x1t_psfree[qp] = vinc(v.tensor_copy(
                            x1t[i % 2][:, q4 * 512:(q4 + 1) * 512],
                            x1t_ps[qp][:, :]))
                    x1t_evt[i] = n["vC"]

                def emit_ga(i):
                    t.wait_ge(vC, x1t_evt[i])
                    for ec in range(16):
                        ins = t.matmul(ga_ps[:, :],
                                       x1t[i % 2][:, ec * 128:(ec + 1) * 128],
                                       gat[:, ec * 136:(ec + 1) * 136],
                                       start=(ec == 0), stop=(ec == 15),
                                       skip_group_check=True)
                    ga_evt[i] = tinc(ins)

                def emit_gate_chain(i):
                    v.wait_ge(tC, ga_evt[i])
                    ve = vinc(v.tensor_reduce(gneg[:, :], ga_ps[:, 0:8],
                                              AX.X, ALU.max, negate=True))
                    sc.wait_ge(vC, ve)
                    ae = ainc(sc.activation(qe[:, :], ga_ps[:, 0:8], AFT.Exp,
                                            bias=gneg[:, :]))
                    v.wait_ge(aC, ae)
                    vsync(v.reduce_max(g1[:, :], qe[:, :], AX.X))
                    vsync(v.tensor_single_scalar(qlt[:, :], qe[:, :],
                                                 g1[:, :], ALU.is_lt))
                    vsync(v.tensor_mul(qlt[:, :], qe[:, :], qlt[:, :]))
                    vsync(v.reduce_max(g2[:, :], qlt[:, :], AX.X))
                    vsync(v.tensor_add(gss[:, :], g1[:, :], g2[:, :]))
                    vsync(v.reciprocal(gss[:, :], gss[:, :]))
                    vsync(v.tensor_scalar_mul(gss[:, :], gss[:, :], 16.0))
                    vsync(v.tensor_single_scalar(qlt[:, :], qe[:, :],
                                                 g2[:, :], ALU.is_ge))
                    vsync(v.scalar_tensor_tensor(cw[:, :], qe[:, :],
                                                 gss[:, :], qlt[:, :],
                                                 ALU.mult, ALU.mult))
                    for ex in range(8):
                        ins = v.tensor_scalar(
                            t_bf[i % 2][:, ex * 16:(ex + 1) * 16],
                            ga_ps[:, 8 + ex * 16: 8 + ex * 16 + 16],
                            cw[:, ex:ex + 1], None, ALU.mult)
                    gate_done[i] = vinc(ins)

                emit_x1t(0)
                emit_ga(0)
                emit_gate_chain(0)
                for i in range(8):
                    pb = i % 2
                    t.wait_ge(vC, x1t_evt[i])
                    if i >= 1 and mnfree[i - 1]:
                        t.wait_ge(vC, mnfree[i - 1])
                    for ec in range(16):
                        for f4 in range(4):
                            ins = t.matmul(
                                mn_ps[:, f4 * 512:(f4 + 1) * 512],
                                x1t[pb][:, ec * 128:(ec + 1) * 128],
                                wft[:, ec * E + f4 * 512:
                                     ec * E + f4 * 512 + 512],
                                start=(ec == 0), stop=False,
                                skip_group_check=True)
                    main_done[i] = tinc(ins)
                    t.wait_ge(vC, gate_done[i])
                    tte = tinc(t.transpose(tt_ps[:, :], t_bf[pb][:, :],
                                           id16[:, :]))
                    v.wait_ge(tC, tte)
                    ttv[i] = vinc(v.tensor_copy(tt_sb[pb][:, :], tt_ps[:, :]))
                    t.wait_ge(vC, ttv[i])
                    for f4 in range(4):
                        ins = t.matmul(mn_ps[:, f4 * 512:(f4 + 1) * 512],
                                       tt_sb[pb][:, :],
                                       btb[:, f4 * 512:(f4 + 1) * 512],
                                       start=False, stop=True,
                                       skip_group_check=True)
                    ffn_done = tinc(ins)
                    if i + 1 < 8:
                        emit_x1t(i + 1)
                    v.wait_ge(tC, ffn_done)
                    for f4 in range(4):
                        ins = v.scalar_tensor_tensor(
                            resid2[pb][:, f4 * 512:(f4 + 1) * 512],
                            mn_ps[:, f4 * 512:(f4 + 1) * 512], 1.0,
                            x1i[pb][:, f4 * 512:(f4 + 1) * 512],
                            ALU.mult, ALU.add)
                    mnfree[i] = vinc(ins)
                    if i + 1 < 8:
                        emit_ga(i + 1)
                        emit_gate_chain(i + 1)
                    if i >= 2:
                        v.wait_ge(s_y[i % 2].sem, 16 * (i // 2))
                    y_evt[i] = ln_tail(resid2[pb][:, :], scr2b[:, :],
                                       lnb2[pb], ysb[pb][:, :], mnfree[i],
                                       scr2b_free)
                    sy.wait_ge(vC, y_evt[i])
                    s_y[i % 2].dma(sy, Y[i * 128:(i + 1) * 128, :],
                                   ysb[pb][:, :])
                    if i + 2 < 8:
                        g.wait_ge(vC, mnfree[i])
                        load_x1(i + 2)

                s_y[0].wait(sy)
                s_y[1].wait(sy)

    return nc


_NC = None


def _get_nc():
    global _NC
    if _NC is None:
        _NC = _build_nc()
    return _NC


def _make_consts():
    ROT = 64
    inv_freq = (1.0 / 10000.0 ** (np.arange(0, ROT, 2) / ROT)).astype(np.float32)
    pos = np.arange(S, dtype=np.float32)
    freqs = pos[None, :] * inv_freq[:, None]
    cos = np.repeat(np.cos(freqs), 2, axis=0).astype(ml_dtypes.bfloat16)
    sin = np.repeat(np.sin(freqs), 2, axis=0).astype(ml_dtypes.bfloat16)
    R = np.zeros((64, 64), np.float32)
    for i in range(32):
        R[2 * i, 2 * i + 1] = -1.0
        R[2 * i + 1, 2 * i] = 1.0
    RT = np.ascontiguousarray(R.T).astype(ml_dtypes.bfloat16)
    id16 = np.eye(128, dtype=np.float32).astype(ml_dtypes.bfloat16)
    id32 = np.eye(128, dtype=np.float32)
    k = np.arange(128)[:, None]
    q = np.arange(QG)[None, :]
    masks = np.concatenate(
        [(k + 128 * d <= q).astype(np.float32) for d in range(4)],
        axis=1).astype(ml_dtypes.bfloat16)
    return cos, sin, RT, id16, id32, masks


def _make_in_maps(L0_attn_hidden_states, L0_attn_w_qkv, L0_attn_w_out,
                  L0_ffn_weight, L0_ffn_gate_w, L0_ffn_A_w, L0_ffn_B_w):
    X = np.ascontiguousarray(np.asarray(L0_attn_hidden_states, np.float32))
    Wqkv = np.asarray(L0_attn_w_qkv, np.float32)
    Wout = np.asarray(L0_attn_w_out, np.float32)
    Wffn = np.asarray(L0_ffn_weight, np.float32)
    Gw = np.asarray(L0_ffn_gate_w, np.float32)
    Aw = np.asarray(L0_ffn_A_w, np.float32)
    Bw = np.asarray(L0_ffn_B_w, np.float32)

    cos, sin, RT, id16, id32, masks = _make_consts()
    WOTn = np.ascontiguousarray(Wout.T)
    WFTn = np.ascontiguousarray(Wffn.T)
    GATn = np.ascontiguousarray(np.concatenate(
        [Gw.T, Aw.reshape(128, E).T], axis=1))
    BTDn = np.ascontiguousarray(Bw.transpose(0, 2, 1).reshape(128, E))

    in_maps = []
    for c in range(8):
        b, gq = c // 4, c % 4
        in_maps.append({
            "XT": np.ascontiguousarray(X[b].T),
            "XR": np.ascontiguousarray(X[b, gq * TOK:(gq + 1) * TOK]),
            "WQT": np.ascontiguousarray(Wqkv[gq * 1536: gq * 1536 + 512].T),
            "WVT": np.ascontiguousarray(
                Wqkv[gq * 1536 + 512: gq * 1536 + 1024].T),
            "WKT": np.ascontiguousarray(
                Wqkv[gq * 1536 + 1024: gq * 1536 + 1536].T),
            "WOT": WOTn, "WFT": WFTn, "GAT": GATn, "BTD": BTDn,
            "COS": cos, "SIN": sin, "RTD": RT, "ID16": id16, "ID32": id32,
            "MASKS": masks,
        })

    return in_maps


def kernel(**inputs):
    in_maps = _make_in_maps(**inputs)
    nc = _get_nc()
    res = run_bass_kernel_spmd(nc, in_maps, core_ids=list(range(8)))
    out = np.empty((B, S, E), np.float32)
    for c in range(8):
        b, gq = c // 4, c % 4
        out[b, gq * TOK:(gq + 1) * TOK] = res.results[c]["Y"]
    return out


# version 11
# speedup vs baseline: 1.9512x; 1.0169x over previous
import numpy as np
import ml_dtypes
from contextlib import ExitStack

import concourse.bass as bass
import concourse.mybir as mybir
from concourse.ap import AP
from concourse.bass_utils import run_bass_kernel_spmd

F32 = mybir.dt.float32
BF16 = mybir.dt.bfloat16
AFT = mybir.ActivationFunctionType
ALU = mybir.AluOpType
AX = mybir.AxisListType

_PHASES = "full"
_DETECT_RACES = False

B, S, E = 2, 4096, 2048
HG = 4
TOK = 1024
QG = 512
INV_SQRT_D = float(1.0 / np.sqrt(128.0))
LN_EPS = 1e-5


def _build_nc():
    nc = bass.Bass("TRN2", target_bir_lowering=False, debug=False,
                   detect_race_conditions=_DETECT_RACES)

    XT = nc.dram_tensor("XT", [E, S], F32, kind="ExternalInput").ap()
    XR = nc.dram_tensor("XR", [TOK, E], F32, kind="ExternalInput").ap()
    WQT = nc.dram_tensor("WQT", [E, 512], F32, kind="ExternalInput").ap()
    WKT = nc.dram_tensor("WKT", [E, 512], F32, kind="ExternalInput").ap()
    WVT = nc.dram_tensor("WVT", [E, 512], F32, kind="ExternalInput").ap()
    WOT = nc.dram_tensor("WOT", [E, E], F32, kind="ExternalInput").ap()
    WFT = nc.dram_tensor("WFT", [E, E], F32, kind="ExternalInput").ap()
    GAT = nc.dram_tensor("GAT", [E, 136], F32, kind="ExternalInput").ap()
    BTD = nc.dram_tensor("BTD", [128, E], F32, kind="ExternalInput").ap()
    COS = nc.dram_tensor("COS", [64, S], BF16, kind="ExternalInput").ap()
    SIN = nc.dram_tensor("SIN", [64, S], BF16, kind="ExternalInput").ap()
    RTD = nc.dram_tensor("RTD", [64, 64], BF16, kind="ExternalInput").ap()
    ID16 = nc.dram_tensor("ID16", [128, 128], BF16, kind="ExternalInput").ap()
    ID32 = nc.dram_tensor("ID32", [128, 128], F32, kind="ExternalInput").ap()
    MASKS = nc.dram_tensor("MASKS", [128, 4 * QG], BF16, kind="ExternalInput").ap()
    Y = nc.dram_tensor("Y", [TOK, E], F32, kind="ExternalOutput").ap()
    PH = _PHASES
    if PH == "A":
        QD = nc.dram_tensor("QD", [128, HG * S], BF16, kind="ExternalOutput").ap()
        KD = nc.dram_tensor("KD", [128, HG * S], BF16, kind="ExternalOutput").ap()
        VD = nc.dram_tensor("VD", [128, HG * 32 * 128], BF16, kind="ExternalOutput").ap()
    if PH == "B":
        AOD = nc.dram_tensor("AOD", [128, HG * S], BF16, kind="ExternalOutput").ap()
    if PH == "X":
        XOD = nc.dram_tensor("XOD", [E, S], BF16, kind="ExternalOutput").ap()
    if PH == "C1":
        X1O = nc.dram_tensor("X1O", [TOK, E], F32, kind="ExternalOutput").ap()
    if PH == "AG2":
        AGD = nc.dram_tensor("AGD", [128, 16 * 128], BF16, kind="ExternalOutput").ap()
    if PH == "OP":
        OPD = nc.dram_tensor("OPD", [128, E], F32, kind="ExternalOutput").ap()
    if PH == "L1":
        L1D = nc.dram_tensor("L1D", [128, E], F32, kind="ExternalOutput").ap()

    AGINH = [nc.dram_tensor(f"AGINH{h}", [128, S], BF16) for h in range(HG)]
    AGOUTH = [nc.dram_tensor(f"AGOUTH{h}", [4 * 128, S], BF16)
              for h in range(HG)]
    BARIN = nc.dram_tensor("BARIN", [1, 16], F32)
    BAROUT = nc.dram_tensor("BAROUT", [4, 16], F32)
    X1D = nc.dram_tensor("X1D", [TOK, E], F32)

    g = nc.gpsimd
    sy = nc.sync
    t = nc.tensor
    v = nc.vector
    sc = nc.scalar

    with ExitStack() as top:
        et = top.enter_context
        ones_col = et(nc.sbuf_tensor("ones_col", [128, 1], BF16))
        ones128 = et(nc.sbuf_tensor("ones128", [128, 128], BF16))
        ones_row = et(nc.sbuf_tensor("ones_row", [1, 128], BF16))
        id16 = et(nc.sbuf_tensor("id16", [128, 128], BF16))
        id32 = et(nc.sbuf_tensor("id32", [128, 128], F32))
        eps_b = et(nc.sbuf_tensor("eps_b", [128, 1], F32))
        zb = et(nc.sbuf_tensor("zb", [1, 16], F32))

        tC = et(nc.semaphore("tC"))
        aC = et(nc.semaphore("aC"))
        vC = et(nc.semaphore("vC"))
        cC = et(nc.semaphore("cC"))

        class Stream:

            def __init__(self, name):
                self.sem = et(nc.semaphore(name))
                self.v = 0

            def dma(self, eng, out, in_):
                eng.dma_start(out=out, in_=in_).then_inc(self.sem, 16)
                self.v += 16
                return self.v

            def wait(self, eng, val=None):
                eng.wait_ge(self.sem, self.v if val is None else val)

        g_xt = [Stream("g_xt0"), Stream("g_xt1")]
        g_w = Stream("g_w")
        g_wot = Stream("g_wot")
        g_wf = Stream("g_wf")
        g_x1 = [Stream("g_x10"), Stream("g_x11")]
        s_const = Stream("s_const")
        s_masks = Stream("s_masks")
        s_agin = Stream("s_agin")
        s_agh = [[Stream(f"s_ag{p}h{h}") for h in range(4)]
                 for p in range(2)]
        s_xr = [Stream("s_xr0"), Stream("s_xr1")]
        s_x1st = [Stream("s_x1st0"), Stream("s_x1st1")]
        s_y = [Stream("s_y0"), Stream("s_y1")]
        s_bar = Stream("s_bar")
        s_dump = Stream("s_dump")

        n = {"tC": 0, "aC": 0, "vC": 0}

        def vsync(ins):
            ev = vinc(ins)
            v.wait_ge(vC, ev)
            return ev

        def tinc(ins):
            ins.then_inc(tC, 1)
            n["tC"] += 1
            return n["tC"]

        def ainc(ins):
            ins.then_inc(aC, 1)
            n["aC"] += 1
            return n["aC"]

        def vinc(ins):
            ins.then_inc(vC, 1)
            n["vC"] += 1
            return n["vC"]

        v.memset(ones_col[:, :], 1.0)
        v.memset(ones128[:, :], 1.0)
        v.memset(ones_row[:, :], 1.0)
        v.memset(eps_b[:, :], LN_EPS)
        zb_evt = vinc(v.memset(zb[:, :], 0.0))
        s_const.dma(sy, id16[:, :], ID16)
        s_const.dma(sy, id32[:, :], ID32)

        with ExitStack() as abx:
            eab = abx.enter_context
            qs = eab(nc.sbuf_tensor("qs", [128, HG * S], BF16, side="right"))
            ks = eab(nc.sbuf_tensor("ks", [128, HG * S], BF16, side="right"))
            vs = eab(nc.sbuf_tensor("vs", [128, HG * 32 * 128], BF16, side="right"))

            with ExitStack() as ax:
                ea = ax.enter_context
                wq = ea(nc.sbuf_tensor("wq", [128, 16 * 512], BF16, side="right"))
                wk = ea(nc.sbuf_tensor("wk", [128, 16 * 512], BF16, side="right"))
                wv = ea(nc.sbuf_tensor("wv", [128, 16 * 512], BF16, side="right"))
                xtb = [ea(nc.sbuf_tensor(f"xtb{i}", [128, 16 * 512], BF16, side="right"))
                       for i in range(2)]
                cos = ea(nc.sbuf_tensor("cos", [64, S], BF16, side="right"))
                sin = ea(nc.sbuf_tensor("sin", [64, S], BF16, side="right"))
                rts = ea(nc.sbuf_tensor("rts", [64, 64], BF16, side="right"))
                rtmp1 = ea(nc.sbuf_tensor("rtmp1", [64, 512], BF16, side="right"))
                rtmp2 = ea(nc.sbuf_tensor("rtmp2", [64, 512], BF16, side="right"))
                qk_ps = [ea(nc.psum_tensor(f"qk_ps{i}", [128, 512], F32))
                         for i in range(2)]
                v_ps = [ea(nc.psum_tensor(f"v_ps{i}", [128, 512], F32))
                        for i in range(2)]
                rz_ps = [ea(nc.psum_tensor(f"rz_ps{i}", [64, 512], F32))
                         for i in range(2)]

                s_const.dma(sy, cos[:, :], COS)
                s_const.dma(sy, sin[:, :], SIN)
                s_const.dma(sy, rts[:, :], RTD)

                def load_xt(i):
                    return g_xt[i % 2].dma(
                        g,
                        xtb[i % 2][:, :].rearrange("p (ec t) -> p ec t", ec=16),
                        XT.rearrange("(ec p) t -> p ec t", p=128)[
                            :, :, i * 512:(i + 1) * 512],
                    )

                load_xt(0)
                g_w.dma(g, wq[:, :].rearrange("p (ec f) -> p ec f", ec=16),
                        WQT.rearrange("(ec p) f -> p ec f", p=128))
                g_w.dma(g, wk[:, :].rearrange("p (ec f) -> p ec f", ec=16),
                        WKT.rearrange("(ec p) f -> p ec f", p=128))
                g_w.dma(g, wv[:, :].rearrange("p (ec f) -> p ec f", ec=16),
                        WVT.rearrange("(ec p) f -> p ec f", p=128))
                load_xt(1)

                qk_free = [0, 0]
                v_free = [0, 0]
                tile_done_t = [0] * 8

                for it in range(8):
                    t0 = it * 512
                    t.wait_ge(g_xt[it % 2].sem, 16 * (it // 2 + 1))
                    if it == 0:
                        g_w.wait(t)
                    for fc in range(8):
                        w = wq if fc < 4 else wk
                        hl = fc % 4
                        ps = qk_ps[fc % 2]
                        if qk_free[fc % 2]:
                            t.wait_ge(vC, qk_free[fc % 2])
                        for ec in range(16):
                            ins = t.matmul(
                                ps[:, :],
                                w[:, ec * 512 + hl * 128: ec * 512 + hl * 128 + 128],
                                xtb[it % 2][:, ec * 512:(ec + 1) * 512],
                                start=(ec == 0), stop=(ec == 15),
                            )
                        evt = tinc(ins)
                        dst = qs if fc < 4 else ks
                        v.wait_ge(tC, evt)
                        qk_free[fc % 2] = vinc(v.tensor_copy(
                            dst[:, hl * S + t0: hl * S + t0 + 512], ps[:, :]))
                    for sub in range(4):
                        ps = v_ps[sub % 2]
                        if v_free[sub % 2]:
                            t.wait_ge(vC, v_free[sub % 2])
                        for ec in range(16):
                            ins = t.matmul(
                                ps[:, :],
                                xtb[it % 2][:, ec * 512 + sub * 128:
                                            ec * 512 + sub * 128 + 128],
                                wv[:, ec * 512:(ec + 1) * 512],
                                start=(ec == 0), stop=(ec == 15),
                            )
                        evt = tinc(ins)
                        kt = it * 4 + sub
                        v.wait_ge(tC, evt)
                        v_free[sub % 2] = vinc(v.tensor_copy(
                            vs[:, :].rearrange("p (h k d) -> p h k d",
                                               h=HG, k=32)[:, :, kt, :],
                            ps[:, :].rearrange("p (h d) -> p h d", h=HG)))
                    tile_done_t[it] = n["tC"]
                    if it + 2 < 8:
                        g.wait_ge(tC, tile_done_t[it])
                        load_xt(it + 2)

                qkv_evac_all = n["vC"]
                t.wait_ge(vC, qkv_evac_all)
                s_const.wait(v)
                s_const.wait(t)
                rz_free = [0, 0]
                ri = 0
                for zs in (qs, ks):
                    for h in range(HG):
                        for sl in range(8):
                            s0 = sl * 512
                            par = ri % 2
                            ri += 1
                            ps = rz_ps[par]
                            if rz_free[par]:
                                t.wait_ge(vC, rz_free[par])
                            ins = t.matmul(
                                ps[:, :], rts[:, :],
                                zs[0:64, h * S + s0: h * S + s0 + 512],
                                start=True, stop=True)
                            evt = tinc(ins)
                            v.wait_ge(tC, evt)
                            v.tensor_mul(rtmp1[:, :], ps[:, :],
                                         sin[:, s0:s0 + 512])
                            vsync(v.tensor_mul(
                                rtmp2[:, :],
                                zs[0:64, h * S + s0: h * S + s0 + 512],
                                cos[:, s0:s0 + 512]))
                            rz_free[par] = vinc(v.tensor_add(
                                zs[0:64, h * S + s0: h * S + s0 + 512],
                                rtmp1[:, :], rtmp2[:, :]))
                rope_done_v = n["vC"]
                rope_done_t = n["tC"]

            if PH == "A":
                sy.wait_ge(vC, rope_done_v)
                s_dump.dma(sy, QD, qs[:, :])
                s_dump.dma(sy, KD, ks[:, :])
                s_dump.dma(sy, VD, vs[:, :])
                s_dump.wait(sy)
                return nc

            wot = et(nc.sbuf_tensor("wot", [128, 16 * E], BF16))
            g.wait_ge(tC, rope_done_t)
            g.wait_ge(vC, rope_done_v)
            g_wot.dma(g, wot[:, :].rearrange("p (ec f) -> p ec f", ec=16),
                      WOT.rearrange("(ec p) f -> p ec f", p=128))

            with ExitStack() as bx:
                ebx = bx.enter_context
                ao = ebx(nc.sbuf_tensor("ao", [128, HG * S], BF16, side="right"))
                masks = ebx(nc.sbuf_tensor("masks", [128, 4 * QG], BF16, side="right"))
                pt = [ebx(nc.sbuf_tensor(f"pt{i}", [128, QG], BF16, side="right"))
                      for i in range(4)]
                nrec = [ebx(nc.sbuf_tensor(f"nrec{i}", [1, QG], BF16, side="right"))
                        for i in range(2)]
                bc_sb = [ebx(nc.sbuf_tensor(f"bc_sb{i}", [128, QG], BF16, side="right"))
                         for i in range(2)]
                s_ps = [ebx(nc.psum_tensor(f"s_ps{i}", [128, QG], F32))
                        for i in range(4)]
                o_ps = [ebx(nc.psum_tensor(f"o_ps{i}", [128, QG], F32))
                        for i in range(2)]
                b_ps = [ebx(nc.psum_tensor(f"b_ps{i}", [128, QG], F32))
                        for i in range(2)]

                sy.wait_ge(tC, rope_done_t)
                sy.wait_ge(vC, rope_done_v)
                s_masks.dma(sy, masks[:, :], MASKS)
                s_masks.wait(v)
                t.wait_ge(vC, rope_done_v)

                sy.wait_ge(vC, zb_evt)
                s_bar.dma(sy, BARIN.ap(), zb[:, :])

                pt_evt_a = [0, 0, 0, 0]
                pt_evt_v = [0, 0, 0, 0]
                pt_masked = [False] * 4
                pt_used = [0, 0, 0, 0]
                s_exped = [0, 0, 0, 0]
                on_free = [0, 0]
                rec_evt_ob = [0, 0]
                ktc = 0
                state = {"pending": None, "bc": None}

                def emit_pv():
                    if state["pending"] is None:
                        return
                    items = state["pending"]
                    state["pending"] = None
                    wa = 0
                    wv = 0
                    for par, *_ in items:
                        if pt_masked[par]:
                            wv = max(wv, pt_evt_v[par])
                        else:
                            wa = max(wa, pt_evt_a[par])
                    if wa:
                        t.wait_ge(aC, wa)
                    if wv:
                        t.wait_ge(vC, wv)
                    for par, ob2, hh, kt2, first, last, q0b in items:
                        t.matmul(o_ps[ob2][:, :],
                                 vs[:, (hh * 32 + kt2) * 128:
                                    (hh * 32 + kt2) * 128 + 128],
                                 pt[par][:, :], start=first, stop=last,
                                 skip_group_check=True)
                        ins2 = t.matmul(b_ps[ob2][:, :], ones128[:, :],
                                        pt[par][:, :], start=first, stop=last,
                                        skip_group_check=True)
                    used = tinc(ins2)
                    for par, *_ in items:
                        pt_used[par] = used
                    par, ob2, hh, kt2, first, last, q0b = items[-1]
                    if last:
                        v.wait_ge(tC, used)
                        with nc.allow_low_precision(
                                reason="softmax normalizer reciprocal kept in"
                                       " bf16; same precision class as the "
                                       "bf16 attention matmul path"):
                            rec_evt_ob[ob2] = vsync(v.reciprocal(
                                bc_sb[ob2][:, :], b_ps[ob2][:, :]))
                        state["bc"] = (ob2, hh, q0b)

                def emit_bcast():
                    if state["bc"] is None:
                        return
                    ob2, hh, q0b = state["bc"]
                    state["bc"] = None
                    on_free[ob2] = vinc(v.tensor_mul(
                        ao[:, hh * S + q0b: hh * S + q0b + QG],
                        o_ps[ob2][:, :], bc_sb[ob2][:, :]))
                    if q0b == 7 * QG:
                        sy.wait_ge(vC, on_free[ob2])
                        s_agin.dma(sy, AGINH[hh].ap(),
                                   ao[:, hh * S:(hh + 1) * S])
                        s_agin.wait(g)
                        g.collective_compute(
                            "AllGather", ALU.bypass,
                            replica_groups=[[0, 1, 2, 3], [4, 5, 6, 7]],
                            ins=[AGINH[hh].ap().opt()],
                            outs=[AGOUTH[hh].ap().opt()],
                        ).then_inc(cC)

                for gi in range(HG * 8):
                    h, qg = gi // 8, gi % 8
                    q0 = qg * QG
                    ob = gi % 2
                    if on_free[ob]:
                        t.wait_ge(vC, on_free[ob])
                    nkt = 4 * (qg + 1)
                    for kp in range(nkt // 2):
                        kt0, kt1 = 2 * kp, 2 * kp + 1
                        par0 = ktc % 4
                        par1 = (ktc + 1) % 4
                        ktc += 2
                        bp = max(s_exped[par0], s_exped[par1])
                        if bp:
                            t.wait_ge(aC, bp)
                        for kt, par in ((kt0, par0), (kt1, par1)):
                            ins = t.matmul(
                                s_ps[par][:, :],
                                ks[:, h * S + kt * 128:
                                   h * S + kt * 128 + 128],
                                qs[:, h * S + q0: h * S + q0 + QG],
                                start=True, stop=True, skip_group_check=True)
                        s_evt = tinc(ins)
                        emit_pv()
                        if kp >= 1:
                            emit_bcast()
                        sc.wait_ge(tC, s_evt)
                        up = max(pt_used[par0], pt_used[par1])
                        if up:
                            sc.wait_ge(tC, up)
                        for kt, par in ((kt0, par0), (kt1, par1)):
                            ins = sc.activation(
                                pt[par][:, :], s_ps[par][:, :], AFT.Exp,
                                scale=INV_SQRT_D)
                        ae2 = ainc(ins)
                        s_exped[par0] = ae2
                        s_exped[par1] = ae2
                        items = []
                        for kt, par in ((kt0, par0), (kt1, par1)):
                            delta = kt - 4 * qg
                            if delta >= 0:
                                v.wait_ge(aC, ae2)
                                pt_evt_v[par] = vinc(v.tensor_mul(
                                    pt[par][:, :], pt[par][:, :],
                                    masks[:, delta * QG:(delta + 1) * QG]))
                                pt_masked[par] = True
                            else:
                                pt_evt_a[par] = ae2
                                pt_masked[par] = False
                            items.append((par, ob, h, kt, kt == 0,
                                          kt == nkt - 1, q0))
                        state["pending"] = items
                emit_pv()
                emit_bcast()
                ao_done_v = n["vC"]

                if PH == "B":
                    sy.wait_ge(vC, ao_done_v)
                    s_dump.dma(sy, AOD, ao[:, :])
                    s_dump.wait(sy)
                    return nc

                s_bar.wait(g)
                g.collective_compute(
                    "AllGather", ALU.bypass,
                    replica_groups=[[0, 1, 2, 3], [4, 5, 6, 7]],
                    ins=[BARIN.ap().opt()], outs=[BAROUT.ap().opt()],
                ).then_inc(cC)

        if PH == "X":
            sy.wait_ge(cC, 5)
            for hh in range(HG):
                s_dump.dma(sy, XOD.rearrange("(cg hh p) q -> cg hh p q",
                                             hh=4, p=128)[:, hh],
                           AGOUTH[hh].ap().rearrange("(cg p) q -> cg p q",
                                                     p=128))
            s_dump.wait(sy)
            return nc

        t0_gp = (g.partition_id() % 4) * TOK

        with ExitStack() as cx:
            ecx = cx.enter_context
            wft = ecx(nc.sbuf_tensor("wft", [128, 16 * E], BF16))
            gat = ecx(nc.sbuf_tensor("gat", [128, 16 * 136], BF16))
            btb = ecx(nc.sbuf_tensor("btb", [128, E], BF16))
            g_wf.dma(g, wft[:, :].rearrange("p (ec f) -> p ec f", ec=16),
                     WFT.rearrange("(ec p) f -> p ec f", p=128))
            g_wf.dma(g, gat[:, :].rearrange("p (ec f) -> p ec f", ec=16),
                     GAT.rearrange("(ec p) f -> p ec f", p=128))
            g_wf.dma(g, btb[:, :], BTD)

            def ln_tail(resid_ap, scr2_ap, lb, out_ap, resid_evt_v,
                        scr2_free):
                ve1 = vinc(v.reduce_sum(lb["mraw"][:, :], resid_ap, AX.X))
                sc.wait_ge(vC, resid_evt_v)
                if scr2_free[0]:
                    sc.wait_ge(vC, scr2_free[0])
                a1 = ainc(sc.activation(scr2_ap, resid_ap, AFT.Square))
                v.wait_ge(vC, ve1)
                v.tensor_scalar_mul(lb["m"][:, :], lb["mraw"][:, :], 1.0 / E)
                v.wait_ge(aC, a1)
                scr2_free[0] = vsync(v.reduce_sum(lb["s2"][:, :], scr2_ap,
                                                  AX.X))
                vsync(v.tensor_mul(lb["msq"][:, :], lb["m"][:, :],
                                   lb["m"][:, :]))
                ve4 = vinc(v.scalar_tensor_tensor(
                    lb["var"][:, :], lb["s2"][:, :], 1.0 / E,
                    lb["msq"][:, :], ALU.mult, ALU.subtract))
                sc.wait_ge(vC, ve4)
                aln = ainc(sc.activation(lb["lnv"][:, :], lb["var"][:, :],
                                         AFT.Ln, bias=eps_b[:, :]))
                sc.wait_ge(aC, aln)
                ae = ainc(sc.activation(lb["rstd"][:, :], lb["lnv"][:, :],
                                        AFT.Exp, scale=-0.5))
                v.wait_ge(aC, ae)
                vsync(v.tensor_scalar(lb["nmr"][:, :], lb["m"][:, :],
                                      lb["rstd"][:, :], -1.0, ALU.mult,
                                      ALU.mult))
                return vinc(v.tensor_scalar(out_ap, resid_ap,
                                            lb["rstd"][:, :], lb["nmr"][:, :],
                                            ALU.mult, ALU.add))

            with ExitStack() as c1x:
                e1 = c1x.enter_context
                agsb = [e1(nc.sbuf_tensor(f"agsb{i}", [128, 16 * 128], BF16))
                        for i in range(2)]
                xr = [e1(nc.sbuf_tensor(f"xr{i}", [128, E], F32))
                      for i in range(2)]
                resid = [e1(nc.sbuf_tensor(f"resid{i}", [128, E], F32))
                         for i in range(2)]
                x1o = [e1(nc.sbuf_tensor(f"x1o{i}", [128, E], F32))
                       for i in range(2)]
                scr2 = e1(nc.sbuf_tensor("scr2", [128, E], F32))
                scr2_free = [0]
                lnb = [{k: e1(nc.sbuf_tensor(f"ln_{k}{i}", [128, 1], F32))
                        for k in ("mraw", "m", "s2", "msq", "var", "lnv",
                                  "rstd", "nmr")} for i in range(2)]
                op_ps = [e1(nc.psum_tensor(f"op_ps{i}", [128, E], F32))
                         for i in range(2)]

                def load_agsb(i, heads=range(HG)):
                    for hh in heads:
                        g.wait_ge(cC, hh + 2)
                        base = AGOUTH[hh].ap().rearrange(
                            "(cg p) q -> p cg q", p=128)[
                            :, 0:4, i * 128:(i + 1) * 128]
                        dyn = AP(base.tensor, base.offset + t0_gp, base.ap)
                        s_agh[i % 2][hh].dma(
                            g,
                            agsb[i % 2][:, :].rearrange(
                                "p (cg hh c) -> p cg hh c", hh=4, c=128)[
                                :, :, hh], dyn)

                def load_xr(i):
                    return s_xr[i % 2].dma(sy, xr[i % 2][:, :],
                                           XR[i * 128:(i + 1) * 128, :])

                for hh in range(HG):
                    load_agsb(0, heads=(hh,))
                    load_agsb(1, heads=(hh,))
                if PH == "AG2":
                    for hh in range(4):
                        s_agh[0][hh].wait(sy)
                    s_dump.dma(sy, AGD, agsb[0][:, :])
                    s_dump.wait(sy)
                    return nc
                load_xr(0)
                load_xr(1)

                mm_done = [0] * 8
                resid_evt = [0] * 8
                opfree = [0, 0]
                x1_evt = [0] * 8
                g_wot.wait(t)
                for i in range(8):
                    pb = i % 2
                    if opfree[pb]:
                        t.wait_ge(vC, opfree[pb])
                    for hh in range(HG):
                        t.wait_ge(s_agh[pb][hh].sem, 16 * (i // 2 + 1))
                        for cg in range(4):
                            ec = cg * 4 + hh
                            for f4 in range(4):
                                ins = t.matmul(
                                    op_ps[pb][:, f4 * 512:(f4 + 1) * 512],
                                    agsb[pb][:, ec * 128:(ec + 1) * 128],
                                    wot[:, ec * E + f4 * 512:
                                        ec * E + f4 * 512 + 512],
                                    start=(hh == 0 and cg == 0),
                                    stop=(hh == 3 and cg == 3),
                                    skip_group_check=True,
                                )
                    mm_done[i] = tinc(ins)
                    if True:
                        v.wait_ge(tC, mm_done[i])
                        v.wait_ge(s_xr[i % 2].sem, 16 * (i // 2 + 1))
                        for f4 in range(4):
                            ins = v.scalar_tensor_tensor(
                                resid[pb][:, f4 * 512:(f4 + 1) * 512],
                                op_ps[pb][:, f4 * 512:(f4 + 1) * 512], 1.0,
                                xr[pb][:, f4 * 512:(f4 + 1) * 512],
                                ALU.mult, ALU.add)
                        opfree[pb] = vinc(ins)
                        resid_evt[i] = opfree[pb]
                        if PH == "OP" and i == 0:
                            sy.wait_ge(vC, resid_evt[0])
                            s_dump.dma(sy, OPD, resid[0][:, :])
                            s_dump.wait(sy)
                            return nc
                        if i >= 2:
                            v.wait_ge(s_x1st[i % 2].sem, 16 * (i // 2))
                        x1_evt[i] = ln_tail(resid[pb][:, :], scr2[:, :],
                                            lnb[pb], x1o[pb][:, :],
                                            resid_evt[i], scr2_free)
                        sy.wait_ge(vC, x1_evt[i])
                        if PH == "L1" and i == 0:
                            s_dump.dma(sy, L1D, x1o[0][:, :])
                            s_dump.wait(sy)
                            return nc
                        s_x1st[i % 2].dma(sy, X1D[i * 128:(i + 1) * 128, :],
                                          x1o[pb][:, :])
                        if i + 2 < 8:
                            g.wait_ge(tC, mm_done[i])
                            load_agsb(i + 2)
                            sy.wait_ge(vC, resid_evt[i])
                            load_xr(i + 2)
            if PH == "C1":
                s_x1st[0].wait(sy)
                s_x1st[1].wait(sy)
                s_dump.dma(sy, X1O, X1D.ap())
                s_dump.wait(sy)
                return nc

            with ExitStack() as c2x:
                e2 = c2x.enter_context
                x1i = [e2(nc.sbuf_tensor(f"x1i{i}", [128, E], F32))
                       for i in range(2)]
                x1t = [e2(nc.sbuf_tensor(f"x1t{i}", [128, 16 * 128], BF16))
                       for i in range(2)]
                ysb = [e2(nc.sbuf_tensor(f"ysb{i}", [128, E], F32))
                       for i in range(2)]
                resid2 = [e2(nc.sbuf_tensor(f"resid2{i}", [128, E], F32))
                          for i in range(2)]
                qe = e2(nc.sbuf_tensor("qe", [128, 8], F32))
                qlt = e2(nc.sbuf_tensor("qlt", [128, 8], F32))
                cw = e2(nc.sbuf_tensor("cw", [128, 8], F32))
                g1 = e2(nc.sbuf_tensor("g1", [128, 1], F32))
                g2 = e2(nc.sbuf_tensor("g2", [128, 1], F32))
                gneg = e2(nc.sbuf_tensor("gneg", [128, 1], F32))
                gss = e2(nc.sbuf_tensor("gss", [128, 1], F32))
                t_bf = [e2(nc.sbuf_tensor(f"t_bf{i}", [128, 128], BF16))
                        for i in range(2)]
                tt_sb = [e2(nc.sbuf_tensor(f"tt_sb{i}", [128, 128], BF16))
                         for i in range(2)]
                scr2b = e2(nc.sbuf_tensor("scr2b", [128, E], F32))
                scr2b_free = [0]
                lnb2 = [{k: e2(nc.sbuf_tensor(f"ln2_{k}{i}", [128, 1], F32))
                         for k in ("mraw", "m", "s2", "msq", "var", "lnv",
                                   "rstd", "nmr")} for i in range(2)]
                mn_ps = e2(nc.psum_tensor("mn_ps", [128, E], F32))
                ga_ps = e2(nc.psum_tensor("ga_ps", [128, 136], F32))
                tt_ps = e2(nc.psum_tensor("tt_ps", [128, 128], BF16))
                x1t_ps = [e2(nc.psum_tensor(f"x1t_ps{i}", [128, 512], F32))
                          for i in range(2)]

                def load_x1(i):
                    return g_x1[i % 2].dma(g, x1i[i % 2][:, :],
                                           X1D[i * 128:(i + 1) * 128, :])

                s_x1st[0].wait(g)
                s_x1st[1].wait(g)
                g.wait_ge(vC, x1_evt[7])
                g.wait_ge(tC, mm_done[7])
                load_x1(0)
                load_x1(1)

                g_wf.wait(t)
                s_const.wait(t)

                x1t_evt = [0] * 8
                main_done = [0] * 8
                ga_evt = [0] * 8
                gate_done = [0] * 8
                ttv = [0] * 8
                x1t_psfree = [0, 0]
                mnfree = [0] * 8
                y_evt = [0] * 8

                def emit_x1t(i):
                    pb2 = i % 2
                    t.wait_ge(g_x1[i % 2].sem, 16 * (i // 2 + 1))
                    for q4 in range(4):
                        qp = q4 % 2
                        if x1t_psfree[qp]:
                            t.wait_ge(vC, x1t_psfree[qp])
                        for j in range(4):
                            ins = t.transpose(
                                x1t_ps[qp][:, j * 128:(j + 1) * 128],
                                x1i[pb2][:, (q4 * 4 + j) * 128:
                                         (q4 * 4 + j) * 128 + 128],
                                id32[:, :])
                        evt = tinc(ins)
                        v.wait_ge(tC, evt)
                        if i >= 2:
                            v.wait_ge(tC, main_done[i - 2])
                        x1t_psfree[qp] = vinc(v.tensor_copy(
                            x1t[i % 2][:, q4 * 512:(q4 + 1) * 512],
                            x1t_ps[qp][:, :]))
                    x1t_evt[i] = n["vC"]

                def emit_ga(i):
                    t.wait_ge(vC, x1t_evt[i])
                    for ec in range(16):
                        ins = t.matmul(ga_ps[:, :],
                                       x1t[i % 2][:, ec * 128:(ec + 1) * 128],
                                       gat[:, ec * 136:(ec + 1) * 136],
                                       start=(ec == 0), stop=(ec == 15),
                                       skip_group_check=True)
                    ga_evt[i] = tinc(ins)

                def emit_gate_chain(i):
                    v.wait_ge(tC, ga_evt[i])
                    ve = vinc(v.tensor_reduce(gneg[:, :], ga_ps[:, 0:8],
                                              AX.X, ALU.max, negate=True))
                    sc.wait_ge(vC, ve)
                    ae = ainc(sc.activation(qe[:, :], ga_ps[:, 0:8], AFT.Exp,
                                            bias=gneg[:, :]))
                    v.wait_ge(aC, ae)
                    vsync(v.reduce_max(g1[:, :], qe[:, :], AX.X))
                    vsync(v.tensor_single_scalar(qlt[:, :], qe[:, :],
                                                 g1[:, :], ALU.is_lt))
                    vsync(v.tensor_mul(qlt[:, :], qe[:, :], qlt[:, :]))
                    vsync(v.reduce_max(g2[:, :], qlt[:, :], AX.X))
                    vsync(v.tensor_add(gss[:, :], g1[:, :], g2[:, :]))
                    vsync(v.reciprocal(gss[:, :], gss[:, :]))
                    vsync(v.tensor_scalar_mul(gss[:, :], gss[:, :], 16.0))
                    vsync(v.tensor_single_scalar(qlt[:, :], qe[:, :],
                                                 g2[:, :], ALU.is_ge))
                    vsync(v.scalar_tensor_tensor(cw[:, :], qe[:, :],
                                                 gss[:, :], qlt[:, :],
                                                 ALU.mult, ALU.mult))
                    for ex in range(8):
                        ins = v.tensor_scalar(
                            t_bf[i % 2][:, ex * 16:(ex + 1) * 16],
                            ga_ps[:, 8 + ex * 16: 8 + ex * 16 + 16],
                            cw[:, ex:ex + 1], None, ALU.mult)
                    gate_done[i] = vinc(ins)

                emit_x1t(0)
                emit_ga(0)
                emit_gate_chain(0)
                for i in range(8):
                    pb = i % 2
                    t.wait_ge(vC, x1t_evt[i])
                    if i >= 1 and mnfree[i - 1]:
                        t.wait_ge(vC, mnfree[i - 1])
                    for ec in range(16):
                        for f4 in range(4):
                            ins = t.matmul(
                                mn_ps[:, f4 * 512:(f4 + 1) * 512],
                                x1t[pb][:, ec * 128:(ec + 1) * 128],
                                wft[:, ec * E + f4 * 512:
                                     ec * E + f4 * 512 + 512],
                                start=(ec == 0), stop=False,
                                skip_group_check=True)
                    main_done[i] = tinc(ins)
                    t.wait_ge(vC, gate_done[i])
                    tte = tinc(t.transpose(tt_ps[:, :], t_bf[pb][:, :],
                                           id16[:, :]))
                    v.wait_ge(tC, tte)
                    ttv[i] = vinc(v.tensor_copy(tt_sb[pb][:, :], tt_ps[:, :]))
                    t.wait_ge(vC, ttv[i])
                    for f4 in range(4):
                        ins = t.matmul(mn_ps[:, f4 * 512:(f4 + 1) * 512],
                                       tt_sb[pb][:, :],
                                       btb[:, f4 * 512:(f4 + 1) * 512],
                                       start=False, stop=True,
                                       skip_group_check=True)
                    ffn_done = tinc(ins)
                    if i + 1 < 8:
                        emit_x1t(i + 1)
                    v.wait_ge(tC, ffn_done)
                    for f4 in range(4):
                        ins = v.scalar_tensor_tensor(
                            resid2[pb][:, f4 * 512:(f4 + 1) * 512],
                            mn_ps[:, f4 * 512:(f4 + 1) * 512], 1.0,
                            x1i[pb][:, f4 * 512:(f4 + 1) * 512],
                            ALU.mult, ALU.add)
                    mnfree[i] = vinc(ins)
                    if i + 1 < 8:
                        emit_ga(i + 1)
                        emit_gate_chain(i + 1)
                    if i >= 2:
                        v.wait_ge(s_y[i % 2].sem, 16 * (i // 2))
                    y_evt[i] = ln_tail(resid2[pb][:, :], scr2b[:, :],
                                       lnb2[pb], ysb[pb][:, :], mnfree[i],
                                       scr2b_free)
                    sy.wait_ge(vC, y_evt[i])
                    s_y[i % 2].dma(sy, Y[i * 128:(i + 1) * 128, :],
                                   ysb[pb][:, :])
                    if i + 2 < 8:
                        g.wait_ge(vC, mnfree[i])
                        load_x1(i + 2)

                s_y[0].wait(sy)
                s_y[1].wait(sy)

    return nc


_NC = None


def _get_nc():
    global _NC
    if _NC is None:
        _NC = _build_nc()
    return _NC


def _make_consts():
    ROT = 64
    inv_freq = (1.0 / 10000.0 ** (np.arange(0, ROT, 2) / ROT)).astype(np.float32)
    pos = np.arange(S, dtype=np.float32)
    freqs = pos[None, :] * inv_freq[:, None]
    cos = np.repeat(np.cos(freqs), 2, axis=0).astype(ml_dtypes.bfloat16)
    sin = np.repeat(np.sin(freqs), 2, axis=0).astype(ml_dtypes.bfloat16)
    R = np.zeros((64, 64), np.float32)
    for i in range(32):
        R[2 * i, 2 * i + 1] = -1.0
        R[2 * i + 1, 2 * i] = 1.0
    RT = np.ascontiguousarray(R.T).astype(ml_dtypes.bfloat16)
    id16 = np.eye(128, dtype=np.float32).astype(ml_dtypes.bfloat16)
    id32 = np.eye(128, dtype=np.float32)
    k = np.arange(128)[:, None]
    q = np.arange(QG)[None, :]
    masks = np.concatenate(
        [(k + 128 * d <= q).astype(np.float32) for d in range(4)],
        axis=1).astype(ml_dtypes.bfloat16)
    return cos, sin, RT, id16, id32, masks


def _make_in_maps(L0_attn_hidden_states, L0_attn_w_qkv, L0_attn_w_out,
                  L0_ffn_weight, L0_ffn_gate_w, L0_ffn_A_w, L0_ffn_B_w):
    X = np.ascontiguousarray(np.asarray(L0_attn_hidden_states, np.float32))
    Wqkv = np.asarray(L0_attn_w_qkv, np.float32)
    Wout = np.asarray(L0_attn_w_out, np.float32)
    Wffn = np.asarray(L0_ffn_weight, np.float32)
    Gw = np.asarray(L0_ffn_gate_w, np.float32)
    Aw = np.asarray(L0_ffn_A_w, np.float32)
    Bw = np.asarray(L0_ffn_B_w, np.float32)

    cos, sin, RT, id16, id32, masks = _make_consts()
    WOTn = np.ascontiguousarray(Wout.T)
    WFTn = np.ascontiguousarray(Wffn.T)
    GATn = np.ascontiguousarray(np.concatenate(
        [Gw.T, Aw.reshape(128, E).T], axis=1))
    BTDn = np.ascontiguousarray(Bw.transpose(0, 2, 1).reshape(128, E))

    in_maps = []
    for c in range(8):
        b, gq = c // 4, c % 4
        in_maps.append({
            "XT": np.ascontiguousarray(X[b].T),
            "XR": np.ascontiguousarray(X[b, gq * TOK:(gq + 1) * TOK]),
            "WQT": np.ascontiguousarray(Wqkv[gq * 1536: gq * 1536 + 512].T),
            "WVT": np.ascontiguousarray(
                Wqkv[gq * 1536 + 512: gq * 1536 + 1024].T),
            "WKT": np.ascontiguousarray(
                Wqkv[gq * 1536 + 1024: gq * 1536 + 1536].T),
            "WOT": WOTn, "WFT": WFTn, "GAT": GATn, "BTD": BTDn,
            "COS": cos, "SIN": sin, "RTD": RT, "ID16": id16, "ID32": id32,
            "MASKS": masks,
        })

    return in_maps


def kernel(**inputs):
    in_maps = _make_in_maps(**inputs)
    nc = _get_nc()
    res = run_bass_kernel_spmd(nc, in_maps, core_ids=list(range(8)))
    out = np.empty((B, S, E), np.float32)
    for c in range(8):
        b, gq = c // 4, c % 4
        out[b, gq * TOK:(gq + 1) * TOK] = res.results[c]["Y"]
    return out
